# version 02
# baseline (speedup 1.0000x reference)
from contextlib import ExitStack

import numpy as np
import concourse.bass as bass
import concourse.mybir as mybir
from concourse.bass_utils import run_bass_kernel_spmd

F32, BF16 = mybir.dt.float32, mybir.dt.bfloat16
AF = mybir.ActivationFunctionType
ALU = mybir.AluOpType
AX = mybir.AxisListType.X

D, L = 1024, 4096
DC = D // 128
NT = L // 512
NTT = L // 128
LN_EPS = 1e-5
RMS_EPS = 1e-6
NHP = 8
EXPERTS = 8

ENGS = ("sync", "tensor", "vector", "scalar", "gpsimd")
DMA_RING = {"sync": 10, "gpsimd": 8, "scalar": 4, "vector": 4, "tensor": 4}


class Op:
    __slots__ = ("eng", "idx", "fn", "is_dma", "marked", "inc_id", "dma_serial",
                 "slot", "cycle", "deps", "tag")

    def __init__(self, eng, idx, fn, is_dma, tag):
        self.eng, self.idx, self.fn, self.is_dma = eng, idx, fn, is_dma
        self.marked = False
        self.inc_id = None
        self.dma_serial = None
        self.slot = None
        self.cycle = None
        self.deps = []
        self.tag = tag


class Sched:
    def __init__(self, nc):
        self.nc = nc
        self.streams = {e: [] for e in ENGS}
        self.bufs = {}
        self.pending = {e: [] for e in ENGS}
        self.dma_count = {e: 0 for e in ENGS}

    def op(self, eng, fn, reads=(), writes=(), dma=False, tag=""):
        o = Op(eng, len(self.streams[eng]), fn, dma, tag)
        if dma:
            o.dma_serial = self.dma_count[eng]
            self.dma_count[eng] += 1
            r = DMA_RING[eng]
            o.slot = o.dma_serial % r
            o.cycle = o.dma_serial // r
        deps = {}

        def add_dep(p):
            if p is o:
                return
            if p.eng == eng and not p.is_dma and not dma:
                return
            if p.is_dma:
                key = (p.eng, "dma", p.slot)
            else:
                key = (p.eng, "eng")
            cur = deps.get(key)
            if cur is None or p.idx > cur.idx:
                deps[key] = p

        for k in reads:
            st = self.bufs.get(k)
            if st:
                for w in st[0]:
                    add_dep(w)
                st[1].append(o)
        for k in writes:
            st = self.bufs.get(k)
            if st is None:
                st = ([], [])
                self.bufs[k] = st
            for w in st[0]:
                add_dep(w)
            for r in st[1]:
                add_dep(r)
            if st[1]:
                st[0].clear()
                st[1].clear()
            st[0].append(o)
        for p in self.pending[eng]:
            add_dep(p)
        self.pending[eng] = []
        o.deps = list(deps.values())
        for p in o.deps:
            if not p.is_dma:
                p.marked = True
        self.streams[eng].append(o)
        return o

    def dma(self, eng, out, in_, reads=(), writes=(), tag=""):
        return self.op(eng, lambda e: e.dma_start(out=out, in_=in_),
                       reads=reads, writes=writes, dma=True, tag=tag)

    def barrier(self):
        lasts = []
        for e in ENGS:
            if self.streams[e]:
                lasts.append(self.streams[e][-1])
        for e in ENGS:
            self.pending[e] = [p for p in lasts
                               if not (p.eng == e and not p.is_dma)]
            for p in self.pending[e]:
                if not p.is_dma:
                    p.marked = True

    def emit(self, stack):
        nc = self.nc
        esem = {}
        dsem = {}
        for e in ENGS:
            ops = self.streams[e]
            cnt = 0
            for o in ops:
                if o.marked and not o.is_dma:
                    cnt += 1
                    o.inc_id = cnt
            if cnt:
                esem[e] = stack.enter_context(nc.semaphore(f"es_{e}"))
            if self.dma_count[e]:
                nslots = min(DMA_RING[e], self.dma_count[e])
                dsem[e] = [stack.enter_context(nc.semaphore(f"ds_{e}{i}"))
                           for i in range(nslots)]

        def emit_stream(eng_obj, e):
            water = {}
            for o in self.streams[e]:
                for p in o.deps:
                    if p.is_dma:
                        sem = dsem[p.eng][p.slot]
                        val = 16 * (p.cycle + 1)
                        wkey = (p.eng, "dma", p.slot)
                    else:
                        sem, val = esem[p.eng], p.inc_id
                        wkey = (p.eng, "eng")
                    if water.get(wkey, 0) < val:
                        water[wkey] = val
                        eng_obj.wait_ge(sem, val)
                if o.is_dma and o.cycle > 0:
                    sem = dsem[e][o.slot]
                    val = 16 * o.cycle
                    wkey = (e, "dma", o.slot)
                    if water.get(wkey, 0) < val:
                        water[wkey] = val
                        eng_obj.wait_ge(sem, val)
                ins = o.fn(eng_obj)
                if o.is_dma:
                    ins.then_inc(dsem[e][o.slot], 16)
                elif o.marked:
                    ins.then_inc(esem[e], 1)

        with nc.Block() as block:
            block.sync(lambda e: emit_stream(e, "sync"))
            block.tensor(lambda e: emit_stream(e, "tensor"))
            block.vector(lambda e: emit_stream(e, "vector"))
            block.scalar(lambda e: emit_stream(e, "scalar"))
            block.gpsimd(lambda e: emit_stream(e, "gpsimd"))

    def stats(self):
        out = {}
        for e in ENGS:
            ops = self.streams[e]
            out[e] = dict(n=len(ops), marked=sum(1 for o in ops if o.marked),
                          deps=sum(len(o.deps) for o in ops))
        return out



def _threefry2x32_np(k1, k2, x0, x1):
    def rotl(x, r):
        return ((x << np.uint32(r)) | (x >> np.uint32(32 - r))).astype(np.uint32)
    rotations = [(13, 15, 26, 6), (17, 29, 16, 24)]
    ks0 = np.uint32(k1)
    ks1 = np.uint32(k2)
    ks2 = np.uint32(np.uint32(0x1BD11BDA) ^ ks0 ^ ks1)
    x0 = (x0 + ks0).astype(np.uint32)
    x1 = (x1 + ks1).astype(np.uint32)
    ks = [ks0, ks1, ks2]
    for i in range(5):
        for r in rotations[i % 2]:
            x0 = (x0 + x1).astype(np.uint32)
            x1 = rotl(x1, r)
            x1 = (x1 ^ x0).astype(np.uint32)
        x0 = (x0 + ks[(i + 1) % 3]).astype(np.uint32)
        x1 = (x1 + ks[(i + 2) % 3] + np.uint32(i + 1)).astype(np.uint32)
    return x0, x1


def _driver_gumbel(shape):
    size = int(np.prod(shape))
    idx = np.arange(size, dtype=np.uint64)
    c1 = (idx >> np.uint64(32)).astype(np.uint32)
    c2 = (idx & np.uint64(0xFFFFFFFF)).astype(np.uint32)
    b1, b2 = _threefry2x32_np(0, 42, c1, c2)
    bits = (b1 ^ b2).astype(np.uint32)
    fl = ((bits >> np.uint32(9)) | np.uint32(0x3F800000)).view(np.float32) \
        - np.float32(1.0)
    tiny = np.float32(np.finfo(np.float32).tiny)
    u = np.maximum(tiny, (fl * (np.float32(1.0) - tiny) + tiny)).astype(np.float32)
    return (-np.log(-np.log(u))).astype(np.float32).reshape(shape)


def host_consts(core_id):
    c = {}
    c["ident"] = np.eye(128, dtype=np.float32)
    half = 32
    inv = 1.0 / 10000.0 ** (np.arange(half, dtype=np.float64) / half)
    jmod = np.arange(128) % half
    ang = np.arange(L, dtype=np.float64)[None, :] * inv[jmod][:, None]
    cos = np.cos(ang).astype(np.float32)
    sin = np.sin(ang).astype(np.float32)
    c["rope_cq"] = (cos / 8.0).astype(np.float32)
    c["rope_sq"] = (sin / 8.0).astype(np.float32)
    c["rope_ck"] = cos
    c["rope_sk"] = sin
    pm = np.zeros((128, 128), dtype=np.float32)
    for m in range(128):
        if m % 64 < 32:
            pm[m + 32, m] = -1.0
        else:
            pm[m - 32, m] = 1.0
    c["rope_pm"] = pm
    c["tri"] = np.where(np.arange(128)[None, :] <= np.arange(128)[:, None],
                        0.0, -1e9).astype(np.float32)
    j = np.arange(896)[None, :]
    k = np.arange(128)[:, None]
    c["wmask"] = np.where(j >= k + 384, 0.0, -1e9).astype(np.float32)
    noise = _driver_gumbel((8, L, 8)) * np.float32(0.05)
    c["noise"] = np.ascontiguousarray(noise[core_id])
    return c


class MK:

    def __init__(self, upto=99, dbg=False):
        self.dbg = dbg
        self.upto = upto
        nc = bass.Bass("TRN2", target_bir_lowering=False)
        self.nc = nc
        self.S = Sched(nc)
        self.stack = ExitStack()
        self.inp = {}

        def ein(name, shape, dtype=F32):
            t = nc.dram_tensor(name, shape, dtype, kind="ExternalInput").ap()
            self.inp[name] = t
            return t

        self.ein = ein
        self.xin = ein("xin", [L, D])
        self.out = nc.dram_tensor("out", [L, D], F32, kind="ExternalOutput").ap()
        self.c_ident = ein("ident", [128, 128])
        self.c_tri = ein("tri", [128, 128])
        for w in ("aWq", "aWk", "aWv", "aWc"):
            ein(w, [D, D])
        for w in ("fWq", "fWk", "fWv", "fWo"):
            ein(w, [D, D])
        for w in ("fW1", "fW2", "fWout"):
            ein(w, [8, D, D])
        ein("fscale1", [128, 8])
        ein("fscale2", [128, 8])
        ein("frw", [128, 8, 8])
        ein("fnoise", [128, 32, 8])
        ein("fb1", [8, 128, 8])
        ein("fb2", [8, 128, 8])
        ein("fboutoh", [8, 8192])
        ein("ropec", [128, L], BF16)
        ein("ropes", [128, L], BF16)
        ein("pmb", [128, 128], BF16)
        ein("wmask", [128, 896])
        ein("oh8", [8, 1024])
        ein("mWup", [D, 256])
        ein("mWdelta", [128, 128])
        ein("mWB", [128, 64])
        ein("mWC", [128, 64])
        ein("mWdown", [128, D])
        ein("mbup", [128, 2])
        ein("mconvw", [128, 3])
        ein("mconvb", [128, 1])
        ein("mbdelta", [128, 1])
        ein("mbB", [64, 1])
        ein("mbC", [64, 1])
        ein("mAlog", [128, 64])
        ein("mdskip", [128, 1])
        ein("mbdown", [128, 8])
        ein("moh", [64, 8192], BF16)
        self.xres = nc.dram_tensor("xres", [D, L], F32, kind="Internal").ap()
        self.zst = nc.dram_tensor("zst", [D, L], F32, kind="Internal").ap()
        self.ao = nc.dram_tensor("ao", [D, L], BF16, kind="Internal").ap()
        self.xmoe = nc.dram_tensor("xmoe", [D, L], BF16, kind="Internal").ap()
        self.wcache = nc.dram_tensor("wcache", [24, 128, 8 * 1024], BF16,
                                     kind="Internal").ap()
        self.khbm = nc.dram_tensor("khbm", [16, 64, L], BF16, kind="Internal").ap()
        self.qhbm = nc.dram_tensor("qhbm", [16, 64, L], BF16, kind="Internal").ap()
        self.vkhbm = nc.dram_tensor("vkhbm", [16, 128, 32 * 65], BF16,
                                    kind="Internal").ap()

        sbt = nc.alloc_sbuf_tensor
        self.cpool = sbt("s_cpool", [128, 16], F32)
        self._cvals = {}
        self.ident = sbt("s_ident", [128, 128], F32)
        self.tri = sbt("s_tri", [128, 128], F32)
        self.ones_p = sbt("s_ones_p", [128, 1], F32)
        self.ones_r = sbt("s_ones_r", [1, 128], F32)
        self.pb = [nc.alloc_psum_tensor(f"pb{i}", [128, 512], F32)
                   for i in range(8)]
        self._pb_rr = 0
        self.fscale1 = sbt("s_fsc1", [128, 8], F32)
        self.fscale2 = sbt("s_fsc2", [128, 8], F32)
        self.ones64 = self.ones_r[:, 0:64]

    def bank(self, i=None):
        if i is None:
            i = self._pb_rr % 8
            self._pb_rr += 1
        return self.pb[i], f"pb{i}"

    def scope(self):
        return ExitStack()

    def sal(self, sc, name, shape, dtype):
        return sc.enter_context(self.nc.sbuf_tensor(name, shape, dtype))

    def cbias(self, v):
        v = float(v)
        if getattr(self, "_cpool_done", False):
            assert v in self._cvals, f"const {v} not pre-seeded"
        i = self._cvals.setdefault(v, len(self._cvals))
        assert i < 16
        return self.cpool[:, i:i + 1]

    def cbias1(self, v):
        return self.cbias(v)[0:1, :]

    def emit_cpool(self):
        S = self.S
        for v in (1e-6, LN_EPS, RMS_EPS, 1.0, 0.5):
            self.cbias(v)
        for v, i in sorted(self._cvals.items(), key=lambda kv: kv[1]):
            S.op("vector",
                 (lambda v=v, i=i: lambda e: e.memset(self.cpool[:, i:i + 1], v))(),
                 writes=["cpool"], tag="cpool")
        self._cpool_done = True

    def load_consts(self):
        S = self.S
        S.dma("sync", self.ident[:], self.c_ident, writes=["ident"], tag="c")
        S.dma("sync", self.tri[:], self.c_tri, writes=["tri"], tag="c")
        S.op("vector", lambda e: e.memset(self.ones_p[:], 1.0), writes=["ones_p"])
        S.op("vector", lambda e: e.memset(self.ones_r[:], 1.0), writes=["ones_r"])
        S.dma("sync", self.fscale1[:], self.inp["fscale1"], writes=["fsc1"], tag="c")
        S.dma("sync", self.fscale2[:], self.inp["fscale2"], writes=["fsc2"], tag="c")

    def p0_transpose_in(self):
        S = self.S
        sc0 = self.scope()
        tin = self.sal(sc0, "t0_tin", [128, 4, 1024], F32)
        tst = self.sal(sc0, "t0_tst", [128, 8, 512], F32)
        for g in range(8):
            for j in range(4):
                ti = g * 4 + j
                S.dma("sync", tin[:, j, :], self.xin[ti * 128:(ti + 1) * 128, :],
                      writes=[("tin", j)], tag="p0")
            for c in range(DC):
                pb, pk = self.bank()
                for j in range(4):
                    S.op("tensor",
                         (lambda pb=pb, j=j, c=c:
                          lambda e: e.transpose(pb[:, j * 128:(j + 1) * 128],
                                                tin[:, j, c * 128:(c + 1) * 128],
                                                self.ident[:]))(),
                         reads=[("tin", j), "ident"], writes=[pk], tag="p0t")
                S.op("scalar",
                     (lambda pb=pb, c=c: lambda e: e.copy(tst[:, c, :], pb[:]))(),
                     reads=[pk], writes=[("tst", c)], tag="p0e")
            for c in range(DC):
                S.dma("gpsimd",
                      self.xres[c * 128:(c + 1) * 128, g * 512:(g + 1) * 512],
                      tst[:, c, :], reads=[("tst", c)], writes=[("xres", c, g)],
                      tag="p0s")
        sc0.close()
        S.barrier()

    def set_cast(self, sc, pref):
        self._cast_stg = [self.sal(sc, pref + f"cstg{b}", [128, 1024], F32)
                          for b in range(2)]
        self._cast_i = 0

    def cast_weight_to(self, wname, dst, dst_key):
        S = self.S
        src = self.inp[wname]
        dout = src.shape[1]
        rs = src.rearrange("(c p) m -> p c m", p=128)
        for c in range(DC):
            b = self._cast_i % 2
            self._cast_i += 1
            stg = self._cast_stg[b]
            S.dma("sync", stg[:, 0:dout], rs[:, c, :],
                  reads=[("cstg", b)], writes=[("cstg", b)], tag="cw")
            S.op("scalar",
                 (lambda b=b, c=c, stg=stg, dout=dout:
                  lambda e: e.copy(dst[:, c, 0:dout], stg[:, 0:dout]))(),
                 reads=[("cstg", b)], writes=[(dst_key, c)], tag="cw")

    def ln_strip(self, xs, xname, zs, zname, sq, rms_scale=None):
        S = self.S
        sv = self.ln_sv
        pb1, pk1 = self.bank()
        pb2, pk2 = self.bank()
        xk = [(xname, c) for c in range(DC)]
        for c in range(DC):
            S.op("tensor",
                 (lambda pb1=pb1, c=c: lambda e: e.matmul(
                     pb1[0:1, :], self.ones_p[:], xs[:, c, :],
                     start=(c == 0), stop=(c == DC - 1)))(),
                 reads=[xk[c], "ones_p"], writes=[pk1], tag="lnmm")
        for c in range(DC):
            S.op("scalar",
                 (lambda c=c: lambda e: e.activation(sq[:], xs[:, c, :],
                                                     AF.Square))(),
                 reads=[xk[c]], writes=["ln_sq"], tag="lnsq")
            S.op("tensor",
                 (lambda pb2=pb2, c=c: lambda e: e.matmul(
                     pb2[0:1, :], self.ones_p[:], sq[:],
                     start=(c == 0), stop=(c == DC - 1)))(),
                 reads=["ln_sq", "ones_p"], writes=[pk2], tag="lnmm2")
        S.op("vector", lambda e: e.tensor_scalar(sv[0:1, 0:512], pb1[0:1, :],
                                                 1.0 / D, None, ALU.mult),
             reads=[pk1], writes=["sv"], tag="lnst")
        S.op("vector", lambda e: e.tensor_scalar(sv[0:1, 512:1024], pb2[0:1, :],
                                                 1.0 / D, None, ALU.mult),
             reads=[pk2], writes=["sv"], tag="lnst")
        S.op("vector", lambda e: e.tensor_mul(sv[0:1, 1536:2048], sv[0:1, 0:512], sv[0:1, 0:512]),
             reads=["sv"], writes=["sv"], tag="lnst")
        S.op("vector", lambda e: e.tensor_sub(sv[0:1, 512:1024], sv[0:1, 512:1024], sv[0:1, 1536:2048]),
             reads=["sv"], writes=["sv"], tag="lnst")
        S.op("scalar", lambda e: e.activation(sv[0:1, 1024:1536], sv[0:1, 512:1024],
                                              AF.Sqrt, bias=self.cbias1(LN_EPS)),
             reads=["sv", "cpool"], writes=["sv"], tag="lnst")
        S.op("vector", lambda e: e.reciprocal(sv[0:1, 1024:1536], sv[0:1, 1024:1536]),
             reads=["sv"], writes=["sv"], tag="lnst")
        S.op("vector", lambda e: e.tensor_mul(sv[0:1, 1536:2048], sv[0:1, 0:512], sv[0:1, 1024:1536]),
             reads=["sv"], writes=["sv"], tag="lnst")
        pb3, pk3 = self.bank()
        pb4, pk4 = self.bank()
        S.op("tensor", lambda e: e.matmul(pb3[:], self.ones_r[:], sv[0:1, 1024:1536],
                                          start=True, stop=True),
             reads=["sv", "ones_r"], writes=[pk3], tag="lnbc")
        S.op("tensor", lambda e: e.matmul(pb4[:], self.ones_r[:], sv[0:1, 1536:2048],
                                          start=True, stop=True),
             reads=["sv", "ones_r"], writes=[pk4], tag="lnbc")
        zk = [(zname, c) for c in range(DC)]
        for c in range(DC):
            S.op("vector",
                 (lambda c=c, pb3=pb3: lambda e: e.tensor_mul(
                     self.ln_nsc[:], xs[:, c, :], pb3[:]))(),
                 reads=[xk[c], pk3], writes=["ln_nsc"], tag="lnn1")
            S.op("vector",
                 (lambda c=c, pb4=pb4: lambda e: e.tensor_sub(
                     zs[:, c, :], self.ln_nsc[:], pb4[:]))(),
                 reads=["ln_nsc", pk4], writes=[zk[c]], tag="lnn2")

    def proj(self, w, wkey, zs, zname, dst, dname, evac="copy"):
        S = self.S
        for m in range(DC):
            pb, pk = self.bank()
            for k in range(DC):
                S.op("tensor",
                     (lambda pb=pb, m=m, k=k, first=(k == 0), last=(k == DC - 1):
                      lambda e: e.matmul(pb[:], w[:, k, m * 128:(m + 1) * 128],
                                         zs[:, k, :], start=first, stop=last))(),
                     reads=[(wkey, k), (zname, k)], writes=[pk], tag="pj")
            if evac == "copy":
                fn = (lambda pb=pb, m=m: lambda e: e.copy(dst[:, m, :], pb[:]))()
            elif evac == "exp":
                fn = (lambda pb=pb, m=m:
                      lambda e: e.activation(dst[:, m, :], pb[:], AF.Exp))()
            elif evac == "sigmoid":
                fn = (lambda pb=pb, m=m:
                      lambda e: e.activation(dst[:, m, :], pb[:], AF.Sigmoid))()
            S.op("scalar", fn, reads=[pk], writes=[(dname, m)], tag="pjev")

    def p1_attn_a(self):
        nc, S = self.nc, self.S
        sc1 = self.scope()
        wq = self.sal(sc1, "a_wq", [128, 8, 1024], BF16)
        wk = self.sal(sc1, "a_wk", [128, 8, 1024], BF16)
        wv = self.sal(sc1, "a_wv", [128, 8, 1024], BF16)
        wc = self.sal(sc1, "a_wc", [128, 8, 1024], BF16)
        self._mc_stg = [self.sal(sc1, f"mcs{b}", [128, 4, 1024], F32)
                        for b in range(2)]
        self._mc_cb = [self.sal(sc1, f"mcb{b}", [128, 4, 1024], BF16)
                       for b in range(2)]
        self._mc_i = 0
        self.set_cast(sc1, "p1")
        self.cast_weight_to("aWv", wv, "a_wv")
        self.cast_weight_to("aWk", wk, "a_wk")
        self.cast_weight_to("aWq", wq, "a_wq")
        self.cast_weight_to("aWc", wc, "a_wc")

        xs = self.sal(sc1, "a_xs", [128, 8, 512], F32)
        zs = self.sal(sc1, "a_zs", [128, 8, 512], BF16)
        b1 = self.sal(sc1, "a_b1", [128, 8, 512], F32)
        b2 = self.sal(sc1, "a_b2", [128, 8, 512], F32)
        carK = self.sal(sc1, "a_carK", [128, 8], F32)
        carKV = self.sal(sc1, "a_carKV", [128, 8], F32)
        S.op("vector", lambda e: e.memset(carK[:], 0.0), writes=["carK"])
        S.op("vector", lambda e: e.memset(carKV[:], 0.0), writes=["carKV"])

        for s in range(NT):
            if self.upto >= 2:
                for j in range(3):
                    mi = s * 3 + j
                    self.cache_moe_matrix(mi, self.inp[("fW1", "fW2", "fWout")[mi % 3]][mi // 3])
            t0 = s * 512
            for c in range(DC):
                S.dma("sync", xs[:, c, :],
                      self.xres[c * 128:(c + 1) * 128, t0:t0 + 512],
                      reads=[("xres", c, s), ("a_xs", c)], writes=[("a_xs", c)],
                      tag="p1ld")
            self.ln_strip(xs, "a_xs", zs, "a_zs", self.ln_sq)
            self.proj(wv, "a_wv", zs, "a_zs", b1, "b1", evac="copy")
            self.proj(wk, "a_wk", zs, "a_zs", b2, "b2", evac="exp")
            for c in range(DC):
                S.op("vector",
                     (lambda c=c: lambda e: e.tensor_mul(b1[:, c, :], b2[:, c, :],
                                                         b1[:, c, :]))(),
                     reads=[("b1", c), ("b2", c)], writes=[("b1", c)], tag="p1kv")
            for (buf, bk, car, ck) in ((b2, "b2", carK, "carK"),
                                       (b1, "b1", carKV, "carKV")):
                for c in range(DC):
                    S.op("vector",
                         (lambda buf=buf, car=car, c=c:
                          lambda e: e.tensor_tensor_scan(
                              buf[:, c, :], buf[:, c, :], buf[:, c, :],
                              car[:, c:c + 1], ALU.add, ALU.bypass))(),
                         reads=[(bk, c), (ck, c)], writes=[(bk, c)], tag="p1sc")
                    S.op("scalar",
                         (lambda buf=buf, car=car, c=c:
                          lambda e: e.copy(car[:, c:c + 1],
                                           buf[:, c, 511:512]))(),
                         reads=[(bk, c)], writes=[(ck, c)], tag="p1cr")
            for c in range(DC):
                S.op("scalar",
                     (lambda c=c: lambda e: e.activation(b2[:, c, :], b2[:, c, :],
                                                         AF.Ln, bias=self.cbias(1e-6)))(),
                     reads=[("b2", c), "cpool"], writes=[("b2", c)], tag="p1ln")
                S.op("scalar",
                     (lambda c=c: lambda e: e.activation(b2[:, c, :], b2[:, c, :],
                                                         AF.Exp, scale=-1.0))(),
                     reads=[("b2", c)], writes=[("b2", c)], tag="p1ex")
                S.op("vector",
                     (lambda c=c: lambda e: e.tensor_mul(b1[:, c, :], b1[:, c, :],
                                                         b2[:, c, :]))(),
                     reads=[("b1", c), ("b2", c)], writes=[("b1", c)], tag="p1s")
            self.proj(wq, "a_wq", zs, "a_zs", b2, "b2", evac="sigmoid")
            for c in range(DC):
                S.op("vector",
                     (lambda c=c: lambda e: e.tensor_mul(zs[:, c, :], b1[:, c, :],
                                                         b2[:, c, :]))(),
                     reads=[("b1", c), ("b2", c)], writes=[("a_zs", c)], tag="p1o")
            for m in range(DC):
                pb, pk = self.bank()
                for k in range(DC):
                    S.op("tensor",
                         (lambda pb=pb, m=m, k=k, first=(k == 0),
                          last=(k == DC - 1):
                          lambda e: e.matmul(pb[:],
                                             wc[:, k, m * 128:(m + 1) * 128],
                                             zs[:, k, :], start=first,
                                             stop=last))(),
                         reads=[("a_wc", k), ("a_zs", k)], writes=[pk], tag="p1mm")
                S.op("vector",
                     (lambda pb=pb, m=m: lambda e: e.tensor_add(
                         xs[:, m, :], pb[:], xs[:, m, :]))(),
                     reads=[pk, ("a_xs", m)], writes=[("a_xs", m)], tag="p1ra")
                S.dma("gpsimd",
                      self.xres[m * 128:(m + 1) * 128, t0:t0 + 512],
                      xs[:, m, :], reads=[("a_xs", m)], writes=[("xres", m, s)],
                      tag="p1st")
        sc1.close()
        S.barrier()


    def cache_moe_matrix(self, mi, wap):
        S = self.S
        rs = wap.rearrange("(c p) m -> p c m", p=128)
        wcr = self.wcache[mi].rearrange("p (c m) -> p c m", m=1024)
        for hhalf in range(2):
            b = self._mc_i % 2
            self._mc_i += 1
            stg = self._mc_stg[b]
            cb = self._mc_cb[b]
            S.dma("sync", stg[:], rs[:, hhalf * 4:hhalf * 4 + 4, :],
                  reads=[("mcs", b)], writes=[("mcs", b)], tag="mc")
            for c in range(4):
                S.op("scalar",
                     (lambda b=b, c=c, stg=stg, cb=cb:
                      lambda e: e.copy(cb[:, c, :], stg[:, c, :]))(),
                     reads=[("mcs", b)], writes=[("mcb", b)], tag="mc")
            S.dma("gpsimd", wcr[:, hhalf * 4:hhalf * 4 + 4, :],
                  cb[:], reads=[("mcb", b)], writes=[("wc", mi)], tag="mc")

    def ffn(self, t):
        nc, S = self.nc, self.S
        inp = self.inp

        scAB = self.scope()
        xat = self.sal(scAB, t + "_xat", [128, DC, L], BF16)
        wq = self.sal(scAB, t + "_wq", [128, 8, 1024], BF16)
        wk = self.sal(scAB, t + "_wk", [128, 8, 1024], BF16)
        wv = self.sal(scAB, t + "_wv", [128, 8, 1024], BF16)

        scA = self.scope()
        fxs = self.sal(scA, t + "_fxs", [128, 8, 512], F32)
        fzs = self.sal(scA, t + "_fzs", [128, 8, 512], F32)
        self.set_cast(scA, t + "A")
        self.cast_weight_to("fWq", wq, t + "wq")
        self.cast_weight_to("fWk", wk, t + "wk")
        self.cast_weight_to("fWv", wv, t + "wv")
        for s in range(NT):
            t0 = s * 512
            for c in range(DC):
                S.dma("sync", fxs[:, c, :],
                      self.xres[c * 128:(c + 1) * 128, t0:t0 + 512],
                      reads=[("xres", c, s), (t + "fxs", c)],
                      writes=[(t + "fxs", c)], tag=t + "Ald")
            self.ln_strip(fxs, t + "fxs", fzs, t + "fzs", self.ln_sq)
            for c in range(DC):
                S.dma("gpsimd", self.zst[c * 128:(c + 1) * 128, t0:t0 + 512],
                      fzs[:, c, :], reads=[(t + "fzs", c)],
                      writes=[("zst", c, s)], tag=t + "Ast")
            pbS, pkS = self.bank()
            for c in range(DC):
                S.op("scalar",
                     (lambda c=c: lambda e: e.activation(self.ln_sq[:],
                                                         fzs[:, c, :],
                                                         AF.Square))(),
                     reads=[(t + "fzs", c)], writes=["ln_sq"], tag=t + "Asq")
                S.op("tensor",
                     (lambda pbS=pbS, c=c: lambda e: e.matmul(
                         pbS[0:1, :], self.ones_p[:], self.ln_sq[:],
                         start=(c == 0), stop=(c == DC - 1)))(),
                     reads=["ln_sq", "ones_p"], writes=[pkS], tag=t + "Amm")
            sv = self.ln_sv
            S.op("vector", (lambda pbS=pbS: lambda e: e.tensor_scalar(
                sv[0:1, 512:1024], pbS[0:1, :], 1.0 / D, None, ALU.mult))(),
                 reads=[pkS], writes=["sv"], tag=t + "Ast2")
            S.op("scalar", lambda e: e.activation(sv[0:1, 1024:1536],
                                                  sv[0:1, 512:1024], AF.Sqrt,
                                                  bias=self.cbias1(RMS_EPS)),
                 reads=["sv", "cpool"], writes=["sv"], tag=t + "Ast2")
            S.op("vector", lambda e: e.reciprocal(sv[0:1, 1024:1536],
                                                  sv[0:1, 1024:1536]),
                 reads=["sv"], writes=["sv"], tag=t + "Ast2")
            pbB, pkB = self.bank()
            S.op("tensor", (lambda pbB=pbB: lambda e: e.matmul(
                pbB[:], self.ones_r[:], sv[0:1, 1024:1536],
                start=True, stop=True))(),
                 reads=["sv", "ones_r"], writes=[pkB], tag=t + "Abc")
            for c in range(DC):
                S.op("vector",
                     (lambda c=c, pbB=pbB, t0=t0: lambda e: e.scalar_tensor_tensor(
                         xat[:, c, t0:t0 + 512], fzs[:, c, :],
                         self.fscale1[:, c:c + 1], pbB[:],
                         ALU.mult, ALU.mult))(),
                     reads=[(t + "fzs", c), pkB, "fsc1"],
                     writes=[(t + "xat", c, s)], tag=t + "Axm")
        scA.close()
        S.barrier()
        if self.dbg and t == "F1":
            dxat = self.nc.dram_tensor("dxat", [128, DC, L], BF16,
                                       kind="ExternalOutput").ap()
            S.dma("sync", dxat, xat[:],
                  reads=[(t + "xat", c, s2) for c in range(DC)
                         for s2 in range(NT)],
                  writes=["dbgxat"], tag="dbg")
            S.barrier()
        scB = self.scope()
        qb = self.sal(scB, t + "_qb", [128, 8, 512], BF16)
        khp = self.sal(scB, t + "_khp", [64, 2 * L], BF16)
        qhp = self.sal(scB, t + "_qhp", [64, 2 * L], BF16)
        vk0 = self.sal(scB, t + "_vk0", [128, 32, 65], BF16)
        vk1 = self.sal(scB, t + "_vk1", [128, 32, 65], BF16)
        s1 = self.sal(scB, t + "_s1", [64, 512], F32)
        s2 = self.sal(scB, t + "_s2", [64, 512], F32)
        ctab = self.sal(scB, t + "_ctab", [128, L], BF16)
        stab = self.sal(scB, t + "_stab", [128, L], BF16)
        pmt = self.sal(scB, t + "_pmt", [128, 128], BF16)
        S.dma("sync", ctab[:], inp["ropec"], writes=["ctab"], tag=t + "B")
        S.dma("sync", stab[:], inp["ropes"], writes=["stab"], tag=t + "B")
        S.dma("sync", pmt[:], inp["pmb"], writes=["pmt"], tag=t + "B")

        for p in range(NHP):
            for (wmat, wkey, dsttab, dkey, hbm) in (
                    (wq, t + "wq", qhp, "qhp", self.qhbm),
                    (wk, t + "wk", khp, "khp", self.khbm)):
                for n in range(8):
                    pbQ, pkQ = self.bank()
                    for k in range(DC):
                        S.op("tensor",
                             (lambda pbQ=pbQ, wmat=wmat, k=k, p=p, n=n,
                              first=(k == 0), last=(k == DC - 1):
                              lambda e: e.matmul(
                                  pbQ[:], wmat[:, k, p * 128:(p + 1) * 128],
                                  xat[:, k, n * 512:(n + 1) * 512],
                                  start=first, stop=last))(),
                             reads=[(wkey, k), (t + "xat", k, n)],
                             writes=[pkQ], tag=t + "Bqk")
                    S.op("scalar",
                         (lambda pbQ=pbQ, n=n: lambda e: e.copy(qb[:, n, :],
                                                                pbQ[:]))(),
                         reads=[pkQ], writes=[("qb", n)], tag=t + "Bqe")
                for h01 in range(2):
                    off = h01 * L
                    rs = slice(h01 * 64, h01 * 64 + 64)
                    for n in range(8):
                        pbP, pkP = self.bank()
                        S.op("tensor",
                             (lambda pbP=pbP, rs=rs, n=n: lambda e: e.matmul(
                                 pbP[0:64, :], pmt[rs, rs], qb[rs, n, :],
                                 start=True, stop=True))(),
                             reads=[("qb", n), "pmt"], writes=[pkP],
                             tag=t + "Brp")
                        S.op("vector",
                             (lambda rs=rs, n=n: lambda e: e.tensor_mul(
                                 s1[:], qb[rs, n, :],
                                 ctab[rs, n * 512:(n + 1) * 512]))(),
                             reads=[("qb", n), "ctab"], writes=["s1"],
                             tag=t + "Brt")
                        S.op("vector",
                             (lambda pbP=pbP, n=n: lambda e: e.tensor_mul(
                                 s2[:], pbP[0:64, :],
                                 stab[0:64, n * 512:(n + 1) * 512]))(),
                             reads=[pkP, "stab"], writes=["s2"], tag=t + "Brs")
                        S.op("vector",
                             (lambda dsttab=dsttab, off=off, n=n:
                              lambda e: e.tensor_add(
                                  dsttab[0:64, off + n * 512:off + (n + 1) * 512],
                                  s1[:], s2[:]))(),
                             reads=["s1", "s2"],
                             writes=[(dkey, h01, n)], tag=t + "Brr")
                    h = 2 * p + h01
                    S.dma("sync", hbm[h], dsttab[0:64, off:off + L],
                          reads=[(dkey, h01, n2) for n2 in range(8)],
                          writes=[("qh" if dkey == "qhp" else "kh", h)],
                          tag=t + "Bqs")
            for (h01, vkb) in ((0, vk0), (1, vk1)):
                S.op("vector",
                     (lambda vkb=vkb: lambda e: e.memset(vkb[:, :, 64:65], 1.0))(),
                     writes=[("vkm", h01)], tag=t + "Bvm")
            for ti in range(NTT):
                pbV, pkV = self.bank()
                for k in range(DC):
                    S.op("tensor",
                         (lambda pbV=pbV, k=k, ti=ti, p=p, first=(k == 0),
                          last=(k == DC - 1): lambda e: e.matmul(
                              pbV[:, 0:128], xat[:, k, ti * 128:(ti + 1) * 128],
                              wv[:, k, p * 128:(p + 1) * 128],
                              start=first, stop=last))(),
                         reads=[(t + "wv", k), (t + "xat", k, ti // 4)],
                         writes=[pkV], tag=t + "Bv")
                for (h01, vkb) in ((0, vk0), (1, vk1)):
                    S.op("scalar",
                         (lambda pbV=pbV, ti=ti, h01=h01, vkb=vkb:
                          lambda e: e.copy(vkb[:, ti, 0:64],
                                           pbV[:, h01 * 64:h01 * 64 + 64]))(),
                         reads=[pkV], writes=[("vkm", h01)], tag=t + "Bve")
            for (h01, vkb) in ((0, vk0), (1, vk1)):
                h = 2 * p + h01
                S.dma("sync", self.vkhbm[h],
                      vkb[:].rearrange("p a b -> p (a b)"),
                      reads=[("vkm", h01)], writes=[("vh", h)], tag=t + "Bvs")
        scB.close()
        scAB.close()
        S.barrier()

        scB2 = self.scope()
        khs = [self.sal(scB2, t + f"_khs{i}", [65, L], BF16) for i in range(2)]
        qhs = [self.sal(scB2, t + f"_qhs{i}", [65, L], BF16) for i in range(2)]
        vks = [self.sal(scB2, t + f"_vks{i}", [128, 32, 65], BF16)
               for i in range(2)]
        aoh = [self.sal(scB2, t + f"_aoh{i}", [64, L], BF16) for i in range(2)]
        ptb = self.sal(scB2, t + "_ptb", [128, 32, 512], BF16)
        dscr = self.sal(scB2, t + "_dscr", [128, 512], F32)
        rmall = self.sal(scB2, t + "_rmall", [128, 320], F32)
        rm = self.sal(scB2, t + "_rm", [128, 1], F32)
        rlb = self.sal(scB2, t + "_rlb", [64, 512], F32)
        rl1 = self.sal(scB2, t + "_rl1", [1, 512], F32)
        wmsk = self.sal(scB2, t + "_wmsk", [128, 896], F32)
        S.dma("sync", wmsk[:], inp["wmask"], writes=["wmsk"], tag=t + "B2")
        for h in range(16):
            i = h % 2
            kh, qh, vk, ao1 = khs[i], qhs[i], vks[i], aoh[i]
            S.dma("sync", kh[0:64, :], self.khbm[h],
                  reads=[("kh", h), ("khs", i)], writes=[("khs", i)],
                  tag=t + "B2l")
            S.op("vector", (lambda kh=kh: lambda e: e.memset(kh[64:65, :], 1.0))(),
                 reads=[("khs", i)], writes=[("khs1", i)], tag=t + "B2l")
            S.dma("sync", qh[0:64, :], self.qhbm[h],
                  reads=[("qh", h), ("qhs", i)], writes=[("qhs", i)],
                  tag=t + "B2l")
            S.dma("sync", vk[:].rearrange("p a b -> p (a b)"), self.vkhbm[h],
                  reads=[("vh", h), ("vks", i)], writes=[("vks", i)],
                  tag=t + "B2l")
            nbs = []
            for qi in range(NTT):
                kmax = 128 * (qi + 1)
                nfull = (kmax - 128) // 512
                rfull = (kmax - 128) % 512
                nb = 0
                for kb in range(nfull + (1 if rfull else 0)):
                    kn = 512 if kb < nfull else rfull
                    pbS, pkS = self.bank(3 + (kb % 2))
                    S.op("tensor",
                         (lambda pbS=pbS, qh=qh, kh=kh, qi=qi, kb=kb, kn=kn:
                          lambda e: e.matmul(
                              pbS[:, 0:kn],
                              qh[0:64, qi * 128:qi * 128 + 128],
                              kh[0:64, kb * 512:kb * 512 + kn],
                              start=True, stop=True))(),
                         reads=[("qhs", i), ("khs", i)],
                         writes=[pkS], tag=t + "Bp1")
                    S.op("vector",
                         (lambda pbS=pbS, qi=qi, nb=nb, kn=kn: lambda e:
                          e.reduce_max(rmall[:, qi * 10 + nb:qi * 10 + nb + 1],
                                       pbS[:, 0:kn], axis=AX))(),
                         reads=[pkS], writes=[("rmall", qi, nb)],
                         tag=t + "Bp1r")
                    nb += 1
                pbD, pkD = self.bank(5)
                S.op("tensor",
                     (lambda pbD=pbD, qh=qh, kh=kh, qi=qi, kmax=kmax:
                      lambda e: e.matmul(
                          pbD[:, 0:128],
                          qh[0:64, qi * 128:qi * 128 + 128],
                          kh[0:64, kmax - 128:kmax],
                          start=True, stop=True))(),
                     reads=[("qhs", i), ("khs", i)],
                     writes=[pkD], tag=t + "Bp1d")
                S.op("vector",
                     (lambda pbD=pbD: lambda e: e.tensor_add(
                         dscr[:, 0:128], pbD[:, 0:128], self.tri[:]))(),
                     reads=[pkD, "tri"], writes=["dscr"], tag=t + "Bp1d")
                S.op("vector",
                     (lambda qi=qi, nb=nb: lambda e: e.reduce_max(
                         rmall[:, qi * 10 + nb:qi * 10 + nb + 1],
                         dscr[:, 0:128], axis=AX))(),
                     reads=["dscr"], writes=[("rmall", qi, nb)],
                     tag=t + "Bp1r")
                nbs.append(nb + 1)
            for qi in range(NTT):
                nb = nbs[qi]
                S.op("vector",
                     (lambda qi=qi, nb=nb: lambda e: e.tensor_reduce(
                         rm[:], rmall[:, qi * 10:qi * 10 + nb], axis=AX,
                         op=ALU.max))(),
                     reads=[("rmall", qi, j) for j in range(nb)] + ["rm"],
                     writes=["rm"], tag=t + "Bp1c")
                pbM, pkM = self.bank(6)
                S.op("tensor",
                     (lambda pbM=pbM: lambda e: e.matmul(
                         pbM[0:1, 0:128], rm[:], self.ident[:],
                         start=True, stop=True))(),
                     reads=["rm", "ident"], writes=[pkM], tag=t + "Bp1t")
                S.op("vector",
                     (lambda pbM=pbM, qh=qh, qi=qi: lambda e:
                      e.tensor_scalar(
                          qh[64:65, qi * 128:qi * 128 + 128],
                          pbM[0:1, 0:128], -1.0, None, ALU.mult))(),
                     reads=[pkM], writes=[("qh64", i, qi // 4)],
                     tag=t + "Bp1w")
            for qJ in range(NT):
                nkb = 4 * (qJ + 1)
                pvb, pvk = self.bank(2)

                def emit_s2(kb, qJ=qJ, i=i, kh=kh, qh=qh):
                    pbA, pkA = self.bank(kb % 2)
                    S.op("tensor",
                         (lambda pbA=pbA, kh=kh, qh=qh, kb=kb, qJ=qJ:
                          lambda e: e.matmul(
                              pbA[:],
                              kh[0:65, kb * 128:kb * 128 + 128],
                              qh[0:65, qJ * 512:qJ * 512 + 512],
                              start=True, stop=True))(),
                         reads=[("khs", i), ("khs1", i), ("qhs", i),
                                ("qh64", i, qJ)],
                         writes=[pkA], tag=t + "Bs2")
                    i_rel = kb - 4 * qJ
                    if i_rel >= 0:
                        S.op("vector",
                             (lambda pbA=pbA, i_rel=i_rel: lambda e:
                              e.tensor_add(dscr[:],
                                           pbA[:],
                                           wmsk[:, 384 - 128 * i_rel:
                                                896 - 128 * i_rel]))(),
                             reads=[pkA, "wmsk"], writes=["dscr"],
                             tag=t + "Bs2m")
                        S.op("scalar",
                             (lambda kb=kb: lambda e: e.activation(
                                 ptb[:, kb, :], dscr[:], AF.Exp,
                                 scale=0.125))(),
                             reads=["dscr"], writes=[("ptb", kb)],
                             tag=t + "Bex")
                    else:
                        S.op("scalar",
                             (lambda pbA=pbA, kb=kb: lambda e: e.activation(
                                 ptb[:, kb, :], pbA[:], AF.Exp,
                                 scale=0.125))(),
                             reads=[pkA], writes=[("ptb", kb)],
                             tag=t + "Bex")

                def emit_pv(kb, qJ=qJ, i=i, vk=vk, nkb=nkb, pvb=pvb, pvk=pvk):
                    S.op("tensor",
                         (lambda pvb=pvb, vk=vk, kb=kb, first=(kb == 0),
                          last=(kb == nkb - 1): lambda e: e.matmul(
                              pvb[0:65, :], vk[:, kb, :],
                              ptb[:, kb, :],
                              start=first, stop=last))(),
                         reads=[("vks", i), ("ptb", kb)],
                         writes=[pvk], tag=t + "Bpv")

                for kb in range(nkb):
                    emit_s2(kb)
                for kb in range(nkb):
                    emit_pv(kb)
                S.op("vector",
                     (lambda pvb=pvb: lambda e: e.reciprocal(
                         rl1[:], pvb[64:65, :]))(),
                     reads=[pvk], writes=["rl1"], tag=t + "Bnr")
                pbL, pkL = self.bank(7)
                S.op("tensor",
                     (lambda pbL=pbL: lambda e: e.matmul(
                         pbL[0:64, :], self.ones64, rl1[:],
                         start=True, stop=True))(),
                     reads=["rl1", "ones_r"], writes=[pkL], tag=t + "Bnb")
                S.op("scalar",
                     (lambda pbL=pbL: lambda e: e.copy(rlb[:], pbL[0:64, :]))(),
                     reads=[pkL], writes=["rlb"], tag=t + "Bne")
                S.op("vector",
                     (lambda pvb=pvb, ao1=ao1, qJ=qJ: lambda e: e.tensor_mul(
                         ao1[:, qJ * 512:(qJ + 1) * 512],
                         pvb[0:64, :], rlb[:]))(),
                     reads=[pvk, "rlb"], writes=[("aoh", i, qJ)],
                     tag=t + "Bno")
            S.dma("gpsimd", self.ao[h * 64:(h + 1) * 64, :], ao1[:],
                  reads=[("aoh", i, qJ2) for qJ2 in range(NT)],
                  writes=[("ao", h // 2)], tag=t + "Bst")
        scB2.close()
        S.barrier()

        scCD = self.scope()
        gfm = self.sal(scCD, t + "_gfm", [8, L], F32)

        scC = self.scope()
        wo = self.sal(scC, t + "_wo", [128, 8, 1024], BF16)
        aos = self.sal(scC, t + "_aos", [128, 8, 512], BF16)
        zsC = self.sal(scC, t + "_zsC", [128, 8, 512], F32)
        hs = self.sal(scC, t + "_hs", [128, 8, 512], F32)
        rw2 = self.sal(scC, t + "_rw2", [128, 8, 8], F32)
        nsb = self.sal(scC, t + "_nsb", [128, 32, 8], F32)
        gl = self.sal(scC, t + "_gl", [128, 4, 8], F32)
        gv = self.sal(scC, t + "_gv", [128, 4, 8], F32)
        gsc = self.sal(scC, t + "_gsc", [128, 4, 4], F32)
        rcol = self.sal(scC, t + "_rcol", [128, 4], F32)
        self.set_cast(scC, t + "C")
        self.cast_weight_to("fWo", wo, t + "wo")
        S.dma("sync", nsb[:], inp["fnoise"], writes=["nsb"], tag=t + "C")
        S.dma("sync", rw2[:], inp["frw"], writes=["rw2"], tag=t + "C")
        for c in range(DC):
            S.op("vector",
                 (lambda c=c: lambda e: e.tensor_scalar(
                     rw2[:, c, :], rw2[:, c, :], self.fscale2[:, c:c + 1],
                     None, ALU.mult))(),
                 reads=["rw2", "fsc2"], writes=["rw2"], tag=t + "C")
        for s in range(NT):
            t0 = s * 512
            for c in range(DC):
                S.dma("sync", aos[:, c, :],
                      self.ao[c * 128:(c + 1) * 128, t0:t0 + 512],
                      reads=[("ao", c), (t + "aos", c)],
                      writes=[(t + "aos", c)], tag=t + "Cld")
                S.dma("sync", zsC[:, c, :],
                      self.zst[c * 128:(c + 1) * 128, t0:t0 + 512],
                      reads=[("zst", c, s), (t + "zsC", c)],
                      writes=[(t + "zsC", c)], tag=t + "Cld")
            for m in range(DC):
                pbW, pkW = self.bank()
                for k in range(DC):
                    S.op("tensor",
                         (lambda pbW=pbW, m=m, k=k, first=(k == 0),
                          last=(k == DC - 1): lambda e: e.matmul(
                              pbW[:], wo[:, k, m * 128:(m + 1) * 128],
                              aos[:, k, :], start=first, stop=last))(),
                         reads=[(t + "wo", k), (t + "aos", k)],
                         writes=[pkW], tag=t + "Cmm")
                S.op("vector",
                     (lambda pbW=pbW, m=m: lambda e: e.tensor_add(
                         hs[:, m, :], pbW[:], zsC[:, m, :]))(),
                     reads=[pkW, (t + "zsC", m)], writes=[(t + "hs", m)],
                     tag=t + "Cha")
                S.dma("gpsimd", self.zst[m * 128:(m + 1) * 128, t0:t0 + 512],
                      hs[:, m, :], reads=[(t + "hs", m)],
                      writes=[("zst", m, s)], tag=t + "Chs")
            pbS, pkS = self.bank()
            for m in range(DC):
                S.op("scalar",
                     (lambda m=m: lambda e: e.activation(self.ln_sq[:],
                                                         hs[:, m, :],
                                                         AF.Square))(),
                     reads=[(t + "hs", m)], writes=["ln_sq"], tag=t + "Csq")
                S.op("tensor",
                     (lambda pbS=pbS, m=m: lambda e: e.matmul(
                         pbS[0:1, :], self.ones_p[:], self.ln_sq[:],
                         start=(m == 0), stop=(m == DC - 1)))(),
                     reads=["ln_sq", "ones_p"], writes=[pkS], tag=t + "Cst")
            sv = self.ln_sv
            S.op("vector", (lambda pbS=pbS: lambda e: e.tensor_scalar(
                sv[0:1, 512:1024], pbS[0:1, :], 1.0 / D, None, ALU.mult))(),
                 reads=[pkS], writes=["sv"], tag=t + "Cs2")
            S.op("scalar", lambda e: e.activation(sv[0:1, 1024:1536],
                                                  sv[0:1, 512:1024], AF.Sqrt,
                                                  bias=self.cbias1(RMS_EPS)),
                 reads=["sv", "cpool"], writes=["sv"], tag=t + "Cs2")
            S.op("vector", lambda e: e.reciprocal(sv[0:1, 1024:1536],
                                                  sv[0:1, 1024:1536]),
                 reads=["sv"], writes=["sv"], tag=t + "Cs2")
            pbB, pkB = self.bank()
            S.op("tensor", (lambda pbB=pbB: lambda e: e.matmul(
                pbB[:], self.ones_r[:], sv[0:1, 1024:1536],
                start=True, stop=True))(),
                 reads=["sv", "ones_r"], writes=[pkB], tag=t + "Cbc")
            for m in range(DC):
                S.op("vector",
                     (lambda m=m, pbB=pbB: lambda e: e.scalar_tensor_tensor(
                         aos[:, m, :], hs[:, m, :],
                         self.fscale2[:, m:m + 1], pbB[:],
                         ALU.mult, ALU.mult))(),
                     reads=[(t + "hs", m), pkB, "fsc2"],
                     writes=[(t + "aos", m)], tag=t + "Cxm")
                S.dma("gpsimd", self.xmoe[m * 128:(m + 1) * 128, t0:t0 + 512],
                      aos[:, m, :], reads=[(t + "aos", m)],
                      writes=[("xmoe", m, s)], tag=t + "Cxs")
            for tj in range(4):
                ti = s * 4 + tj
                pbR, pkR = self.bank()
                for k in range(DC):
                    S.op("tensor",
                         (lambda pbR=pbR, k=k, tj=tj, first=(k == 0),
                          last=(k == DC - 1): lambda e: e.matmul(
                              pbR[:, 0:8], hs[:, k, tj * 128:(tj + 1) * 128],
                              rw2[:, k, :], start=first, stop=last))(),
                         reads=[(t + "hs", k), "rw2"], writes=[pkR],
                         tag=t + "Crt")
                pbX, pkX = self.bank()
                S.op("tensor",
                     (lambda pbX=pbX, tj=tj: lambda e: e.matmul(
                         pbX[:, 0:1],
                         sv[0:1, 1024 + tj * 128:1024 + tj * 128 + 128],
                         self.ones_r[0:1, 0:1], start=True, stop=True))(),
                     reads=["sv", "ones_r"], writes=[pkX], tag=t + "Crc")
                S.op("scalar",
                     (lambda pbX=pbX, tj=tj: lambda e: e.copy(
                         rcol[:, tj:tj + 1], pbX[:, 0:1]))(),
                     reads=[pkX], writes=[("rcol", tj)], tag=t + "Crc")
                S.op("vector",
                     (lambda pbR=pbR, tj=tj, ti=ti: lambda e:
                      e.scalar_tensor_tensor(
                          gl[:, tj, :], pbR[:, 0:8], rcol[:, tj:tj + 1],
                          nsb[:, ti, :], ALU.mult, ALU.add))(),
                     reads=[pkR, ("rcol", tj), "nsb"], writes=[("gl", tj)],
                     tag=t + "Cg")
            def gstep(fn, rd, wr):
                for tj in range(4):
                    S.op("vector", fn(tj), reads=[(r, tj) for r in rd],
                         writes=[(w, tj) for w in wr], tag=t + "Cg")
            gstep(lambda tj: (lambda e: e.reduce_max(gsc[:, tj, 0:1],
                                                     gl[:, tj, :], axis=AX)),
                  ["gl"], ["gsc"])
            gstep(lambda tj: (lambda e: e.tensor_scalar(gsc[:, tj, 1:2],
                                                        gsc[:, tj, 0:1], -1.0,
                                                        None, ALU.mult)),
                  ["gsc"], ["gsc"])
            for tj in range(4):
                S.op("scalar",
                     (lambda tj=tj: lambda e: e.activation(
                         gv[:, tj, :], gl[:, tj, :], AF.Exp,
                         bias=gsc[:, tj, 1:2]))(),
                     reads=[("gl", tj), ("gsc", tj)], writes=[("gv", tj)],
                     tag=t + "Cg")
            gstep(lambda tj: (lambda e: e.reduce_sum(gsc[:, tj, 2:3],
                                                     gv[:, tj, :], axis=AX)),
                  ["gv"], ["gsc"])
            gstep(lambda tj: (lambda e: e.reciprocal(gsc[:, tj, 3:4],
                                                     gsc[:, tj, 2:3])),
                  ["gsc"], ["gsc"])
            gstep(lambda tj: (lambda e: e.tensor_scalar(gv[:, tj, :],
                                                        gv[:, tj, :],
                                                        gsc[:, tj, 3:4],
                                                        None, ALU.mult)),
                  ["gv", "gsc"], ["gv"])
            gstep(lambda tj: (lambda e: e.reduce_max(gsc[:, tj, 0:1],
                                                     gv[:, tj, :], axis=AX)),
                  ["gv"], ["gsc"])
            gstep(lambda tj: (lambda e: e.tensor_scalar(gl[:, tj, :],
                                                        gv[:, tj, :],
                                                        gsc[:, tj, 0:1],
                                                        None, ALU.is_equal)),
                  ["gv", "gsc"], ["gl"])
            gstep(lambda tj: (lambda e: e.scalar_tensor_tensor(
                gl[:, tj, :], gl[:, tj, :], -2.0, gv[:, tj, :],
                ALU.mult, ALU.add)), ["gl", "gv"], ["gl"])
            gstep(lambda tj: (lambda e: e.reduce_max(gsc[:, tj, 1:2],
                                                     gl[:, tj, :], axis=AX)),
                  ["gl"], ["gsc"])
            gstep(lambda tj: (lambda e: e.tensor_scalar(gl[:, tj, :],
                                                        gv[:, tj, :],
                                                        gsc[:, tj, 1:2],
                                                        None, ALU.is_ge)),
                  ["gv", "gsc"], ["gl"])
            gstep(lambda tj: (lambda e: e.tensor_mul(gv[:, tj, :],
                                                     gv[:, tj, :],
                                                     gl[:, tj, :])),
                  ["gv", "gl"], ["gv"])
            for tj in range(4):
                ti = s * 4 + tj
                pbT, pkT = self.bank()
                S.op("tensor",
                     (lambda pbT=pbT, tj=tj: lambda e: e.matmul(
                         pbT[0:8, 0:128], gv[:, tj, :], self.ident[:],
                         start=True, stop=True))(),
                     reads=[("gv", tj), "ident"], writes=[pkT], tag=t + "Cgt")
                S.op("scalar",
                     (lambda pbT=pbT, ti=ti: lambda e: e.copy(
                         gfm[0:8, ti * 128:(ti + 1) * 128],
                         pbT[0:8, 0:128]))(),
                     reads=[pkT], writes=[("gfm", ti // 4)], tag=t + "Cge")
        scC.close()
        S.barrier()

        scD = self.scope()
        xm = self.sal(scD, t + "_xm", [128, 8, 4, 512], BF16)
        yb = self.sal(scD, t + "_yb", [128, 8, 4, 512], F32)
        hb = self.sal(scD, t + "_hb", [128, 8, 512], BF16)
        ws = [self.sal(scD, t + f"_ws{j}", [128, 8, 1024], BF16)
              for j in range(3)]
        gsb = self.sal(scD, t + "_gsb", [128, 512], F32)
        asc = self.sal(scD, t + "_asc", [128, 512], F32)
        bsc = self.sal(scD, t + "_bsc", [128, 512], F32)
        b1a = self.sal(scD, t + "_b1a", [128, 8, 8], F32)
        b2a = self.sal(scD, t + "_b2a", [128, 8, 8], F32)
        boh = self.sal(scD, t + "_boh", [8, 1024], F32)
        exs = self.sal(scD, t + "_exs", [128, 2, 512], F32)
        ehs = self.sal(scD, t + "_ehs", [128, 2, 512], F32)
        S.dma("sync", b1a[:], inp["fb1"].rearrange("e p c -> p e c"),
              writes=["b1a"], tag=t + "D")
        S.dma("sync", b2a[:], inp["fb2"].rearrange("e p c -> p e c"),
              writes=["b2a"], tag=t + "D")

        oh8 = self.sal(scD, t + "_oh8", [8, 1024], F32)
        S.dma("sync", oh8[:], inp["oh8"], writes=["oh8"], tag=t + "D")
        wser = 0
        for hf in range(2):
            h0 = hf * 2048
            for c in range(DC):
                S.dma("sync", xm[:, c, :, :].rearrange("p a b -> p (a b)"),
                      self.xmoe[c * 128:(c + 1) * 128, h0:h0 + 2048],
                      reads=[("xmoe", c, 4 * hf), ("xmoe", c, 4 * hf + 1),
                             ("xmoe", c, 4 * hf + 2), ("xmoe", c, 4 * hf + 3),
                             (t + "xm", c)],
                      writes=[(t + "xm", c)], tag=t + "Dxm")
            for e8 in range(EXPERTS):
                wkeys = []
                for j in range(3):
                    sl = wser % 3
                    wser += 1
                    S.dma("sync", ws[sl][:].rearrange("p c m -> p (c m)"),
                          self.wcache[e8 * 3 + j],
                          reads=[("wc", e8 * 3 + j), ("wsl", sl)],
                          writes=[("wsl", sl)], tag=t + "Dwl")
                    wkeys.append(("wsl", sl))
                w1s, w2s, wos = ws[(wser - 3) % 3], ws[(wser - 2) % 3], ws[(wser - 1) % 3]
                k1, k2, ko = wkeys
                S.dma("sync", boh[:], inp["fboutoh"][0:8, e8 * 1024:(e8 + 1) * 1024],
                      reads=["boh"], writes=["boh"], tag=t + "Dbl")
                for n in range(4):
                    gn = hf * 2048 + n * 512
                    pbG, pkG = self.bank(0)
                    S.op("tensor",
                         (lambda pbG=pbG, e8=e8, gn=gn: lambda e: e.matmul(
                             pbG[:], oh8[0:8, e8 * 128:(e8 + 1) * 128],
                             gfm[0:8, gn:gn + 512], start=True, stop=True))(),
                         reads=[("gfm", gn // 512), "oh8"],
                         writes=[pkG], tag=t + "Dg")
                    S.op("scalar",
                         (lambda pbG=pbG: lambda e: e.copy(gsb[:], pbG[:]))(),
                         reads=[pkG], writes=["gsb"], tag=t + "Dge")
                    for m in range(DC):
                        pbA, pkA = self.bank(1 + (m % 2))
                        for k in range(DC):
                            S.op("tensor",
                                 (lambda pbA=pbA, w1s=w1s, k=k, m=m, n=n,
                                  first=(k == 0), last=(k == DC - 1):
                                  lambda e: e.matmul(
                                      pbA[:], w1s[:, k, m * 128:(m + 1) * 128],
                                      xm[:, k, n, :], start=first, stop=last))(),
                                 reads=[k1, (t + "xm", k)], writes=[pkA],
                                 tag=t + "Da")
                        S.op("scalar",
                             (lambda pbA=pbA, e8=e8, m=m: lambda e: e.activation(
                                 asc[:], pbA[:], AF.Silu,
                                 bias=b1a[:, e8, m:m + 1]))(),
                             reads=[pkA, "b1a"], writes=["asc"], tag=t + "Dae")
                        pbB, pkB = self.bank(3 + (m % 2))
                        for k in range(DC):
                            S.op("tensor",
                                 (lambda pbB=pbB, w2s=w2s, k=k, m=m, n=n,
                                  first=(k == 0), last=(k == DC - 1):
                                  lambda e: e.matmul(
                                      pbB[:], w2s[:, k, m * 128:(m + 1) * 128],
                                      xm[:, k, n, :], start=first, stop=last))(),
                                 reads=[k2, (t + "xm", k)], writes=[pkB],
                                 tag=t + "Db")
                        S.op("vector",
                             (lambda pbB=pbB, e8=e8, m=m: lambda e:
                              e.scalar_tensor_tensor(
                                  bsc[:], pbB[:], b2a[:, e8, m:m + 1], gsb[:],
                                  ALU.add, ALU.mult))(),
                             reads=[pkB, "b2a", "gsb"], writes=["bsc"],
                             tag=t + "Dbe")
                        S.op("vector",
                             (lambda m=m: lambda e: e.tensor_mul(
                                 hb[:, m, :], asc[:], bsc[:]))(),
                             reads=["asc", "bsc"], writes=[("hb", m)],
                             tag=t + "Dh")
                    for m2 in range(DC):
                        pbY, pkY = self.bank(5 + (m2 % 2))
                        for k in range(DC):
                            S.op("tensor",
                                 (lambda pbY=pbY, wos=wos, k=k, m2=m2,
                                  first=(k == 0): lambda e: e.matmul(
                                      pbY[:], wos[:, k, m2 * 128:(m2 + 1) * 128],
                                      hb[:, k, :], start=first, stop=False))(),
                                 reads=[ko, ("hb", k)], writes=[pkY],
                                 tag=t + "Dw")
                        S.op("tensor",
                             (lambda pbY=pbY, e8=e8, m2=m2, gn=gn:
                              lambda e: e.matmul(
                                  pbY[:], boh[0:8, m2 * 128:
                                              (m2 + 1) * 128],
                                  gfm[0:8, gn:gn + 512],
                                  start=False, stop=True))(),
                             reads=["boh", ("gfm", gn // 512)], writes=[pkY],
                             tag=t + "Dwb")
                        if e8 == 0:
                            S.op("vector",
                                 (lambda pbY=pbY, m2=m2, n=n: lambda e:
                                  e.tensor_copy(yb[:, m2, n, :], pbY[:]))(),
                                 reads=[pkY], writes=[(t + "yb", m2, n)],
                                 tag=t + "Dy")
                        else:
                            S.op("vector",
                                 (lambda pbY=pbY, m2=m2, n=n: lambda e:
                                  e.tensor_add(yb[:, m2, n, :], pbY[:],
                                               yb[:, m2, n, :]))(),
                                 reads=[pkY, (t + "yb", m2, n)],
                                 writes=[(t + "yb", m2, n)], tag=t + "Dy")
            for n in range(4):
                s = hf * 4 + n
                t0 = s * 512
                for c in range(DC):
                    sl = c % 2
                    S.dma("sync", exs[:, sl, :],
                          self.xres[c * 128:(c + 1) * 128, t0:t0 + 512],
                          reads=[("xres", c, s), (t + "exs", sl)],
                          writes=[(t + "exs", sl)], tag=t + "Del")
                    S.dma("sync", ehs[:, sl, :],
                          self.zst[c * 128:(c + 1) * 128, t0:t0 + 512],
                          reads=[("zst", c, s), (t + "ehs", sl)],
                          writes=[(t + "ehs", sl)], tag=t + "Del")
                    S.op("vector",
                         (lambda c=c, n=n, sl=sl: lambda e: e.tensor_add(
                             ehs[:, sl, :], yb[:, c, n, :], ehs[:, sl, :]))(),
                         reads=[(t + "yb", c, n), (t + "ehs", sl)],
                         writes=[(t + "ehs", sl)], tag=t + "De1")
                    S.op("vector",
                         (lambda c=c, sl=sl: lambda e: e.tensor_add(
                             exs[:, sl, :], ehs[:, sl, :], exs[:, sl, :]))(),
                         reads=[(t + "ehs", sl), (t + "exs", sl)],
                         writes=[(t + "exs", sl)], tag=t + "De2")
                    S.dma("gpsimd",
                          self.xres[c * 128:(c + 1) * 128, t0:t0 + 512],
                          exs[:, sl, :], reads=[(t + "exs", sl)],
                          writes=[("xres", c, s)], tag=t + "Des")
        scD.close()
        scCD.close()
        S.barrier()


    def mamba(self):
        nc, S = self.nc, self.S
        inp = self.inp
        HL = L // 2
        sc = self.scope()
        wup = self.sal(sc, "m_wup", [128, 8, 256], BF16)
        wde = self.sal(sc, "m_wde", [128, 128], BF16)
        wB = self.sal(sc, "m_wB", [128, 64], BF16)
        wC = self.sal(sc, "m_wC", [128, 64], BF16)
        wdn = self.sal(sc, "m_wdn", [128, 1024], BF16)
        moh = self.sal(sc, "m_oh", [64, 8192], BF16)
        self.set_cast(sc, "m")
        self.cast_weight_to("mWup", wup, "m_wup")
        stg = self._cast_stg[0]
        for (wn, dst, dk) in (("mWdelta", wde, "m_wde"), ("mWB", wB, "m_wB"),
                              ("mWC", wC, "m_wC"), ("mWdown", wdn, "m_wdn")):
            dout = inp[wn].shape[1]
            S.dma("sync", stg[:, 0:dout], inp[wn], reads=[("cstg", 0)],
                  writes=[("cstg", 0)], tag="m_cw")
            S.op("scalar",
                 (lambda dst=dst, dout=dout, stg=stg:
                  lambda e: e.copy(dst[:, 0:dout], stg[:, 0:dout]))(),
                 reads=[("cstg", 0)], writes=[(dk,)], tag="m_cw")
        S.dma("sync", moh[:], inp["moh"], writes=["moh"], tag="m_c")
        vecs = self.sal(sc, "m_vec", [128, 32], F32)
        S.dma("sync", vecs[:, 0:2], inp["mbup"], writes=["mvec"], tag="m_c")
        S.dma("sync", vecs[:, 2:5], inp["mconvw"], writes=["mvec"], tag="m_c")
        S.dma("sync", vecs[:, 5:6], inp["mconvb"], writes=["mvec"], tag="m_c")
        S.dma("sync", vecs[:, 6:7], inp["mbdelta"], writes=["mvec"], tag="m_c")
        S.dma("sync", vecs[:, 7:8], inp["mdskip"], writes=["mvec"], tag="m_c")
        S.dma("sync", vecs[:, 8:16], inp["mbdown"], writes=["mvec"], tag="m_c")
        bBC = self.sal(sc, "m_bbc", [64, 2], F32)
        S.dma("sync", bBC[:, 0:1], inp["mbB"], writes=["mbbc"], tag="m_c")
        S.dma("sync", bBC[:, 1:2], inp["mbC"], writes=["mbbc"], tag="m_c")
        amat = self.sal(sc, "m_amat", [128, 64], F32)
        S.dma("sync", amat[:], inp["mAlog"], writes=["amat"], tag="m_c")
        S.op("scalar", lambda e: e.activation(amat[:], amat[:], AF.Exp),
             reads=["amat"], writes=["amat"], tag="m_c")
        S.op("vector", lambda e: e.tensor_scalar(amat[:], amat[:], -1.0, None,
                                                 ALU.mult),
             reads=["amat"], writes=["amat"], tag="m_c")
        mcar = self.sal(sc, "m_car", [128, 64], F32)
        S.op("vector", lambda e: e.memset(mcar[:], 0.0), writes=["mcar"],
             tag="m_c")

        lpad = self.sal(sc, "m_lpad", [128, 2 + HL], F32)
        rsig = self.sal(sc, "m_rsig", [128, HL], BF16)
        xmf = self.sal(sc, "m_xmf", [128, HL], F32)
        xmb = self.sal(sc, "m_xmb", [128, HL], BF16)
        delta = self.sal(sc, "m_delta", [128, HL], F32)
        dxb = self.sal(sc, "m_dx", [128, HL], F32)
        Bmat = self.sal(sc, "m_B", [64, HL], BF16)
        Cmat = self.sal(sc, "m_C", [64, HL], BF16)
        da = self.sal(sc, "m_da", [128, HL], F32)
        dbx = self.sal(sc, "m_dbx", [128, HL], F32)
        yac = self.sal(sc, "m_y", [128, HL], F32)
        ytmp = self.sal(sc, "m_yt", [128, 2, 512], F32)
        cbe = self.sal(sc, "m_cbe", [128, 512], F32)
        exm = self.sal(sc, "m_exm", [128, 2, 512], F32)
        scM1 = self.scope()
        xsm = self.sal(scM1, "m_xs", [128, 8, 512], F32)
        zsm = self.sal(scM1, "m_zs", [128, 8, 512], BF16)
        S.op("vector", lambda e: e.memset(lpad[:, 0:2], 0.0), writes=["lpad0"],
             tag="m_i")
        NH2 = NT // 2

        for hf in range(2):
            if hf == 1:
                S.op("scalar", lambda e: e.copy(lpad[:, 0:2], lpad[:, HL:HL + 2]),
                     reads=[("lpad", 3)], writes=["lpad0"], tag="m_bd")
            for sh in range(NH2):
                s = hf * NH2 + sh
                t0 = s * 512
                th = sh * 512
                for c in range(DC):
                    S.dma("sync", xsm[:, c, :],
                          self.xres[c * 128:(c + 1) * 128, t0:t0 + 512],
                          reads=[("xres", c, s), ("m_xs", c)],
                          writes=[("m_xs", c)], tag="m_ld")
                self.ln_strip(xsm, "m_xs", zsm, "m_zs", self.ln_sq)
                pbL, pkL = self.bank()
                for k in range(DC):
                    S.op("tensor",
                         (lambda pbL=pbL, k=k, first=(k == 0),
                          last=(k == DC - 1):
                          lambda e: e.matmul(pbL[:], wup[:, k, 0:128],
                                             zsm[:, k, :], start=first,
                                             stop=last))(),
                         reads=[("m_wup", k), ("m_zs", k)], writes=[pkL],
                         tag="m_um")
                S.op("vector",
                     (lambda pbL=pbL, th=th: lambda e: e.tensor_scalar(
                         lpad[:, 2 + th:2 + th + 512], pbL[:], vecs[:, 0:1],
                         None, ALU.add))(),
                     reads=[pkL, "mvec"], writes=[("lpad", sh)], tag="m_ue")
                pbR, pkR = self.bank()
                for k in range(DC):
                    S.op("tensor",
                         (lambda pbR=pbR, k=k, first=(k == 0),
                          last=(k == DC - 1):
                          lambda e: e.matmul(pbR[:], wup[:, k, 128:256],
                                             zsm[:, k, :], start=first,
                                             stop=last))(),
                         reads=[("m_wup", k), ("m_zs", k)], writes=[pkR],
                         tag="m_um")
                S.op("scalar",
                     (lambda pbR=pbR, th=th: lambda e: e.activation(
                         rsig[:, th:th + 512], pbR[:], AF.Silu,
                         bias=vecs[:, 1:2]))(),
                     reads=[pkR, "mvec"], writes=[("rsig", sh)], tag="m_ue")
            lk = [("lpad", sh) for sh in range(NH2)] + ["lpad0"]
            S.op("vector", lambda e: e.tensor_scalar(xmf[:], lpad[:, 0:HL],
                                                     vecs[:, 2:3], None,
                                                     ALU.mult),
                 reads=lk + ["mvec", "xmf"], writes=["xmf"], tag="m_cv")
            S.op("vector", lambda e: e.scalar_tensor_tensor(
                xmf[:], lpad[:, 1:1 + HL], vecs[:, 3:4], xmf[:], ALU.mult,
                ALU.add),
                 reads=lk + ["mvec", "xmf"], writes=["xmf"], tag="m_cv")
            S.op("vector", lambda e: e.scalar_tensor_tensor(
                xmf[:], lpad[:, 2:2 + HL], vecs[:, 4:5], xmf[:], ALU.mult,
                ALU.add),
                 reads=lk + ["mvec", "xmf"], writes=["xmf"], tag="m_cv")
            S.op("scalar", lambda e: e.activation(xmf[:], xmf[:], AF.Silu,
                                                  bias=vecs[:, 5:6]),
                 reads=["xmf", "mvec"], writes=["xmf"], tag="m_cv")
            S.op("scalar", lambda e: e.copy(xmb[:], xmf[:]),
                 reads=["xmf"], writes=["xmb"], tag="m_cv")
            for nblk in range(NH2):
                th = nblk * 512
                pbD, pkD = self.bank()
                S.op("tensor",
                     (lambda pbD=pbD, th=th: lambda e: e.matmul(
                         pbD[:], wde[:], xmb[:, th:th + 512], start=True,
                         stop=True))(),
                     reads=[("m_wde",), "xmb"], writes=[pkD], tag="m_dm")
                S.op("vector",
                     (lambda pbD=pbD: lambda e: e.tensor_scalar(
                         cbe[:], pbD[:], vecs[:, 6:7], None, ALU.add))(),
                     reads=[pkD, "mvec"], writes=["m_cbe"], tag="m_dsp")
                S.op("scalar",
                     (lambda th=th: lambda e: e.activation(
                         delta[:, th:th + 512], cbe[:], AF.Relu))(),
                     reads=["m_cbe"], writes=[("delta", nblk)], tag="m_dsp")
                S.op("scalar",
                     lambda e: e.activation(cbe[:], cbe[:], AF.Abs),
                     reads=["m_cbe"], writes=["m_cbe"], tag="m_dsp")
                S.op("scalar", lambda e: e.activation(cbe[:], cbe[:], AF.Exp,
                                                      scale=-1.0),
                     reads=["m_cbe"], writes=["m_cbe"], tag="m_dsp")
                S.op("scalar", lambda e: e.activation(cbe[:], cbe[:], AF.Ln,
                                                      bias=self.cbias(1.0)),
                     reads=["m_cbe", "cpool"], writes=["m_cbe"], tag="m_dsp")
                S.op("vector",
                     (lambda th=th: lambda e: e.tensor_add(
                         delta[:, th:th + 512], delta[:, th:th + 512],
                         cbe[:]))(),
                     reads=[("delta", nblk), "m_cbe"],
                     writes=[("delta", nblk)], tag="m_dsp")
                for (wmat, wkey, dst, dk, bcol) in (
                        (wB, "m_wB", Bmat, "m_B", 0),
                        (wC, "m_wC", Cmat, "m_C", 1)):
                    pbB, pkB = self.bank()
                    S.op("tensor",
                         (lambda pbB=pbB, wmat=wmat, th=th: lambda e: e.matmul(
                             pbB[0:64, :], wmat[:], xmb[:, th:th + 512],
                             start=True, stop=True))(),
                         reads=[(wkey,), "xmb"], writes=[pkB], tag="m_bc")
                    S.op("vector",
                         (lambda pbB=pbB, dst=dst, th=th, bcol=bcol:
                          lambda e: e.tensor_scalar(
                              dst[0:64, th:th + 512], pbB[0:64, :],
                              bBC[:, bcol:bcol + 1], None, ALU.add))(),
                         reads=[pkB, "mbbc"], writes=[(dk, nblk)], tag="m_bc")
            dk_all = [("delta", nblk) for nblk in range(NH2)]
            S.op("vector", lambda e: e.tensor_mul(dxb[:], delta[:], xmf[:]),
                 reads=dk_all + ["xmf", "m_dxk"], writes=["m_dxk"], tag="m_dx")
            bk_all = [("m_B", nblk) for nblk in range(NH2)]
            ck_all = [("m_C", nblk) for nblk in range(NH2)]
            for n64 in range(64):
                S.op("vector",
                     (lambda n64=n64: lambda e: e.tensor_scalar(
                         da[:], delta[:], amat[:, n64:n64 + 1], None,
                         ALU.mult))(),
                     reads=dk_all + ["amat", "m_dak"], writes=["m_dak"],
                     tag="m_n1")
                S.op("scalar", lambda e: e.activation(da[:], da[:], AF.Exp),
                     reads=["m_dak"], writes=["m_dak"], tag="m_n2")
                for nblk in range(NH2):
                    th = nblk * 512
                    pbB, pkB = self.bank(0 if nblk % 2 == 0 else 1)
                    S.op("tensor",
                         (lambda pbB=pbB, n64=n64, th=th: lambda e: e.matmul(
                             pbB[:], moh[:, n64 * 128:(n64 + 1) * 128],
                             Bmat[0:64, th:th + 512], start=True, stop=True))(),
                         reads=[bk_all[nblk], "moh"], writes=[pkB], tag="m_n3")
                    S.op("vector",
                         (lambda pbB=pbB, th=th: lambda e: e.tensor_mul(
                             dbx[:, th:th + 512], dxb[:, th:th + 512],
                             pbB[:]))(),
                         reads=[pkB, "m_dxk", "m_dbk"], writes=["m_dbk"],
                         tag="m_n3")
                S.op("vector",
                     (lambda n64=n64: lambda e: e.tensor_tensor_scan(
                         dbx[:], da[:], dbx[:], mcar[:, n64:n64 + 1],
                         ALU.mult, ALU.add))(),
                     reads=["m_dak", "m_dbk", ("mcar", n64)],
                     writes=["m_dbk"], tag="m_n4")
                if hf == 0:
                    S.op("scalar",
                         (lambda n64=n64: lambda e: e.copy(
                             mcar[:, n64:n64 + 1], dbx[:, HL - 1:HL]))(),
                         reads=["m_dbk"], writes=[("mcar", n64)], tag="m_n4c")
                for nblk in range(NH2):
                    th = nblk * 512
                    sl = nblk % 2
                    pbC, pkC = self.bank(2 if nblk % 2 == 0 else 3)
                    S.op("tensor",
                         (lambda pbC=pbC, n64=n64, th=th: lambda e: e.matmul(
                             pbC[:], moh[:, n64 * 128:(n64 + 1) * 128],
                             Cmat[0:64, th:th + 512], start=True, stop=True))(),
                         reads=[ck_all[nblk], "moh"], writes=[pkC], tag="m_n5")
                    S.op("vector",
                         (lambda pbC=pbC, th=th, sl=sl: lambda e: e.tensor_mul(
                             ytmp[:, sl, :], dbx[:, th:th + 512], pbC[:]))(),
                         reads=[pkC, "m_dbk", ("m_ytk", sl)],
                         writes=[("m_ytk", sl)], tag="m_n5")
                    if n64 == 0:
                        S.op("gpsimd",
                             (lambda th=th, sl=sl: lambda e: e.tensor_copy(
                                 yac[:, th:th + 512], ytmp[:, sl, :]))(),
                             reads=[("m_ytk", sl)], writes=[("m_yk", nblk)],
                             tag="m_n6")
                    else:
                        S.op("gpsimd",
                             (lambda th=th, sl=sl: lambda e: e.tensor_add(
                                 yac[:, th:th + 512], ytmp[:, sl, :],
                                 yac[:, th:th + 512]))(),
                             reads=[("m_ytk", sl), ("m_yk", nblk)],
                             writes=[("m_yk", nblk)], tag="m_n6")
            yk_all = [("m_yk", nblk) for nblk in range(NH2)]
            S.op("vector", lambda e: e.scalar_tensor_tensor(
                yac[:], xmf[:], vecs[:, 7:8], yac[:], ALU.mult, ALU.add),
                 reads=["xmf", "mvec"] + yk_all, writes=yk_all, tag="m_y")
            rk_all = [("rsig", sh) for sh in range(NH2)]
            S.op("vector", lambda e: e.tensor_mul(xmb[:], yac[:], rsig[:]),
                 reads=yk_all + rk_all + ["xmb"], writes=["xmb"], tag="m_g")
            for nblk in range(NH2):
                s = hf * NH2 + nblk
                t0 = s * 512
                th = nblk * 512
                for m in range(DC):
                    sl = m % 2
                    pbY, pkY = self.bank(4 + sl)
                    S.op("tensor",
                         (lambda pbY=pbY, m=m, th=th: lambda e: e.matmul(
                             pbY[:], wdn[:, m * 128:(m + 1) * 128],
                             xmb[:, th:th + 512], start=True, stop=True))(),
                         reads=[("m_wdn",), "xmb"], writes=[pkY], tag="m_om")
                    S.dma("sync", exm[:, sl, :],
                          self.xres[m * 128:(m + 1) * 128, t0:t0 + 512],
                          reads=[("xres", m, s), ("m_exm", sl)],
                          writes=[("m_exm", sl)], tag="m_ol")
                    S.op("vector",
                         (lambda pbY=pbY, m=m, sl=sl: lambda e:
                          e.scalar_tensor_tensor(
                              exm[:, sl, :], pbY[:], vecs[:, 8 + m:9 + m],
                              exm[:, sl, :], ALU.add, ALU.add))(),
                         reads=[pkY, "mvec", ("m_exm", sl)],
                         writes=[("m_exm", sl)], tag="m_oa")
                    S.dma("gpsimd",
                          self.xres[m * 128:(m + 1) * 128, t0:t0 + 512],
                          exm[:, sl, :], reads=[("m_exm", sl)],
                          writes=[("xres", m, s)], tag="m_os")
        scM1.close()
        sc.close()
        S.barrier()

    def p7_out(self):
        S = self.S
        S.barrier()
        sc7 = self.scope()
        tin = self.sal(sc7, "t7_tin", [128, 4, 1024], F32)
        tst = self.sal(sc7, "t7_tst", [128, 8, 512], F32)
        for g in range(8):
            for c in range(DC):
                S.dma("sync", tst[:, c, :],
                      self.xres[c * 128:(c + 1) * 128, g * 512:(g + 1) * 512],
                      reads=[("xres", c, g), ("tst", c)], writes=[("tst", c)],
                      tag="p7ld")
            for j in range(4):
                for c in range(DC):
                    pb, pk = self.bank()
                    S.op("tensor",
                         (lambda pb=pb, c=c, j=j: lambda e: e.transpose(
                             pb[:, 0:128], tst[:, c, j * 128:(j + 1) * 128],
                             self.ident[:]))(),
                         reads=[("tst", c), "ident"], writes=[pk], tag="p7t")
                    S.op("scalar",
                         (lambda pb=pb, c=c, j=j: lambda e: e.copy(
                             tin[:, j, c * 128:(c + 1) * 128], pb[:, 0:128]))(),
                         reads=[pk], writes=[("tin", j)], tag="p7e")
            for j in range(4):
                ti = g * 4 + j
                S.dma("gpsimd", self.out[ti * 128:(ti + 1) * 128, :],
                      tin[:, j, :], reads=[("tin", j)], writes=[("out", ti)],
                      tag="p7st")
        sc7.close()

    def finalize(self):
        S = self.S
        S.barrier()
        S.op("vector", lambda e: e.memset(self.ones_p[0:1, 0:1], 1.0),
             writes=["ones_p"], tag="fin")
        S.emit(self.stack)
        return self.nc

    def build(self):
        nc = self.nc
        self.ln_sq = nc.alloc_sbuf_tensor("ln_sq", [128, 512], F32)
        self.ln_nsc = nc.alloc_sbuf_tensor("ln_nsc", [128, 512], F32)
        self.ln_sv = nc.alloc_sbuf_tensor("ln_sv", [1, 4 * 512], F32)
        self.load_consts()
        self.emit_cpool()
        self.p0_transpose_in()
        if self.upto >= 1:
            self.p1_attn_a()
        if self.upto >= 2:
            self.ffn("F1")
        if self.upto >= 3:
            self.mamba()
        if self.upto >= 4:
            self.ffn("F2")
        if self.dbg:
            S = self.S
            S.barrier()
            dz = self.nc.dram_tensor("dzst", [D, L], F32, kind="ExternalOutput").ap()
            da_ = self.nc.dram_tensor("dao", [D, L], BF16, kind="ExternalOutput").ap()
            dx_ = self.nc.dram_tensor("dxmoe", [D, L], BF16, kind="ExternalOutput").ap()
            S.dma("sync", dz, self.zst, reads=[], writes=["dbg1"], tag="dbg")
            dk_ = self.nc.dram_tensor("dkh", [16, 64, L], BF16, kind="ExternalOutput").ap()
            dq_ = self.nc.dram_tensor("dqh", [16, 64, L], BF16, kind="ExternalOutput").ap()
            dv_ = self.nc.dram_tensor("dvk", [16, 128, 32 * 65], BF16, kind="ExternalOutput").ap()
            S.dma("sync", dk_, self.khbm, reads=[], writes=["dbg4"], tag="dbg")
            S.dma("sync", dq_, self.qhbm, reads=[], writes=["dbg5"], tag="dbg")
            S.dma("sync", dv_, self.vkhbm, reads=[], writes=["dbg6"], tag="dbg")
            S.dma("sync", da_, self.ao, reads=[], writes=["dbg2"], tag="dbg")
            S.dma("sync", dx_, self.xmoe, reads=[], writes=["dbg3"], tag="dbg")
        self.p7_out()
        return self.finalize()


def build_program(upto=99, dbg=False):
    mk = MK(upto, dbg)
    nc = mk.build()
    return nc, mk


def make_in_maps(inputs, n_cores=8):
    import ml_dtypes
    bf16 = ml_dtypes.bfloat16
    g = lambda k: np.ascontiguousarray(np.asarray(inputs[k], dtype=np.float32))
    ppvec = lambda v: np.ascontiguousarray(v.reshape(DC, 128).T)

    shared = {}
    shared["ident"] = np.eye(128, dtype=np.float32)
    c0 = host_consts(0)
    shared["tri"] = c0["tri"]
    shared["wmask"] = c0["wmask"]
    shared["ropec"] = c0["rope_ck"].astype(bf16)
    shared["ropes"] = c0["rope_sk"].astype(bf16)
    shared["pmb"] = c0["rope_pm"].astype(bf16)
    for w, k in (("aWq", "L0_attn_a_Wq"), ("aWk", "L0_attn_a_Wk"),
                 ("aWv", "L0_attn_a_Wv"), ("aWc", "L0_attn_a_Wc"),
                 ("fWq", "L0_ffn_wq"), ("fWk", "L0_ffn_wk"),
                 ("fWv", "L0_ffn_wv"), ("fWo", "L0_ffn_wo"),
                 ("fW1", "L0_ffn_w1"), ("fW2", "L0_ffn_w2"),
                 ("fWout", "L0_ffn_wout"), ("mWup", "L0_attn_b_W_up"),
                 ("mWdelta", "L0_attn_b_W_delta"), ("mWB", "L0_attn_b_W_B"),
                 ("mWC", "L0_attn_b_W_C"), ("mWdown", "L0_attn_b_W_down"),
                 ("mAlog", "L0_attn_b_A_log"), ("mconvw", "L0_attn_b_conv_w")):
        shared[w] = g(k)
    shared["fscale1"] = ppvec(g("L0_ffn_scale1"))
    shared["fscale2"] = ppvec(g("L0_ffn_scale2"))
    rw = g("L0_ffn_router_w")
    shared["frw"] = np.ascontiguousarray(rw.reshape(DC, 128, 8).transpose(1, 0, 2))
    for w, k in (("fb1", "L0_ffn_b1"), ("fb2", "L0_ffn_b2")):
        b = g(k)
        shared[w] = np.ascontiguousarray(b.reshape(8, DC, 128).transpose(0, 2, 1))
    bout = g("L0_ffn_bout")
    boh = np.zeros((8, 8192), dtype=np.float32)
    for e in range(8):
        boh[e, e * 1024:(e + 1) * 1024] = bout[e]
    shared["fboutoh"] = boh
    oh8 = np.zeros((8, 1024), dtype=np.float32)
    for e in range(8):
        oh8[e, e * 128:(e + 1) * 128] = 1.0
    shared["oh8"] = oh8
    moh = np.zeros((64, 8192), dtype=np.float32)
    for n in range(64):
        moh[n, n * 128:(n + 1) * 128] = 1.0
    shared["moh"] = moh.astype(bf16)
    shared["mbup"] = np.ascontiguousarray(
        g("L0_attn_b_b_up").reshape(2, 128).T)
    shared["mconvb"] = g("L0_attn_b_conv_b").reshape(128, 1)
    shared["mbdelta"] = g("L0_attn_b_b_delta").reshape(128, 1)
    shared["mbB"] = g("L0_attn_b_b_B").reshape(64, 1)
    shared["mbC"] = g("L0_attn_b_b_C").reshape(64, 1)
    shared["mdskip"] = g("L0_attn_b_D_skip").reshape(128, 1)
    shared["mbdown"] = ppvec(g("L0_attn_b_b_down"))

    X = np.asarray(inputs["L0_attn_a_X"], dtype=np.float32)
    noise = _driver_gumbel((8, L, 8)) * np.float32(0.05)
    maps = []
    for core in range(n_cores):
        m = dict(shared)
        m["xin"] = np.ascontiguousarray(X[core])
        m["fnoise"] = np.ascontiguousarray(
            noise[core].reshape(32, 128, 8).transpose(1, 0, 2))
        maps.append(m)
    return maps


_CACHE = {}


def kernel(**inputs):
    if "nc" not in _CACHE:
        nc, mk = build_program(4)
        _CACHE["nc"] = nc
    nc = _CACHE["nc"]
    in_maps = make_in_maps(inputs, 8)
    res = run_bass_kernel_spmd(nc, in_maps, core_ids=list(range(8)))
    out = np.stack([res.results[c]["out"] for c in range(8)], axis=0)
    return out


# version 04
# speedup vs baseline: 1.0398x; 1.0398x over previous
from contextlib import ExitStack

import numpy as np
import concourse.bass as bass
import concourse.mybir as mybir
from concourse.bass_utils import run_bass_kernel_spmd

F32, BF16 = mybir.dt.float32, mybir.dt.bfloat16
AF = mybir.ActivationFunctionType
ALU = mybir.AluOpType
AX = mybir.AxisListType.X

D, L = 1024, 4096
DC = D // 128
NT = L // 512
NTT = L // 128
LN_EPS = 1e-5
RMS_EPS = 1e-6
NHP = 8
EXPERTS = 8

ENGS = ("sync", "tensor", "vector", "scalar", "gpsimd")
DMA_RING = {"sync": 10, "gpsimd": 8, "scalar": 4, "vector": 4, "tensor": 4}


class Op:
    __slots__ = ("eng", "idx", "fn", "is_dma", "marked", "inc_id", "dma_serial",
                 "slot", "cycle", "deps", "tag")

    def __init__(self, eng, idx, fn, is_dma, tag):
        self.eng, self.idx, self.fn, self.is_dma = eng, idx, fn, is_dma
        self.marked = False
        self.inc_id = None
        self.dma_serial = None
        self.slot = None
        self.cycle = None
        self.deps = []
        self.tag = tag


class Sched:
    def __init__(self, nc):
        self.nc = nc
        self.streams = {e: [] for e in ENGS}
        self.bufs = {}
        self.pending = {e: [] for e in ENGS}
        self.dma_count = {e: 0 for e in ENGS}

    def op(self, eng, fn, reads=(), writes=(), dma=False, tag=""):
        o = Op(eng, len(self.streams[eng]), fn, dma, tag)
        if dma:
            o.dma_serial = self.dma_count[eng]
            self.dma_count[eng] += 1
            r = DMA_RING[eng]
            o.slot = o.dma_serial % r
            o.cycle = o.dma_serial // r
        deps = {}

        def add_dep(p):
            if p is o:
                return
            if p.eng == eng and not p.is_dma and not dma:
                return
            if p.is_dma:
                key = (p.eng, "dma", p.slot)
            else:
                key = (p.eng, "eng")
            cur = deps.get(key)
            if cur is None or p.idx > cur.idx:
                deps[key] = p

        for k in reads:
            st = self.bufs.get(k)
            if st:
                for w in st[0]:
                    add_dep(w)
                st[1].append(o)
        for k in writes:
            st = self.bufs.get(k)
            if st is None:
                st = ([], [])
                self.bufs[k] = st
            for w in st[0]:
                add_dep(w)
            for r in st[1]:
                add_dep(r)
            if st[1]:
                st[0].clear()
                st[1].clear()
            st[0].append(o)
        for p in self.pending[eng]:
            add_dep(p)
        self.pending[eng] = []
        o.deps = list(deps.values())
        for p in o.deps:
            if not p.is_dma:
                p.marked = True
        self.streams[eng].append(o)
        return o

    def dma(self, eng, out, in_, reads=(), writes=(), tag=""):
        return self.op(eng, lambda e: e.dma_start(out=out, in_=in_),
                       reads=reads, writes=writes, dma=True, tag=tag)

    def barrier(self):
        lasts = []
        for e in ENGS:
            if self.streams[e]:
                lasts.append(self.streams[e][-1])
        for e in ENGS:
            self.pending[e] = [p for p in lasts
                               if not (p.eng == e and not p.is_dma)]
            for p in self.pending[e]:
                if not p.is_dma:
                    p.marked = True

    def emit(self, stack):
        nc = self.nc
        esem = {}
        dsem = {}
        for e in ENGS:
            ops = self.streams[e]
            cnt = 0
            for o in ops:
                if o.marked and not o.is_dma:
                    cnt += 1
                    o.inc_id = cnt
            if cnt:
                esem[e] = stack.enter_context(nc.semaphore(f"es_{e}"))
            if self.dma_count[e]:
                nslots = min(DMA_RING[e], self.dma_count[e])
                dsem[e] = [stack.enter_context(nc.semaphore(f"ds_{e}{i}"))
                           for i in range(nslots)]

        def emit_stream(eng_obj, e):
            water = {}
            for o in self.streams[e]:
                for p in o.deps:
                    if p.is_dma:
                        sem = dsem[p.eng][p.slot]
                        val = 16 * (p.cycle + 1)
                        wkey = (p.eng, "dma", p.slot)
                    else:
                        sem, val = esem[p.eng], p.inc_id
                        wkey = (p.eng, "eng")
                    if water.get(wkey, 0) < val:
                        water[wkey] = val
                        eng_obj.wait_ge(sem, val)
                if o.is_dma and o.cycle > 0:
                    sem = dsem[e][o.slot]
                    val = 16 * o.cycle
                    wkey = (e, "dma", o.slot)
                    if water.get(wkey, 0) < val:
                        water[wkey] = val
                        eng_obj.wait_ge(sem, val)
                ins = o.fn(eng_obj)
                if o.is_dma:
                    ins.then_inc(dsem[e][o.slot], 16)
                elif o.marked:
                    ins.then_inc(esem[e], 1)

        with nc.Block() as block:
            block.sync(lambda e: emit_stream(e, "sync"))
            block.tensor(lambda e: emit_stream(e, "tensor"))
            block.vector(lambda e: emit_stream(e, "vector"))
            block.scalar(lambda e: emit_stream(e, "scalar"))
            block.gpsimd(lambda e: emit_stream(e, "gpsimd"))

    def stats(self):
        out = {}
        for e in ENGS:
            ops = self.streams[e]
            out[e] = dict(n=len(ops), marked=sum(1 for o in ops if o.marked),
                          deps=sum(len(o.deps) for o in ops))
        return out



def _threefry2x32_np(k1, k2, x0, x1):
    def rotl(x, r):
        return ((x << np.uint32(r)) | (x >> np.uint32(32 - r))).astype(np.uint32)
    rotations = [(13, 15, 26, 6), (17, 29, 16, 24)]
    ks0 = np.uint32(k1)
    ks1 = np.uint32(k2)
    ks2 = np.uint32(np.uint32(0x1BD11BDA) ^ ks0 ^ ks1)
    x0 = (x0 + ks0).astype(np.uint32)
    x1 = (x1 + ks1).astype(np.uint32)
    ks = [ks0, ks1, ks2]
    for i in range(5):
        for r in rotations[i % 2]:
            x0 = (x0 + x1).astype(np.uint32)
            x1 = rotl(x1, r)
            x1 = (x1 ^ x0).astype(np.uint32)
        x0 = (x0 + ks[(i + 1) % 3]).astype(np.uint32)
        x1 = (x1 + ks[(i + 2) % 3] + np.uint32(i + 1)).astype(np.uint32)
    return x0, x1


def _driver_gumbel(shape):
    size = int(np.prod(shape))
    idx = np.arange(size, dtype=np.uint64)
    c1 = (idx >> np.uint64(32)).astype(np.uint32)
    c2 = (idx & np.uint64(0xFFFFFFFF)).astype(np.uint32)
    b1, b2 = _threefry2x32_np(0, 42, c1, c2)
    bits = (b1 ^ b2).astype(np.uint32)
    fl = ((bits >> np.uint32(9)) | np.uint32(0x3F800000)).view(np.float32) \
        - np.float32(1.0)
    tiny = np.float32(np.finfo(np.float32).tiny)
    u = np.maximum(tiny, (fl * (np.float32(1.0) - tiny) + tiny)).astype(np.float32)
    return (-np.log(-np.log(u))).astype(np.float32).reshape(shape)


def host_consts(core_id):
    c = {}
    c["ident"] = np.eye(128, dtype=np.float32)
    half = 32
    inv = 1.0 / 10000.0 ** (np.arange(half, dtype=np.float64) / half)
    jmod = np.arange(128) % half
    ang = np.arange(L, dtype=np.float64)[None, :] * inv[jmod][:, None]
    cos = np.cos(ang).astype(np.float32)
    sin = np.sin(ang).astype(np.float32)
    c["rope_cq"] = (cos / 8.0).astype(np.float32)
    c["rope_sq"] = (sin / 8.0).astype(np.float32)
    c["rope_ck"] = cos
    c["rope_sk"] = sin
    pm = np.zeros((128, 128), dtype=np.float32)
    for m in range(128):
        if m % 64 < 32:
            pm[m + 32, m] = -1.0
        else:
            pm[m - 32, m] = 1.0
    c["rope_pm"] = pm
    c["tri"] = np.where(np.arange(128)[None, :] <= np.arange(128)[:, None],
                        0.0, -1e9).astype(np.float32)
    j = np.arange(896)[None, :]
    k = np.arange(128)[:, None]
    c["wmask"] = np.where(j >= k + 384, 0.0, -1e9).astype(np.float32)
    noise = _driver_gumbel((8, L, 8)) * np.float32(0.05)
    c["noise"] = np.ascontiguousarray(noise[core_id])
    return c


class MK:

    def __init__(self, upto=99, dbg=False):
        self.dbg = dbg
        self.upto = upto
        nc = bass.Bass("TRN2", target_bir_lowering=False)
        self.nc = nc
        self.S = Sched(nc)
        self.stack = ExitStack()
        self.inp = {}

        def ein(name, shape, dtype=F32):
            t = nc.dram_tensor(name, shape, dtype, kind="ExternalInput").ap()
            self.inp[name] = t
            return t

        self.ein = ein
        self.xin = ein("xin", [L, D])
        self.out = nc.dram_tensor("out", [L, D], F32, kind="ExternalOutput").ap()
        self.c_ident = ein("ident", [128, 128])
        self.c_tri = ein("tri", [128, 128])
        for w in ("aWq", "aWk", "aWv", "aWc"):
            ein(w, [D, D])
        for w in ("fWq", "fWk", "fWv", "fWo"):
            ein(w, [D, D])
        for w in ("fW1", "fW2", "fWout"):
            ein(w, [8, D, D])
        ein("fscale1", [128, 8])
        ein("fscale2", [128, 8])
        ein("frw", [128, 8, 8])
        ein("fnoise", [128, 32, 8])
        ein("fb1", [8, 128, 8])
        ein("fb2", [8, 128, 8])
        ein("fboutall", [8, 1024])
        ein("ropec", [128, L], BF16)
        ein("ropes", [128, L], BF16)
        ein("pmb", [128, 128], BF16)
        ein("wmask", [128, 896])
        ein("oh8", [8, 1024])
        ein("mWup", [D, 256])
        ein("mWdelta", [128, 128])
        ein("mWB", [128, 64])
        ein("mWC", [128, 64])
        ein("mWdown", [128, D])
        ein("mbup", [128, 2])
        ein("mconvw", [128, 3])
        ein("mconvb", [128, 1])
        ein("mbdelta", [128, 1])
        ein("mbB", [64, 1])
        ein("mbC", [64, 1])
        ein("mAlog", [128, 64])
        ein("mdskip", [128, 1])
        ein("mbdown", [128, 8])
        ein("moh", [64, 8192], BF16)
        self.xres = nc.dram_tensor("xres", [D, L], F32, kind="Internal").ap()
        self.zst = nc.dram_tensor("zst", [D, L], F32, kind="Internal").ap()
        self.ao = nc.dram_tensor("ao", [D, L], BF16, kind="Internal").ap()
        self.xmoe = nc.dram_tensor("xmoe", [D, L], BF16, kind="Internal").ap()
        self.wcache = nc.dram_tensor("wcache", [24, 128, 8 * 1024], BF16,
                                     kind="Internal").ap()
        self.khbm = nc.dram_tensor("khbm", [16, 64, L], BF16, kind="Internal").ap()
        self.qhbm = nc.dram_tensor("qhbm", [16, 64, L], BF16, kind="Internal").ap()
        self.vkhbm = nc.dram_tensor("vkhbm", [16, 128, 32 * 65], BF16,
                                    kind="Internal").ap()

        sbt = nc.alloc_sbuf_tensor
        self.cpool = sbt("s_cpool", [128, 16], F32)
        self._cvals = {}
        self.ident = sbt("s_ident", [128, 128], F32)
        self.tri = sbt("s_tri", [128, 128], F32)
        self.ones_p = sbt("s_ones_p", [128, 1], F32)
        self.ones_r = sbt("s_ones_r", [1, 128], F32)
        self.pb = [nc.alloc_psum_tensor(f"pb{i}", [128, 512], F32)
                   for i in range(8)]
        self._pb_rr = 0
        self.fscale1 = sbt("s_fsc1", [128, 8], F32)
        self.fscale2 = sbt("s_fsc2", [128, 8], F32)
        self.ones64 = self.ones_r[:, 0:64]

    def bank(self, i=None):
        if i is None:
            i = self._pb_rr % 8
            self._pb_rr += 1
        return self.pb[i], f"pb{i}"

    def scope(self):
        return ExitStack()

    def sal(self, sc, name, shape, dtype):
        return sc.enter_context(self.nc.sbuf_tensor(name, shape, dtype))

    def cbias(self, v):
        v = float(v)
        if getattr(self, "_cpool_done", False):
            assert v in self._cvals, f"const {v} not pre-seeded"
        i = self._cvals.setdefault(v, len(self._cvals))
        assert i < 16
        return self.cpool[:, i:i + 1]

    def cbias1(self, v):
        return self.cbias(v)[0:1, :]

    def emit_cpool(self):
        S = self.S
        for v in (1e-6, LN_EPS, RMS_EPS, 1.0, 0.5):
            self.cbias(v)
        for v, i in sorted(self._cvals.items(), key=lambda kv: kv[1]):
            S.op("vector",
                 (lambda v=v, i=i: lambda e: e.memset(self.cpool[:, i:i + 1], v))(),
                 writes=["cpool"], tag="cpool")
        self._cpool_done = True

    def load_consts(self):
        S = self.S
        S.dma("sync", self.ident[:], self.c_ident, writes=["ident"], tag="c")
        S.dma("sync", self.tri[:], self.c_tri, writes=["tri"], tag="c")
        S.op("vector", lambda e: e.memset(self.ones_p[:], 1.0), writes=["ones_p"])
        S.op("vector", lambda e: e.memset(self.ones_r[:], 1.0), writes=["ones_r"])
        S.dma("sync", self.fscale1[:], self.inp["fscale1"], writes=["fsc1"], tag="c")
        S.dma("sync", self.fscale2[:], self.inp["fscale2"], writes=["fsc2"], tag="c")

    def p0_transpose_in(self):
        S = self.S
        sc0 = self.scope()
        tin = self.sal(sc0, "t0_tin", [128, 4, 1024], F32)
        tst = self.sal(sc0, "t0_tst", [128, 8, 512], F32)
        for g in range(8):
            for j in range(4):
                ti = g * 4 + j
                S.dma("sync", tin[:, j, :], self.xin[ti * 128:(ti + 1) * 128, :],
                      writes=[("tin", j)], tag="p0")
            for c in range(DC):
                pb, pk = self.bank()
                for j in range(4):
                    S.op("tensor",
                         (lambda pb=pb, j=j, c=c:
                          lambda e: e.transpose(pb[:, j * 128:(j + 1) * 128],
                                                tin[:, j, c * 128:(c + 1) * 128],
                                                self.ident[:]))(),
                         reads=[("tin", j), "ident"], writes=[pk], tag="p0t")
                S.op("scalar",
                     (lambda pb=pb, c=c: lambda e: e.copy(tst[:, c, :], pb[:]))(),
                     reads=[pk], writes=[("tst", c)], tag="p0e")
            for c in range(DC):
                S.dma("gpsimd",
                      self.xres[c * 128:(c + 1) * 128, g * 512:(g + 1) * 512],
                      tst[:, c, :], reads=[("tst", c)], writes=[("xres", c, g)],
                      tag="p0s")
        sc0.close()
        S.barrier()

    def set_cast(self, sc, pref):
        self._cast_stg = [self.sal(sc, pref + f"cstg{b}", [128, 1024], F32)
                          for b in range(2)]
        self._cast_i = 0

    def cast_weight_to(self, wname, dst, dst_key):
        S = self.S
        src = self.inp[wname]
        dout = src.shape[1]
        rs = src.rearrange("(c p) m -> p c m", p=128)
        for c in range(DC):
            b = self._cast_i % 2
            self._cast_i += 1
            stg = self._cast_stg[b]
            S.dma("sync", stg[:, 0:dout], rs[:, c, :],
                  reads=[("cstg", b)], writes=[("cstg", b)], tag="cw")
            S.op("scalar",
                 (lambda b=b, c=c, stg=stg, dout=dout:
                  lambda e: e.copy(dst[:, c, 0:dout], stg[:, 0:dout]))(),
                 reads=[("cstg", b)], writes=[(dst_key, c)], tag="cw")

    def ln_strip(self, xs, xname, zs, zname, sq, rms_scale=None):
        S = self.S
        sv = self.ln_sv
        pb1, pk1 = self.bank()
        pb2, pk2 = self.bank()
        xk = [(xname, c) for c in range(DC)]
        for c in range(DC):
            S.op("tensor",
                 (lambda pb1=pb1, c=c: lambda e: e.matmul(
                     pb1[0:1, :], self.ones_p[:], xs[:, c, :],
                     start=(c == 0), stop=(c == DC - 1)))(),
                 reads=[xk[c], "ones_p"], writes=[pk1], tag="lnmm")
        for c in range(DC):
            S.op("scalar",
                 (lambda c=c: lambda e: e.activation(sq[:], xs[:, c, :],
                                                     AF.Square))(),
                 reads=[xk[c]], writes=["ln_sq"], tag="lnsq")
            S.op("tensor",
                 (lambda pb2=pb2, c=c: lambda e: e.matmul(
                     pb2[0:1, :], self.ones_p[:], sq[:],
                     start=(c == 0), stop=(c == DC - 1)))(),
                 reads=["ln_sq", "ones_p"], writes=[pk2], tag="lnmm2")
        S.op("vector", lambda e: e.tensor_scalar(sv[0:1, 0:512], pb1[0:1, :],
                                                 1.0 / D, None, ALU.mult),
             reads=[pk1], writes=["sv"], tag="lnst")
        S.op("vector", lambda e: e.tensor_scalar(sv[0:1, 512:1024], pb2[0:1, :],
                                                 1.0 / D, None, ALU.mult),
             reads=[pk2], writes=["sv"], tag="lnst")
        S.op("vector", lambda e: e.tensor_mul(sv[0:1, 1536:2048], sv[0:1, 0:512], sv[0:1, 0:512]),
             reads=["sv"], writes=["sv"], tag="lnst")
        S.op("vector", lambda e: e.tensor_sub(sv[0:1, 512:1024], sv[0:1, 512:1024], sv[0:1, 1536:2048]),
             reads=["sv"], writes=["sv"], tag="lnst")
        S.op("scalar", lambda e: e.activation(sv[0:1, 1024:1536], sv[0:1, 512:1024],
                                              AF.Sqrt, bias=self.cbias1(LN_EPS)),
             reads=["sv", "cpool"], writes=["sv"], tag="lnst")
        S.op("vector", lambda e: e.reciprocal(sv[0:1, 1024:1536], sv[0:1, 1024:1536]),
             reads=["sv"], writes=["sv"], tag="lnst")
        S.op("vector", lambda e: e.tensor_mul(sv[0:1, 1536:2048], sv[0:1, 0:512], sv[0:1, 1024:1536]),
             reads=["sv"], writes=["sv"], tag="lnst")
        pb3, pk3 = self.bank()
        pb4, pk4 = self.bank()
        S.op("tensor", lambda e: e.matmul(pb3[:], self.ones_r[:], sv[0:1, 1024:1536],
                                          start=True, stop=True),
             reads=["sv", "ones_r"], writes=[pk3], tag="lnbc")
        S.op("tensor", lambda e: e.matmul(pb4[:], self.ones_r[:], sv[0:1, 1536:2048],
                                          start=True, stop=True),
             reads=["sv", "ones_r"], writes=[pk4], tag="lnbc")
        zk = [(zname, c) for c in range(DC)]
        for c in range(DC):
            S.op("vector",
                 (lambda c=c, pb3=pb3: lambda e: e.tensor_mul(
                     self.ln_nsc[:], xs[:, c, :], pb3[:]))(),
                 reads=[xk[c], pk3], writes=["ln_nsc"], tag="lnn1")
            S.op("vector",
                 (lambda c=c, pb4=pb4: lambda e: e.tensor_sub(
                     zs[:, c, :], self.ln_nsc[:], pb4[:]))(),
                 reads=["ln_nsc", pk4], writes=[zk[c]], tag="lnn2")

    def proj(self, w, wkey, zs, zname, dst, dname, evac="copy"):
        S = self.S
        for m in range(DC):
            pb, pk = self.bank()
            for k in range(DC):
                S.op("tensor",
                     (lambda pb=pb, m=m, k=k, first=(k == 0), last=(k == DC - 1):
                      lambda e: e.matmul(pb[:], w[:, k, m * 128:(m + 1) * 128],
                                         zs[:, k, :], start=first, stop=last))(),
                     reads=[(wkey, k), (zname, k)], writes=[pk], tag="pj")
            if evac == "copy":
                fn = (lambda pb=pb, m=m: lambda e: e.copy(dst[:, m, :], pb[:]))()
            elif evac == "exp":
                fn = (lambda pb=pb, m=m:
                      lambda e: e.activation(dst[:, m, :], pb[:], AF.Exp))()
            elif evac == "sigmoid":
                fn = (lambda pb=pb, m=m:
                      lambda e: e.activation(dst[:, m, :], pb[:], AF.Sigmoid))()
            S.op("scalar", fn, reads=[pk], writes=[(dname, m)], tag="pjev")

    def p1_attn_a(self):
        nc, S = self.nc, self.S
        sc1 = self.scope()
        wq = self.sal(sc1, "a_wq", [128, 8, 1024], BF16)
        wk = self.sal(sc1, "a_wk", [128, 8, 1024], BF16)
        wv = self.sal(sc1, "a_wv", [128, 8, 1024], BF16)
        wc = self.sal(sc1, "a_wc", [128, 8, 1024], BF16)
        self._mc_stg = [self.sal(sc1, f"mcs{b}", [128, 4, 1024], F32)
                        for b in range(2)]
        self._mc_cb = [self.sal(sc1, f"mcb{b}", [128, 4, 1024], BF16)
                       for b in range(2)]
        self._mc_i = 0
        self.set_cast(sc1, "p1")
        self.cast_weight_to("aWv", wv, "a_wv")
        self.cast_weight_to("aWk", wk, "a_wk")
        self.cast_weight_to("aWq", wq, "a_wq")
        self.cast_weight_to("aWc", wc, "a_wc")

        xs = self.sal(sc1, "a_xs", [128, 8, 512], F32)
        zs = self.sal(sc1, "a_zs", [128, 8, 512], BF16)
        b1 = self.sal(sc1, "a_b1", [128, 8, 512], F32)
        b2 = self.sal(sc1, "a_b2", [128, 8, 512], F32)
        carK = self.sal(sc1, "a_carK", [128, 8], F32)
        carKV = self.sal(sc1, "a_carKV", [128, 8], F32)
        S.op("vector", lambda e: e.memset(carK[:], 0.0), writes=["carK"])
        S.op("vector", lambda e: e.memset(carKV[:], 0.0), writes=["carKV"])

        for s in range(NT):
            if self.upto >= 2:
                for j in range(3):
                    mi = s * 3 + j
                    self.cache_moe_matrix(mi, self.inp[("fW1", "fW2", "fWout")[mi % 3]][mi // 3])
            t0 = s * 512
            for c in range(DC):
                S.dma("sync", xs[:, c, :],
                      self.xres[c * 128:(c + 1) * 128, t0:t0 + 512],
                      reads=[("xres", c, s), ("a_xs", c)], writes=[("a_xs", c)],
                      tag="p1ld")
            self.ln_strip(xs, "a_xs", zs, "a_zs", self.ln_sq)
            self.proj(wv, "a_wv", zs, "a_zs", b1, "b1", evac="copy")
            self.proj(wk, "a_wk", zs, "a_zs", b2, "b2", evac="exp")
            for c in range(DC):
                S.op("vector",
                     (lambda c=c: lambda e: e.tensor_mul(b1[:, c, :], b2[:, c, :],
                                                         b1[:, c, :]))(),
                     reads=[("b1", c), ("b2", c)], writes=[("b1", c)], tag="p1kv")
            for (buf, bk, car, ck) in ((b2, "b2", carK, "carK"),
                                       (b1, "b1", carKV, "carKV")):
                for c in range(DC):
                    S.op("vector",
                         (lambda buf=buf, car=car, c=c:
                          lambda e: e.tensor_tensor_scan(
                              buf[:, c, :], buf[:, c, :], buf[:, c, :],
                              car[:, c:c + 1], ALU.add, ALU.bypass))(),
                         reads=[(bk, c), (ck, c)], writes=[(bk, c)], tag="p1sc")
                    S.op("scalar",
                         (lambda buf=buf, car=car, c=c:
                          lambda e: e.copy(car[:, c:c + 1],
                                           buf[:, c, 511:512]))(),
                         reads=[(bk, c)], writes=[(ck, c)], tag="p1cr")
            for c in range(DC):
                S.op("scalar",
                     (lambda c=c: lambda e: e.activation(b2[:, c, :], b2[:, c, :],
                                                         AF.Ln, bias=self.cbias(1e-6)))(),
                     reads=[("b2", c), "cpool"], writes=[("b2", c)], tag="p1ln")
                S.op("scalar",
                     (lambda c=c: lambda e: e.activation(b2[:, c, :], b2[:, c, :],
                                                         AF.Exp, scale=-1.0))(),
                     reads=[("b2", c)], writes=[("b2", c)], tag="p1ex")
                S.op("vector",
                     (lambda c=c: lambda e: e.tensor_mul(b1[:, c, :], b1[:, c, :],
                                                         b2[:, c, :]))(),
                     reads=[("b1", c), ("b2", c)], writes=[("b1", c)], tag="p1s")
            self.proj(wq, "a_wq", zs, "a_zs", b2, "b2", evac="sigmoid")
            for c in range(DC):
                S.op("vector",
                     (lambda c=c: lambda e: e.tensor_mul(zs[:, c, :], b1[:, c, :],
                                                         b2[:, c, :]))(),
                     reads=[("b1", c), ("b2", c)], writes=[("a_zs", c)], tag="p1o")
            for m in range(DC):
                pb, pk = self.bank()
                for k in range(DC):
                    S.op("tensor",
                         (lambda pb=pb, m=m, k=k, first=(k == 0),
                          last=(k == DC - 1):
                          lambda e: e.matmul(pb[:],
                                             wc[:, k, m * 128:(m + 1) * 128],
                                             zs[:, k, :], start=first,
                                             stop=last))(),
                         reads=[("a_wc", k), ("a_zs", k)], writes=[pk], tag="p1mm")
                S.op("vector",
                     (lambda pb=pb, m=m: lambda e: e.tensor_add(
                         xs[:, m, :], pb[:], xs[:, m, :]))(),
                     reads=[pk, ("a_xs", m)], writes=[("a_xs", m)], tag="p1ra")
                S.dma("gpsimd",
                      self.xres[m * 128:(m + 1) * 128, t0:t0 + 512],
                      xs[:, m, :], reads=[("a_xs", m)], writes=[("xres", m, s)],
                      tag="p1st")
        sc1.close()
        S.barrier()


    def cache_moe_matrix(self, mi, wap):
        S = self.S
        rs = wap.rearrange("(c p) m -> p c m", p=128)
        wcr = self.wcache[mi].rearrange("p (c m) -> p c m", m=1024)
        for hhalf in range(2):
            b = self._mc_i % 2
            self._mc_i += 1
            stg = self._mc_stg[b]
            cb = self._mc_cb[b]
            S.dma("sync", stg[:], rs[:, hhalf * 4:hhalf * 4 + 4, :],
                  reads=[("mcs", b)], writes=[("mcs", b)], tag="mc")
            for c in range(4):
                S.op("scalar",
                     (lambda b=b, c=c, stg=stg, cb=cb:
                      lambda e: e.copy(cb[:, c, :], stg[:, c, :]))(),
                     reads=[("mcs", b)], writes=[("mcb", b)], tag="mc")
            S.dma("gpsimd", wcr[:, hhalf * 4:hhalf * 4 + 4, :],
                  cb[:], reads=[("mcb", b)], writes=[("wc", mi)], tag="mc")

    def ffn(self, t):
        nc, S = self.nc, self.S
        inp = self.inp

        scAB = self.scope()
        xat = self.sal(scAB, t + "_xat", [128, DC, L], BF16)
        wq = self.sal(scAB, t + "_wq", [128, 8, 1024], BF16)
        wk = self.sal(scAB, t + "_wk", [128, 8, 1024], BF16)
        wv = self.sal(scAB, t + "_wv", [128, 8, 1024], BF16)

        scA = self.scope()
        fxs = self.sal(scA, t + "_fxs", [128, 8, 512], F32)
        fzs = self.sal(scA, t + "_fzs", [128, 8, 512], F32)
        self.set_cast(scA, t + "A")
        self.cast_weight_to("fWq", wq, t + "wq")
        self.cast_weight_to("fWk", wk, t + "wk")
        self.cast_weight_to("fWv", wv, t + "wv")
        for s in range(NT):
            t0 = s * 512
            for c in range(DC):
                S.dma("sync", fxs[:, c, :],
                      self.xres[c * 128:(c + 1) * 128, t0:t0 + 512],
                      reads=[("xres", c, s), (t + "fxs", c)],
                      writes=[(t + "fxs", c)], tag=t + "Ald")
            self.ln_strip(fxs, t + "fxs", fzs, t + "fzs", self.ln_sq)
            for c in range(DC):
                S.dma("gpsimd", self.zst[c * 128:(c + 1) * 128, t0:t0 + 512],
                      fzs[:, c, :], reads=[(t + "fzs", c)],
                      writes=[("zst", c, s)], tag=t + "Ast")
            pbS, pkS = self.bank()
            for c in range(DC):
                S.op("scalar",
                     (lambda c=c: lambda e: e.activation(self.ln_sq[:],
                                                         fzs[:, c, :],
                                                         AF.Square))(),
                     reads=[(t + "fzs", c)], writes=["ln_sq"], tag=t + "Asq")
                S.op("tensor",
                     (lambda pbS=pbS, c=c: lambda e: e.matmul(
                         pbS[0:1, :], self.ones_p[:], self.ln_sq[:],
                         start=(c == 0), stop=(c == DC - 1)))(),
                     reads=["ln_sq", "ones_p"], writes=[pkS], tag=t + "Amm")
            sv = self.ln_sv
            S.op("vector", (lambda pbS=pbS: lambda e: e.tensor_scalar(
                sv[0:1, 512:1024], pbS[0:1, :], 1.0 / D, None, ALU.mult))(),
                 reads=[pkS], writes=["sv"], tag=t + "Ast2")
            S.op("scalar", lambda e: e.activation(sv[0:1, 1024:1536],
                                                  sv[0:1, 512:1024], AF.Sqrt,
                                                  bias=self.cbias1(RMS_EPS)),
                 reads=["sv", "cpool"], writes=["sv"], tag=t + "Ast2")
            S.op("vector", lambda e: e.reciprocal(sv[0:1, 1024:1536],
                                                  sv[0:1, 1024:1536]),
                 reads=["sv"], writes=["sv"], tag=t + "Ast2")
            pbB, pkB = self.bank()
            S.op("tensor", (lambda pbB=pbB: lambda e: e.matmul(
                pbB[:], self.ones_r[:], sv[0:1, 1024:1536],
                start=True, stop=True))(),
                 reads=["sv", "ones_r"], writes=[pkB], tag=t + "Abc")
            for c in range(DC):
                S.op("vector",
                     (lambda c=c, pbB=pbB, t0=t0: lambda e: e.scalar_tensor_tensor(
                         xat[:, c, t0:t0 + 512], fzs[:, c, :],
                         self.fscale1[:, c:c + 1], pbB[:],
                         ALU.mult, ALU.mult))(),
                     reads=[(t + "fzs", c), pkB, "fsc1"],
                     writes=[(t + "xat", c, s)], tag=t + "Axm")
        scA.close()
        S.barrier()
        if self.dbg and t == "F1":
            dxat = self.nc.dram_tensor("dxat", [128, DC, L], BF16,
                                       kind="ExternalOutput").ap()
            S.dma("sync", dxat, xat[:],
                  reads=[(t + "xat", c, s2) for c in range(DC)
                         for s2 in range(NT)],
                  writes=["dbgxat"], tag="dbg")
            S.barrier()
        scB = self.scope()
        qb = self.sal(scB, t + "_qb", [128, 8, 512], BF16)
        khp = self.sal(scB, t + "_khp", [64, 2 * L], BF16)
        qhp = self.sal(scB, t + "_qhp", [64, 2 * L], BF16)
        vk0 = self.sal(scB, t + "_vk0", [128, 32, 65], BF16)
        vk1 = self.sal(scB, t + "_vk1", [128, 32, 65], BF16)
        s1 = self.sal(scB, t + "_s1", [64, 512], F32)
        s2 = self.sal(scB, t + "_s2", [64, 512], F32)
        ctab = self.sal(scB, t + "_ctab", [128, L], BF16)
        stab = self.sal(scB, t + "_stab", [128, L], BF16)
        pmt = self.sal(scB, t + "_pmt", [128, 128], BF16)
        S.dma("sync", ctab[:], inp["ropec"], writes=["ctab"], tag=t + "B")
        S.dma("sync", stab[:], inp["ropes"], writes=["stab"], tag=t + "B")
        S.dma("sync", pmt[:], inp["pmb"], writes=["pmt"], tag=t + "B")

        for p in range(NHP):
            for (wmat, wkey, dsttab, dkey, hbm) in (
                    (wq, t + "wq", qhp, "qhp", self.qhbm),
                    (wk, t + "wk", khp, "khp", self.khbm)):
                for n in range(8):
                    pbQ, pkQ = self.bank()
                    for k in range(DC):
                        S.op("tensor",
                             (lambda pbQ=pbQ, wmat=wmat, k=k, p=p, n=n,
                              first=(k == 0), last=(k == DC - 1):
                              lambda e: e.matmul(
                                  pbQ[:], wmat[:, k, p * 128:(p + 1) * 128],
                                  xat[:, k, n * 512:(n + 1) * 512],
                                  start=first, stop=last))(),
                             reads=[(wkey, k), (t + "xat", k, n)],
                             writes=[pkQ], tag=t + "Bqk")
                    S.op("scalar",
                         (lambda pbQ=pbQ, n=n: lambda e: e.copy(qb[:, n, :],
                                                                pbQ[:]))(),
                         reads=[pkQ], writes=[("qb", n)], tag=t + "Bqe")
                for h01 in range(2):
                    off = h01 * L
                    rs = slice(h01 * 64, h01 * 64 + 64)
                    for n in range(8):
                        pbP, pkP = self.bank()
                        S.op("tensor",
                             (lambda pbP=pbP, rs=rs, n=n: lambda e: e.matmul(
                                 pbP[0:64, :], pmt[rs, rs], qb[rs, n, :],
                                 start=True, stop=True))(),
                             reads=[("qb", n), "pmt"], writes=[pkP],
                             tag=t + "Brp")
                        S.op("vector",
                             (lambda rs=rs, n=n: lambda e: e.tensor_mul(
                                 s1[:], qb[rs, n, :],
                                 ctab[rs, n * 512:(n + 1) * 512]))(),
                             reads=[("qb", n), "ctab"], writes=["s1"],
                             tag=t + "Brt")
                        S.op("vector",
                             (lambda pbP=pbP, n=n: lambda e: e.tensor_mul(
                                 s2[:], pbP[0:64, :],
                                 stab[0:64, n * 512:(n + 1) * 512]))(),
                             reads=[pkP, "stab"], writes=["s2"], tag=t + "Brs")
                        S.op("vector",
                             (lambda dsttab=dsttab, off=off, n=n:
                              lambda e: e.tensor_add(
                                  dsttab[0:64, off + n * 512:off + (n + 1) * 512],
                                  s1[:], s2[:]))(),
                             reads=["s1", "s2"],
                             writes=[(dkey, h01, n)], tag=t + "Brr")
                    h = 2 * p + h01
                    S.dma("sync", hbm[h], dsttab[0:64, off:off + L],
                          reads=[(dkey, h01, n2) for n2 in range(8)],
                          writes=[("qh" if dkey == "qhp" else "kh", h)],
                          tag=t + "Bqs")
            for (h01, vkb) in ((0, vk0), (1, vk1)):
                S.op("vector",
                     (lambda vkb=vkb: lambda e: e.memset(vkb[:, :, 64:65], 1.0))(),
                     writes=[("vkm", h01)], tag=t + "Bvm")
            for ti in range(NTT):
                pbV, pkV = self.bank()
                for k in range(DC):
                    S.op("tensor",
                         (lambda pbV=pbV, k=k, ti=ti, p=p, first=(k == 0),
                          last=(k == DC - 1): lambda e: e.matmul(
                              pbV[:, 0:128], xat[:, k, ti * 128:(ti + 1) * 128],
                              wv[:, k, p * 128:(p + 1) * 128],
                              start=first, stop=last))(),
                         reads=[(t + "wv", k), (t + "xat", k, ti // 4)],
                         writes=[pkV], tag=t + "Bv")
                for (h01, vkb) in ((0, vk0), (1, vk1)):
                    S.op("scalar",
                         (lambda pbV=pbV, ti=ti, h01=h01, vkb=vkb:
                          lambda e: e.copy(vkb[:, ti, 0:64],
                                           pbV[:, h01 * 64:h01 * 64 + 64]))(),
                         reads=[pkV], writes=[("vkm", h01)], tag=t + "Bve")
            for (h01, vkb) in ((0, vk0), (1, vk1)):
                h = 2 * p + h01
                S.dma("sync", self.vkhbm[h],
                      vkb[:].rearrange("p a b -> p (a b)"),
                      reads=[("vkm", h01)], writes=[("vh", h)], tag=t + "Bvs")
        scB.close()
        scAB.close()
        S.barrier()

        scB2 = self.scope()
        khs = [self.sal(scB2, t + f"_khs{i}", [65, L], BF16) for i in range(2)]
        qhs = [self.sal(scB2, t + f"_qhs{i}", [65, L], BF16) for i in range(2)]
        vks = [self.sal(scB2, t + f"_vks{i}", [128, 32, 65], BF16)
               for i in range(2)]
        aoh = [self.sal(scB2, t + f"_aoh{i}", [64, L], BF16) for i in range(2)]
        ptb = self.sal(scB2, t + "_ptb", [128, 32, 512], BF16)
        dscr = self.sal(scB2, t + "_dscr", [128, 4, 512], F32)
        rmall = self.sal(scB2, t + "_rmall", [128, 320], F32)
        rm = self.sal(scB2, t + "_rm", [128, 1], F32)
        rlb = self.sal(scB2, t + "_rlb", [64, 512], F32)
        rl1 = self.sal(scB2, t + "_rl1", [1, 512], F32)
        wmsk = self.sal(scB2, t + "_wmsk", [128, 896], F32)
        S.dma("sync", wmsk[:], inp["wmask"], writes=["wmsk"], tag=t + "B2")
        for h in range(16):
            i = h % 2
            kh, qh, vk, ao1 = khs[i], qhs[i], vks[i], aoh[i]
            S.dma("sync", kh[0:64, :], self.khbm[h],
                  reads=[("kh", h), ("khs", i)], writes=[("khs", i)],
                  tag=t + "B2l")
            S.op("vector", (lambda kh=kh: lambda e: e.memset(kh[64:65, :], 1.0))(),
                 reads=[("khs", i)], writes=[("khs1", i)], tag=t + "B2l")
            S.dma("sync", qh[0:64, :], self.qhbm[h],
                  reads=[("qh", h), ("qhs", i)], writes=[("qhs", i)],
                  tag=t + "B2l")
            S.dma("sync", vk[:].rearrange("p a b -> p (a b)"), self.vkhbm[h],
                  reads=[("vh", h), ("vks", i)], writes=[("vks", i)],
                  tag=t + "B2l")
            nbs = []
            for qi in range(NTT):
                kmax = 128 * (qi + 1)
                nfull = (kmax - 128) // 512
                rfull = (kmax - 128) % 512
                nb = 0
                for kb in range(nfull + (1 if rfull else 0)):
                    kn = 512 if kb < nfull else rfull
                    pbS, pkS = self.bank(3 + (kb % 2))
                    S.op("tensor",
                         (lambda pbS=pbS, qh=qh, kh=kh, qi=qi, kb=kb, kn=kn:
                          lambda e: e.matmul(
                              pbS[:, 0:kn],
                              qh[0:64, qi * 128:qi * 128 + 128],
                              kh[0:64, kb * 512:kb * 512 + kn],
                              start=True, stop=True))(),
                         reads=[("qhs", i), ("khs", i)],
                         writes=[pkS], tag=t + "Bp1")
                    S.op("vector",
                         (lambda pbS=pbS, qi=qi, nb=nb, kn=kn: lambda e:
                          e.reduce_max(rmall[:, qi * 10 + nb:qi * 10 + nb + 1],
                                       pbS[:, 0:kn], axis=AX))(),
                         reads=[pkS], writes=[("rmall", qi, nb)],
                         tag=t + "Bp1r")
                    nb += 1
                pbD, pkD = self.bank(5)
                S.op("tensor",
                     (lambda pbD=pbD, qh=qh, kh=kh, qi=qi, kmax=kmax:
                      lambda e: e.matmul(
                          pbD[:, 0:128],
                          qh[0:64, qi * 128:qi * 128 + 128],
                          kh[0:64, kmax - 128:kmax],
                          start=True, stop=True))(),
                     reads=[("qhs", i), ("khs", i)],
                     writes=[pkD], tag=t + "Bp1d")
                S.op("vector",
                     (lambda pbD=pbD: lambda e: e.tensor_add(
                         dscr[:, 0, 0:128], pbD[:, 0:128], self.tri[:]))(),
                     reads=[pkD, "tri"], writes=[("dscr", 0)], tag=t + "Bp1d")
                S.op("vector",
                     (lambda qi=qi, nb=nb: lambda e: e.reduce_max(
                         rmall[:, qi * 10 + nb:qi * 10 + nb + 1],
                         dscr[:, 0, 0:128], axis=AX))(),
                     reads=[("dscr", 0)], writes=[("rmall", qi, nb)],
                     tag=t + "Bp1r")
                nbs.append(nb + 1)
            for qi in range(NTT):
                nb = nbs[qi]
                S.op("vector",
                     (lambda qi=qi, nb=nb: lambda e: e.tensor_reduce(
                         rm[:], rmall[:, qi * 10:qi * 10 + nb], axis=AX,
                         op=ALU.max))(),
                     reads=[("rmall", qi, j) for j in range(nb)] + ["rm"],
                     writes=["rm"], tag=t + "Bp1c")
                pbM, pkM = self.bank(6)
                S.op("tensor",
                     (lambda pbM=pbM: lambda e: e.matmul(
                         pbM[0:1, 0:128], rm[:], self.ident[:],
                         start=True, stop=True))(),
                     reads=["rm", "ident"], writes=[pkM], tag=t + "Bp1t")
                S.op("vector",
                     (lambda pbM=pbM, qh=qh, qi=qi: lambda e:
                      e.tensor_scalar(
                          qh[64:65, qi * 128:qi * 128 + 128],
                          pbM[0:1, 0:128], -1.0, None, ALU.mult))(),
                     reads=[pkM], writes=[("qh64", i, qi // 4)],
                     tag=t + "Bp1w")
            for qJ in range(NT):
                nkb = 4 * (qJ + 1)
                pvb, pvk = self.bank(2)

                def emit_s2(kb, qJ=qJ, i=i, kh=kh, qh=qh):
                    pbA, pkA = self.bank(kb % 2)
                    S.op("tensor",
                         (lambda pbA=pbA, kh=kh, qh=qh, kb=kb, qJ=qJ:
                          lambda e: e.matmul(
                              pbA[:],
                              kh[0:65, kb * 128:kb * 128 + 128],
                              qh[0:65, qJ * 512:qJ * 512 + 512],
                              start=True, stop=True))(),
                         reads=[("khs", i), ("khs1", i), ("qhs", i),
                                ("qh64", i, qJ)],
                         writes=[pkA], tag=t + "Bs2")
                    i_rel = kb - 4 * qJ
                    sl = kb % 4
                    if i_rel >= 0:
                        S.op("vector",
                             (lambda pbA=pbA, i_rel=i_rel, sl=sl: lambda e:
                              e.tensor_add(dscr[:, sl, :],
                                           pbA[:],
                                           wmsk[:, 384 - 128 * i_rel:
                                                896 - 128 * i_rel]))(),
                             reads=[pkA, "wmsk"], writes=[("dscr", sl)],
                             tag=t + "Bs2m")
                        S.op("scalar",
                             (lambda kb=kb, sl=sl: lambda e: e.activation(
                                 ptb[:, kb, :], dscr[:, sl, :], AF.Exp,
                                 scale=0.125))(),
                             reads=[("dscr", sl)], writes=[("ptb", kb)],
                             tag=t + "Bex")
                    else:
                        S.op("scalar",
                             (lambda pbA=pbA, kb=kb: lambda e: e.activation(
                                 ptb[:, kb, :], pbA[:], AF.Exp,
                                 scale=0.125))(),
                             reads=[pkA], writes=[("ptb", kb)],
                             tag=t + "Bex")

                def emit_pv(kb, qJ=qJ, i=i, vk=vk, nkb=nkb, pvb=pvb, pvk=pvk):
                    S.op("tensor",
                         (lambda pvb=pvb, vk=vk, kb=kb, first=(kb == 0),
                          last=(kb == nkb - 1): lambda e: e.matmul(
                              pvb[0:65, :], vk[:, kb, :],
                              ptb[:, kb, :],
                              start=first, stop=last))(),
                         reads=[("vks", i), ("ptb", kb)],
                         writes=[pvk], tag=t + "Bpv")

                for kb in range(nkb):
                    emit_s2(kb)
                for kb in range(nkb):
                    emit_pv(kb)
                S.op("vector",
                     (lambda pvb=pvb: lambda e: e.reciprocal(
                         rl1[:], pvb[64:65, :]))(),
                     reads=[pvk], writes=["rl1"], tag=t + "Bnr")
                pbL, pkL = self.bank(7)
                S.op("tensor",
                     (lambda pbL=pbL: lambda e: e.matmul(
                         pbL[0:64, :], self.ones64, rl1[:],
                         start=True, stop=True))(),
                     reads=["rl1", "ones_r"], writes=[pkL], tag=t + "Bnb")
                S.op("scalar",
                     (lambda pbL=pbL: lambda e: e.copy(rlb[:], pbL[0:64, :]))(),
                     reads=[pkL], writes=["rlb"], tag=t + "Bne")
                S.op("vector",
                     (lambda pvb=pvb, ao1=ao1, qJ=qJ: lambda e: e.tensor_mul(
                         ao1[:, qJ * 512:(qJ + 1) * 512],
                         pvb[0:64, :], rlb[:]))(),
                     reads=[pvk, "rlb"], writes=[("aoh", i, qJ)],
                     tag=t + "Bno")
            S.dma("gpsimd", self.ao[h * 64:(h + 1) * 64, :], ao1[:],
                  reads=[("aoh", i, qJ2) for qJ2 in range(NT)],
                  writes=[("ao", h // 2)], tag=t + "Bst")
        scB2.close()
        S.barrier()

        scCD = self.scope()
        gfm = self.sal(scCD, t + "_gfm", [8, L], F32)

        scC = self.scope()
        wo = self.sal(scC, t + "_wo", [128, 8, 1024], BF16)
        aos = self.sal(scC, t + "_aos", [128, 8, 512], BF16)
        zsC = self.sal(scC, t + "_zsC", [128, 8, 512], F32)
        hs = self.sal(scC, t + "_hs", [128, 8, 512], F32)
        rw2 = self.sal(scC, t + "_rw2", [128, 8, 8], F32)
        nsb = self.sal(scC, t + "_nsb", [128, 32, 8], F32)
        gl = self.sal(scC, t + "_gl", [128, 4, 8], F32)
        gv = self.sal(scC, t + "_gv", [128, 4, 8], F32)
        gsc = self.sal(scC, t + "_gsc", [128, 4, 4], F32)
        rcol = self.sal(scC, t + "_rcol", [128, 4], F32)
        self.set_cast(scC, t + "C")
        self.cast_weight_to("fWo", wo, t + "wo")
        S.dma("sync", nsb[:], inp["fnoise"], writes=["nsb"], tag=t + "C")
        S.dma("sync", rw2[:], inp["frw"], writes=["rw2"], tag=t + "C")
        for c in range(DC):
            S.op("vector",
                 (lambda c=c: lambda e: e.tensor_scalar(
                     rw2[:, c, :], rw2[:, c, :], self.fscale2[:, c:c + 1],
                     None, ALU.mult))(),
                 reads=["rw2", "fsc2"], writes=["rw2"], tag=t + "C")
        for s in range(NT):
            t0 = s * 512
            for c in range(DC):
                S.dma("sync", aos[:, c, :],
                      self.ao[c * 128:(c + 1) * 128, t0:t0 + 512],
                      reads=[("ao", c), (t + "aos", c)],
                      writes=[(t + "aos", c)], tag=t + "Cld")
                S.dma("sync", zsC[:, c, :],
                      self.zst[c * 128:(c + 1) * 128, t0:t0 + 512],
                      reads=[("zst", c, s), (t + "zsC", c)],
                      writes=[(t + "zsC", c)], tag=t + "Cld")
            for m in range(DC):
                pbW, pkW = self.bank()
                for k in range(DC):
                    S.op("tensor",
                         (lambda pbW=pbW, m=m, k=k, first=(k == 0),
                          last=(k == DC - 1): lambda e: e.matmul(
                              pbW[:], wo[:, k, m * 128:(m + 1) * 128],
                              aos[:, k, :], start=first, stop=last))(),
                         reads=[(t + "wo", k), (t + "aos", k)],
                         writes=[pkW], tag=t + "Cmm")
                S.op("vector",
                     (lambda pbW=pbW, m=m: lambda e: e.tensor_add(
                         hs[:, m, :], pbW[:], zsC[:, m, :]))(),
                     reads=[pkW, (t + "zsC", m)], writes=[(t + "hs", m)],
                     tag=t + "Cha")
                S.dma("gpsimd", self.zst[m * 128:(m + 1) * 128, t0:t0 + 512],
                      hs[:, m, :], reads=[(t + "hs", m)],
                      writes=[("zst", m, s)], tag=t + "Chs")
            pbS, pkS = self.bank()
            for m in range(DC):
                S.op("scalar",
                     (lambda m=m: lambda e: e.activation(self.ln_sq[:],
                                                         hs[:, m, :],
                                                         AF.Square))(),
                     reads=[(t + "hs", m)], writes=["ln_sq"], tag=t + "Csq")
                S.op("tensor",
                     (lambda pbS=pbS, m=m: lambda e: e.matmul(
                         pbS[0:1, :], self.ones_p[:], self.ln_sq[:],
                         start=(m == 0), stop=(m == DC - 1)))(),
                     reads=["ln_sq", "ones_p"], writes=[pkS], tag=t + "Cst")
            sv = self.ln_sv
            S.op("vector", (lambda pbS=pbS: lambda e: e.tensor_scalar(
                sv[0:1, 512:1024], pbS[0:1, :], 1.0 / D, None, ALU.mult))(),
                 reads=[pkS], writes=["sv"], tag=t + "Cs2")
            S.op("scalar", lambda e: e.activation(sv[0:1, 1024:1536],
                                                  sv[0:1, 512:1024], AF.Sqrt,
                                                  bias=self.cbias1(RMS_EPS)),
                 reads=["sv", "cpool"], writes=["sv"], tag=t + "Cs2")
            S.op("vector", lambda e: e.reciprocal(sv[0:1, 1024:1536],
                                                  sv[0:1, 1024:1536]),
                 reads=["sv"], writes=["sv"], tag=t + "Cs2")
            pbB, pkB = self.bank()
            S.op("tensor", (lambda pbB=pbB: lambda e: e.matmul(
                pbB[:], self.ones_r[:], sv[0:1, 1024:1536],
                start=True, stop=True))(),
                 reads=["sv", "ones_r"], writes=[pkB], tag=t + "Cbc")
            for m in range(DC):
                S.op("vector",
                     (lambda m=m, pbB=pbB: lambda e: e.scalar_tensor_tensor(
                         aos[:, m, :], hs[:, m, :],
                         self.fscale2[:, m:m + 1], pbB[:],
                         ALU.mult, ALU.mult))(),
                     reads=[(t + "hs", m), pkB, "fsc2"],
                     writes=[(t + "aos", m)], tag=t + "Cxm")
                S.dma("gpsimd", self.xmoe[m * 128:(m + 1) * 128, t0:t0 + 512],
                      aos[:, m, :], reads=[(t + "aos", m)],
                      writes=[("xmoe", m, s)], tag=t + "Cxs")
            for tj in range(4):
                ti = s * 4 + tj
                pbR, pkR = self.bank()
                for k in range(DC):
                    S.op("tensor",
                         (lambda pbR=pbR, k=k, tj=tj, first=(k == 0),
                          last=(k == DC - 1): lambda e: e.matmul(
                              pbR[:, 0:8], hs[:, k, tj * 128:(tj + 1) * 128],
                              rw2[:, k, :], start=first, stop=last))(),
                         reads=[(t + "hs", k), "rw2"], writes=[pkR],
                         tag=t + "Crt")
                pbX, pkX = self.bank()
                S.op("tensor",
                     (lambda pbX=pbX, tj=tj: lambda e: e.matmul(
                         pbX[:, 0:1],
                         sv[0:1, 1024 + tj * 128:1024 + tj * 128 + 128],
                         self.ones_r[0:1, 0:1], start=True, stop=True))(),
                     reads=["sv", "ones_r"], writes=[pkX], tag=t + "Crc")
                S.op("scalar",
                     (lambda pbX=pbX, tj=tj: lambda e: e.copy(
                         rcol[:, tj:tj + 1], pbX[:, 0:1]))(),
                     reads=[pkX], writes=[("rcol", tj)], tag=t + "Crc")
                S.op("vector",
                     (lambda pbR=pbR, tj=tj, ti=ti: lambda e:
                      e.scalar_tensor_tensor(
                          gl[:, tj, :], pbR[:, 0:8], rcol[:, tj:tj + 1],
                          nsb[:, ti, :], ALU.mult, ALU.add))(),
                     reads=[pkR, ("rcol", tj), "nsb"], writes=[("gl", tj)],
                     tag=t + "Cg")
            def gstep(fn, rd, wr):
                for tj in range(4):
                    S.op("vector", fn(tj), reads=[(r, tj) for r in rd],
                         writes=[(w, tj) for w in wr], tag=t + "Cg")
            gstep(lambda tj: (lambda e: e.reduce_max(gsc[:, tj, 0:1],
                                                     gl[:, tj, :], axis=AX)),
                  ["gl"], ["gsc"])
            gstep(lambda tj: (lambda e: e.tensor_scalar(gsc[:, tj, 1:2],
                                                        gsc[:, tj, 0:1], -1.0,
                                                        None, ALU.mult)),
                  ["gsc"], ["gsc"])
            for tj in range(4):
                S.op("scalar",
                     (lambda tj=tj: lambda e: e.activation(
                         gv[:, tj, :], gl[:, tj, :], AF.Exp,
                         bias=gsc[:, tj, 1:2]))(),
                     reads=[("gl", tj), ("gsc", tj)], writes=[("gv", tj)],
                     tag=t + "Cg")
            gstep(lambda tj: (lambda e: e.reduce_sum(gsc[:, tj, 2:3],
                                                     gv[:, tj, :], axis=AX)),
                  ["gv"], ["gsc"])
            gstep(lambda tj: (lambda e: e.reciprocal(gsc[:, tj, 3:4],
                                                     gsc[:, tj, 2:3])),
                  ["gsc"], ["gsc"])
            gstep(lambda tj: (lambda e: e.tensor_scalar(gv[:, tj, :],
                                                        gv[:, tj, :],
                                                        gsc[:, tj, 3:4],
                                                        None, ALU.mult)),
                  ["gv", "gsc"], ["gv"])
            gstep(lambda tj: (lambda e: e.reduce_max(gsc[:, tj, 0:1],
                                                     gv[:, tj, :], axis=AX)),
                  ["gv"], ["gsc"])
            gstep(lambda tj: (lambda e: e.tensor_scalar(gl[:, tj, :],
                                                        gv[:, tj, :],
                                                        gsc[:, tj, 0:1],
                                                        None, ALU.is_equal)),
                  ["gv", "gsc"], ["gl"])
            gstep(lambda tj: (lambda e: e.scalar_tensor_tensor(
                gl[:, tj, :], gl[:, tj, :], -2.0, gv[:, tj, :],
                ALU.mult, ALU.add)), ["gl", "gv"], ["gl"])
            gstep(lambda tj: (lambda e: e.reduce_max(gsc[:, tj, 1:2],
                                                     gl[:, tj, :], axis=AX)),
                  ["gl"], ["gsc"])
            gstep(lambda tj: (lambda e: e.tensor_scalar(gl[:, tj, :],
                                                        gv[:, tj, :],
                                                        gsc[:, tj, 1:2],
                                                        None, ALU.is_ge)),
                  ["gv", "gsc"], ["gl"])
            gstep(lambda tj: (lambda e: e.tensor_mul(gv[:, tj, :],
                                                     gv[:, tj, :],
                                                     gl[:, tj, :])),
                  ["gv", "gl"], ["gv"])
            for tj in range(4):
                ti = s * 4 + tj
                pbT, pkT = self.bank()
                S.op("tensor",
                     (lambda pbT=pbT, tj=tj: lambda e: e.matmul(
                         pbT[0:8, 0:128], gv[:, tj, :], self.ident[:],
                         start=True, stop=True))(),
                     reads=[("gv", tj), "ident"], writes=[pkT], tag=t + "Cgt")
                S.op("scalar",
                     (lambda pbT=pbT, ti=ti: lambda e: e.copy(
                         gfm[0:8, ti * 128:(ti + 1) * 128],
                         pbT[0:8, 0:128]))(),
                     reads=[pkT], writes=[("gfm", ti // 4)], tag=t + "Cge")
        scC.close()
        S.barrier()

        scD = self.scope()
        xm = self.sal(scD, t + "_xm", [128, 8, 4, 512], BF16)
        yb = self.sal(scD, t + "_yb", [128, 8, 4, 512], F32)
        hb = self.sal(scD, t + "_hb", [128, 8, 512], BF16)
        ws = [self.sal(scD, t + f"_ws{j}", [128, 8, 1024], BF16)
              for j in range(3)]
        gsb = self.sal(scD, t + "_gsb", [128, 512], F32)
        asc = self.sal(scD, t + "_asc", [128, 512], F32)
        bsc = self.sal(scD, t + "_bsc", [128, 512], F32)
        b1a = self.sal(scD, t + "_b1a", [128, 8, 8], F32)
        b2a = self.sal(scD, t + "_b2a", [128, 8, 8], F32)
        boh = self.sal(scD, t + "_boh", [8, 1024], F32)
        exs = self.sal(scD, t + "_exs", [128, 2, 512], F32)
        ehs = self.sal(scD, t + "_ehs", [128, 2, 512], F32)
        S.dma("sync", b1a[:], inp["fb1"].rearrange("e p c -> p e c"),
              writes=["b1a"], tag=t + "D")
        S.dma("sync", b2a[:], inp["fb2"].rearrange("e p c -> p e c"),
              writes=["b2a"], tag=t + "D")

        oh8 = self.sal(scD, t + "_oh8", [8, 1024], F32)
        S.dma("sync", oh8[:], inp["oh8"], writes=["oh8"], tag=t + "D")
        S.dma("sync", boh[:], inp["fboutall"], writes=["boh"], tag=t + "D")
        wser = 0
        for hf in range(2):
            h0 = hf * 2048
            for c in range(DC):
                S.dma("sync", xm[:, c, :, :].rearrange("p a b -> p (a b)"),
                      self.xmoe[c * 128:(c + 1) * 128, h0:h0 + 2048],
                      reads=[("xmoe", c, 4 * hf), ("xmoe", c, 4 * hf + 1),
                             ("xmoe", c, 4 * hf + 2), ("xmoe", c, 4 * hf + 3),
                             (t + "xm", c)],
                      writes=[(t + "xm", c)], tag=t + "Dxm")
            for e8 in range(EXPERTS):
                wkeys = []
                for j in range(3):
                    sl = wser % 3
                    wser += 1
                    S.dma("sync", ws[sl][:].rearrange("p c m -> p (c m)"),
                          self.wcache[e8 * 3 + j],
                          reads=[("wc", e8 * 3 + j), ("wsl", sl)],
                          writes=[("wsl", sl)], tag=t + "Dwl")
                    wkeys.append(("wsl", sl))
                w1s, w2s, wos = ws[(wser - 3) % 3], ws[(wser - 2) % 3], ws[(wser - 1) % 3]
                k1, k2, ko = wkeys
                for n in range(4):
                    gn = hf * 2048 + n * 512
                    pbG, pkG = self.bank(0)
                    S.op("tensor",
                         (lambda pbG=pbG, e8=e8, gn=gn: lambda e: e.matmul(
                             pbG[:], oh8[0:8, e8 * 128:(e8 + 1) * 128],
                             gfm[0:8, gn:gn + 512], start=True, stop=True))(),
                         reads=[("gfm", gn // 512), "oh8"],
                         writes=[pkG], tag=t + "Dg")
                    S.op("scalar",
                         (lambda pbG=pbG: lambda e: e.copy(gsb[:], pbG[:]))(),
                         reads=[pkG], writes=["gsb"], tag=t + "Dge")
                    for m in range(DC):
                        pbA, pkA = self.bank(1 + (m % 2))
                        for k in range(DC):
                            S.op("tensor",
                                 (lambda pbA=pbA, w1s=w1s, k=k, m=m, n=n,
                                  first=(k == 0), last=(k == DC - 1):
                                  lambda e: e.matmul(
                                      pbA[:], w1s[:, k, m * 128:(m + 1) * 128],
                                      xm[:, k, n, :], start=first, stop=last))(),
                                 reads=[k1, (t + "xm", k)], writes=[pkA],
                                 tag=t + "Da")
                        S.op("scalar",
                             (lambda pbA=pbA, e8=e8, m=m: lambda e: e.activation(
                                 asc[:], pbA[:], AF.Silu,
                                 bias=b1a[:, e8, m:m + 1]))(),
                             reads=[pkA, "b1a"], writes=["asc"], tag=t + "Dae")
                        pbB, pkB = self.bank(3 + (m % 2))
                        for k in range(DC):
                            S.op("tensor",
                                 (lambda pbB=pbB, w2s=w2s, k=k, m=m, n=n,
                                  first=(k == 0), last=(k == DC - 1):
                                  lambda e: e.matmul(
                                      pbB[:], w2s[:, k, m * 128:(m + 1) * 128],
                                      xm[:, k, n, :], start=first, stop=last))(),
                                 reads=[k2, (t + "xm", k)], writes=[pkB],
                                 tag=t + "Db")
                        S.op("vector",
                             (lambda pbB=pbB, e8=e8, m=m: lambda e:
                              e.scalar_tensor_tensor(
                                  bsc[:], pbB[:], b2a[:, e8, m:m + 1], gsb[:],
                                  ALU.add, ALU.mult))(),
                             reads=[pkB, "b2a", "gsb"], writes=["bsc"],
                             tag=t + "Dbe")
                        S.op("vector",
                             (lambda m=m: lambda e: e.tensor_mul(
                                 hb[:, m, :], asc[:], bsc[:]))(),
                             reads=["asc", "bsc"], writes=[("hb", m)],
                             tag=t + "Dh")
                    for m2 in range(DC):
                        pbY, pkY = self.bank(5 + (m2 % 2))
                        for k in range(DC):
                            S.op("tensor",
                                 (lambda pbY=pbY, wos=wos, k=k, m2=m2,
                                  first=(k == 0),
                                  last=(k == DC - 1 and e8 > 0):
                                  lambda e: e.matmul(
                                      pbY[:], wos[:, k, m2 * 128:(m2 + 1) * 128],
                                      hb[:, k, :], start=first, stop=last))(),
                                 reads=[ko, ("hb", k)], writes=[pkY],
                                 tag=t + "Dw")
                        if e8 == 0:
                            S.op("tensor",
                                 (lambda pbY=pbY, m2=m2, gn=gn:
                                  lambda e: e.matmul(
                                      pbY[:], boh[0:8, m2 * 128:
                                                  (m2 + 1) * 128],
                                      gfm[0:8, gn:gn + 512],
                                      start=False, stop=True))(),
                                 reads=["boh", ("gfm", gn // 512)], writes=[pkY],
                                 tag=t + "Dwb")
                        if e8 == 0:
                            S.op("vector",
                                 (lambda pbY=pbY, m2=m2, n=n: lambda e:
                                  e.tensor_copy(yb[:, m2, n, :], pbY[:]))(),
                                 reads=[pkY], writes=[(t + "yb", m2, n)],
                                 tag=t + "Dy")
                        else:
                            S.op("vector",
                                 (lambda pbY=pbY, m2=m2, n=n: lambda e:
                                  e.tensor_add(yb[:, m2, n, :], pbY[:],
                                               yb[:, m2, n, :]))(),
                                 reads=[pkY, (t + "yb", m2, n)],
                                 writes=[(t + "yb", m2, n)], tag=t + "Dy")
            for n in range(4):
                s = hf * 4 + n
                t0 = s * 512
                for c in range(DC):
                    sl = c % 2
                    S.dma("sync", exs[:, sl, :],
                          self.xres[c * 128:(c + 1) * 128, t0:t0 + 512],
                          reads=[("xres", c, s), (t + "exs", sl)],
                          writes=[(t + "exs", sl)], tag=t + "Del")
                    S.dma("sync", ehs[:, sl, :],
                          self.zst[c * 128:(c + 1) * 128, t0:t0 + 512],
                          reads=[("zst", c, s), (t + "ehs", sl)],
                          writes=[(t + "ehs", sl)], tag=t + "Del")
                    S.op("vector",
                         (lambda c=c, n=n, sl=sl: lambda e: e.tensor_add(
                             ehs[:, sl, :], yb[:, c, n, :], ehs[:, sl, :]))(),
                         reads=[(t + "yb", c, n), (t + "ehs", sl)],
                         writes=[(t + "ehs", sl)], tag=t + "De1")
                    S.op("vector",
                         (lambda c=c, sl=sl: lambda e: e.tensor_add(
                             exs[:, sl, :], ehs[:, sl, :], exs[:, sl, :]))(),
                         reads=[(t + "ehs", sl), (t + "exs", sl)],
                         writes=[(t + "exs", sl)], tag=t + "De2")
                    S.dma("gpsimd",
                          self.xres[c * 128:(c + 1) * 128, t0:t0 + 512],
                          exs[:, sl, :], reads=[(t + "exs", sl)],
                          writes=[("xres", c, s)], tag=t + "Des")
        scD.close()
        scCD.close()
        S.barrier()


    def mamba(self):
        nc, S = self.nc, self.S
        inp = self.inp
        HL = L // 2
        sc = self.scope()
        wup = self.sal(sc, "m_wup", [128, 8, 256], BF16)
        wde = self.sal(sc, "m_wde", [128, 128], BF16)
        wB = self.sal(sc, "m_wB", [128, 64], BF16)
        wC = self.sal(sc, "m_wC", [128, 64], BF16)
        wdn = self.sal(sc, "m_wdn", [128, 1024], BF16)
        moh = self.sal(sc, "m_oh", [64, 8192], BF16)
        self.set_cast(sc, "m")
        self.cast_weight_to("mWup", wup, "m_wup")
        stg = self._cast_stg[0]
        for (wn, dst, dk) in (("mWdelta", wde, "m_wde"), ("mWB", wB, "m_wB"),
                              ("mWC", wC, "m_wC"), ("mWdown", wdn, "m_wdn")):
            dout = inp[wn].shape[1]
            S.dma("sync", stg[:, 0:dout], inp[wn], reads=[("cstg", 0)],
                  writes=[("cstg", 0)], tag="m_cw")
            S.op("scalar",
                 (lambda dst=dst, dout=dout, stg=stg:
                  lambda e: e.copy(dst[:, 0:dout], stg[:, 0:dout]))(),
                 reads=[("cstg", 0)], writes=[(dk,)], tag="m_cw")
        S.dma("sync", moh[:], inp["moh"], writes=["moh"], tag="m_c")
        vecs = self.sal(sc, "m_vec", [128, 32], F32)
        S.dma("sync", vecs[:, 0:2], inp["mbup"], writes=["mvec"], tag="m_c")
        S.dma("sync", vecs[:, 2:5], inp["mconvw"], writes=["mvec"], tag="m_c")
        S.dma("sync", vecs[:, 5:6], inp["mconvb"], writes=["mvec"], tag="m_c")
        S.dma("sync", vecs[:, 6:7], inp["mbdelta"], writes=["mvec"], tag="m_c")
        S.dma("sync", vecs[:, 7:8], inp["mdskip"], writes=["mvec"], tag="m_c")
        S.dma("sync", vecs[:, 8:16], inp["mbdown"], writes=["mvec"], tag="m_c")
        bBC = self.sal(sc, "m_bbc", [64, 2], F32)
        S.dma("sync", bBC[:, 0:1], inp["mbB"], writes=["mbbc"], tag="m_c")
        S.dma("sync", bBC[:, 1:2], inp["mbC"], writes=["mbbc"], tag="m_c")
        amat = self.sal(sc, "m_amat", [128, 64], F32)
        S.dma("sync", amat[:], inp["mAlog"], writes=["amat"], tag="m_c")
        S.op("scalar", lambda e: e.activation(amat[:], amat[:], AF.Exp),
             reads=["amat"], writes=["amat"], tag="m_c")
        S.op("vector", lambda e: e.tensor_scalar(amat[:], amat[:], -1.0, None,
                                                 ALU.mult),
             reads=["amat"], writes=["amat"], tag="m_c")
        mcar = self.sal(sc, "m_car", [128, 64], F32)
        S.op("vector", lambda e: e.memset(mcar[:], 0.0), writes=["mcar"],
             tag="m_c")

        lpad = self.sal(sc, "m_lpad", [128, 2 + HL], F32)
        rsig = self.sal(sc, "m_rsig", [128, HL], BF16)
        xmf = self.sal(sc, "m_xmf", [128, HL], F32)
        xmb = self.sal(sc, "m_xmb", [128, HL], BF16)
        delta = self.sal(sc, "m_delta", [128, HL], F32)
        dxb = self.sal(sc, "m_dx", [128, HL], F32)
        Bmat = self.sal(sc, "m_B", [64, HL], BF16)
        Cmat = self.sal(sc, "m_C", [64, HL], BF16)
        da = self.sal(sc, "m_da", [128, HL], F32)
        dbx = self.sal(sc, "m_dbx", [128, HL], F32)
        yac = self.sal(sc, "m_y", [128, HL], F32)
        ytmp = self.sal(sc, "m_yt", [128, 2, 512], F32)
        cbe = self.sal(sc, "m_cbe", [128, 512], F32)
        exm = self.sal(sc, "m_exm", [128, 2, 512], F32)
        scM1 = self.scope()
        xsm = self.sal(scM1, "m_xs", [128, 8, 512], F32)
        zsm = self.sal(scM1, "m_zs", [128, 8, 512], BF16)
        S.op("vector", lambda e: e.memset(lpad[:, 0:2], 0.0), writes=["lpad0"],
             tag="m_i")
        NH2 = NT // 2

        for hf in range(2):
            if hf == 1:
                S.op("scalar", lambda e: e.copy(lpad[:, 0:2], lpad[:, HL:HL + 2]),
                     reads=[("lpad", 3)], writes=["lpad0"], tag="m_bd")
            for sh in range(NH2):
                s = hf * NH2 + sh
                t0 = s * 512
                th = sh * 512
                for c in range(DC):
                    S.dma("sync", xsm[:, c, :],
                          self.xres[c * 128:(c + 1) * 128, t0:t0 + 512],
                          reads=[("xres", c, s), ("m_xs", c)],
                          writes=[("m_xs", c)], tag="m_ld")
                self.ln_strip(xsm, "m_xs", zsm, "m_zs", self.ln_sq)
                pbL, pkL = self.bank()
                for k in range(DC):
                    S.op("tensor",
                         (lambda pbL=pbL, k=k, first=(k == 0),
                          last=(k == DC - 1):
                          lambda e: e.matmul(pbL[:], wup[:, k, 0:128],
                                             zsm[:, k, :], start=first,
                                             stop=last))(),
                         reads=[("m_wup", k), ("m_zs", k)], writes=[pkL],
                         tag="m_um")
                S.op("vector",
                     (lambda pbL=pbL, th=th: lambda e: e.tensor_scalar(
                         lpad[:, 2 + th:2 + th + 512], pbL[:], vecs[:, 0:1],
                         None, ALU.add))(),
                     reads=[pkL, "mvec"], writes=[("lpad", sh)], tag="m_ue")
                pbR, pkR = self.bank()
                for k in range(DC):
                    S.op("tensor",
                         (lambda pbR=pbR, k=k, first=(k == 0),
                          last=(k == DC - 1):
                          lambda e: e.matmul(pbR[:], wup[:, k, 128:256],
                                             zsm[:, k, :], start=first,
                                             stop=last))(),
                         reads=[("m_wup", k), ("m_zs", k)], writes=[pkR],
                         tag="m_um")
                S.op("scalar",
                     (lambda pbR=pbR, th=th: lambda e: e.activation(
                         rsig[:, th:th + 512], pbR[:], AF.Silu,
                         bias=vecs[:, 1:2]))(),
                     reads=[pkR, "mvec"], writes=[("rsig", sh)], tag="m_ue")
            lk = [("lpad", sh) for sh in range(NH2)] + ["lpad0"]
            S.op("vector", lambda e: e.tensor_scalar(xmf[:], lpad[:, 0:HL],
                                                     vecs[:, 2:3], None,
                                                     ALU.mult),
                 reads=lk + ["mvec", "xmf"], writes=["xmf"], tag="m_cv")
            S.op("vector", lambda e: e.scalar_tensor_tensor(
                xmf[:], lpad[:, 1:1 + HL], vecs[:, 3:4], xmf[:], ALU.mult,
                ALU.add),
                 reads=lk + ["mvec", "xmf"], writes=["xmf"], tag="m_cv")
            S.op("vector", lambda e: e.scalar_tensor_tensor(
                xmf[:], lpad[:, 2:2 + HL], vecs[:, 4:5], xmf[:], ALU.mult,
                ALU.add),
                 reads=lk + ["mvec", "xmf"], writes=["xmf"], tag="m_cv")
            S.op("scalar", lambda e: e.activation(xmf[:], xmf[:], AF.Silu,
                                                  bias=vecs[:, 5:6]),
                 reads=["xmf", "mvec"], writes=["xmf"], tag="m_cv")
            S.op("scalar", lambda e: e.copy(xmb[:], xmf[:]),
                 reads=["xmf"], writes=["xmb"], tag="m_cv")
            for nblk in range(NH2):
                th = nblk * 512
                pbD, pkD = self.bank()
                S.op("tensor",
                     (lambda pbD=pbD, th=th: lambda e: e.matmul(
                         pbD[:], wde[:], xmb[:, th:th + 512], start=True,
                         stop=True))(),
                     reads=[("m_wde",), "xmb"], writes=[pkD], tag="m_dm")
                S.op("vector",
                     (lambda pbD=pbD: lambda e: e.tensor_scalar(
                         cbe[:], pbD[:], vecs[:, 6:7], None, ALU.add))(),
                     reads=[pkD, "mvec"], writes=["m_cbe"], tag="m_dsp")
                S.op("scalar",
                     (lambda th=th: lambda e: e.activation(
                         delta[:, th:th + 512], cbe[:], AF.Relu))(),
                     reads=["m_cbe"], writes=[("delta", nblk)], tag="m_dsp")
                S.op("scalar",
                     lambda e: e.activation(cbe[:], cbe[:], AF.Abs),
                     reads=["m_cbe"], writes=["m_cbe"], tag="m_dsp")
                S.op("scalar", lambda e: e.activation(cbe[:], cbe[:], AF.Exp,
                                                      scale=-1.0),
                     reads=["m_cbe"], writes=["m_cbe"], tag="m_dsp")
                S.op("scalar", lambda e: e.activation(cbe[:], cbe[:], AF.Ln,
                                                      bias=self.cbias(1.0)),
                     reads=["m_cbe", "cpool"], writes=["m_cbe"], tag="m_dsp")
                S.op("vector",
                     (lambda th=th: lambda e: e.tensor_add(
                         delta[:, th:th + 512], delta[:, th:th + 512],
                         cbe[:]))(),
                     reads=[("delta", nblk), "m_cbe"],
                     writes=[("delta", nblk)], tag="m_dsp")
                for (wmat, wkey, dst, dk, bcol) in (
                        (wB, "m_wB", Bmat, "m_B", 0),
                        (wC, "m_wC", Cmat, "m_C", 1)):
                    pbB, pkB = self.bank()
                    S.op("tensor",
                         (lambda pbB=pbB, wmat=wmat, th=th: lambda e: e.matmul(
                             pbB[0:64, :], wmat[:], xmb[:, th:th + 512],
                             start=True, stop=True))(),
                         reads=[(wkey,), "xmb"], writes=[pkB], tag="m_bc")
                    S.op("vector",
                         (lambda pbB=pbB, dst=dst, th=th, bcol=bcol:
                          lambda e: e.tensor_scalar(
                              dst[0:64, th:th + 512], pbB[0:64, :],
                              bBC[:, bcol:bcol + 1], None, ALU.add))(),
                         reads=[pkB, "mbbc"], writes=[(dk, nblk)], tag="m_bc")
            dk_all = [("delta", nblk) for nblk in range(NH2)]
            S.op("vector", lambda e: e.tensor_mul(dxb[:], delta[:], xmf[:]),
                 reads=dk_all + ["xmf", "m_dxk"], writes=["m_dxk"], tag="m_dx")
            bk_all = [("m_B", nblk) for nblk in range(NH2)]
            ck_all = [("m_C", nblk) for nblk in range(NH2)]
            for n64 in range(64):
                S.op("vector",
                     (lambda n64=n64: lambda e: e.tensor_scalar(
                         da[:], delta[:], amat[:, n64:n64 + 1], None,
                         ALU.mult))(),
                     reads=dk_all + ["amat", "m_dak"], writes=["m_dak"],
                     tag="m_n1")
                S.op("scalar", lambda e: e.activation(da[:], da[:], AF.Exp),
                     reads=["m_dak"], writes=["m_dak"], tag="m_n2")
                for nblk in range(NH2):
                    th = nblk * 512
                    pbB, pkB = self.bank(0 if nblk % 2 == 0 else 1)
                    S.op("tensor",
                         (lambda pbB=pbB, n64=n64, th=th: lambda e: e.matmul(
                             pbB[:], moh[:, n64 * 128:(n64 + 1) * 128],
                             Bmat[0:64, th:th + 512], start=True, stop=True))(),
                         reads=[bk_all[nblk], "moh"], writes=[pkB], tag="m_n3")
                    S.op("vector",
                         (lambda pbB=pbB, th=th: lambda e: e.tensor_mul(
                             dbx[:, th:th + 512], dxb[:, th:th + 512],
                             pbB[:]))(),
                         reads=[pkB, "m_dxk", "m_dbk"], writes=["m_dbk"],
                         tag="m_n3")
                S.op("vector",
                     (lambda n64=n64: lambda e: e.tensor_tensor_scan(
                         dbx[:], da[:], dbx[:], mcar[:, n64:n64 + 1],
                         ALU.mult, ALU.add))(),
                     reads=["m_dak", "m_dbk", ("mcar", n64)],
                     writes=["m_dbk"], tag="m_n4")
                if hf == 0:
                    S.op("scalar",
                         (lambda n64=n64: lambda e: e.copy(
                             mcar[:, n64:n64 + 1], dbx[:, HL - 1:HL]))(),
                         reads=["m_dbk"], writes=[("mcar", n64)], tag="m_n4c")
                for nblk in range(NH2):
                    th = nblk * 512
                    sl = nblk % 2
                    pbC, pkC = self.bank(2 if nblk % 2 == 0 else 3)
                    S.op("tensor",
                         (lambda pbC=pbC, n64=n64, th=th: lambda e: e.matmul(
                             pbC[:], moh[:, n64 * 128:(n64 + 1) * 128],
                             Cmat[0:64, th:th + 512], start=True, stop=True))(),
                         reads=[ck_all[nblk], "moh"], writes=[pkC], tag="m_n5")
                    S.op("vector",
                         (lambda pbC=pbC, th=th, sl=sl: lambda e: e.tensor_mul(
                             ytmp[:, sl, :], dbx[:, th:th + 512], pbC[:]))(),
                         reads=[pkC, "m_dbk", ("m_ytk", sl)],
                         writes=[("m_ytk", sl)], tag="m_n5")
                    if n64 == 0:
                        S.op("gpsimd",
                             (lambda th=th, sl=sl: lambda e: e.tensor_copy(
                                 yac[:, th:th + 512], ytmp[:, sl, :]))(),
                             reads=[("m_ytk", sl)], writes=[("m_yk", nblk)],
                             tag="m_n6")
                    else:
                        S.op("gpsimd",
                             (lambda th=th, sl=sl: lambda e: e.tensor_add(
                                 yac[:, th:th + 512], ytmp[:, sl, :],
                                 yac[:, th:th + 512]))(),
                             reads=[("m_ytk", sl), ("m_yk", nblk)],
                             writes=[("m_yk", nblk)], tag="m_n6")
            yk_all = [("m_yk", nblk) for nblk in range(NH2)]
            S.op("vector", lambda e: e.scalar_tensor_tensor(
                yac[:], xmf[:], vecs[:, 7:8], yac[:], ALU.mult, ALU.add),
                 reads=["xmf", "mvec"] + yk_all, writes=yk_all, tag="m_y")
            rk_all = [("rsig", sh) for sh in range(NH2)]
            S.op("vector", lambda e: e.tensor_mul(xmb[:], yac[:], rsig[:]),
                 reads=yk_all + rk_all + ["xmb"], writes=["xmb"], tag="m_g")
            for nblk in range(NH2):
                s = hf * NH2 + nblk
                t0 = s * 512
                th = nblk * 512
                for m in range(DC):
                    sl = m % 2
                    pbY, pkY = self.bank(4 + sl)
                    S.op("tensor",
                         (lambda pbY=pbY, m=m, th=th: lambda e: e.matmul(
                             pbY[:], wdn[:, m * 128:(m + 1) * 128],
                             xmb[:, th:th + 512], start=True, stop=True))(),
                         reads=[("m_wdn",), "xmb"], writes=[pkY], tag="m_om")
                    S.dma("sync", exm[:, sl, :],
                          self.xres[m * 128:(m + 1) * 128, t0:t0 + 512],
                          reads=[("xres", m, s), ("m_exm", sl)],
                          writes=[("m_exm", sl)], tag="m_ol")
                    S.op("vector",
                         (lambda pbY=pbY, m=m, sl=sl: lambda e:
                          e.scalar_tensor_tensor(
                              exm[:, sl, :], pbY[:], vecs[:, 8 + m:9 + m],
                              exm[:, sl, :], ALU.add, ALU.add))(),
                         reads=[pkY, "mvec", ("m_exm", sl)],
                         writes=[("m_exm", sl)], tag="m_oa")
                    S.dma("gpsimd",
                          self.xres[m * 128:(m + 1) * 128, t0:t0 + 512],
                          exm[:, sl, :], reads=[("m_exm", sl)],
                          writes=[("xres", m, s)], tag="m_os")
        scM1.close()
        sc.close()
        S.barrier()

    def p7_out(self):
        S = self.S
        S.barrier()
        sc7 = self.scope()
        tin = self.sal(sc7, "t7_tin", [128, 4, 1024], F32)
        tst = self.sal(sc7, "t7_tst", [128, 8, 512], F32)
        for g in range(8):
            for c in range(DC):
                S.dma("sync", tst[:, c, :],
                      self.xres[c * 128:(c + 1) * 128, g * 512:(g + 1) * 512],
                      reads=[("xres", c, g), ("tst", c)], writes=[("tst", c)],
                      tag="p7ld")
            for j in range(4):
                for c in range(DC):
                    pb, pk = self.bank()
                    S.op("tensor",
                         (lambda pb=pb, c=c, j=j: lambda e: e.transpose(
                             pb[:, 0:128], tst[:, c, j * 128:(j + 1) * 128],
                             self.ident[:]))(),
                         reads=[("tst", c), "ident"], writes=[pk], tag="p7t")
                    S.op("scalar",
                         (lambda pb=pb, c=c, j=j: lambda e: e.copy(
                             tin[:, j, c * 128:(c + 1) * 128], pb[:, 0:128]))(),
                         reads=[pk], writes=[("tin", j)], tag="p7e")
            for j in range(4):
                ti = g * 4 + j
                S.dma("gpsimd", self.out[ti * 128:(ti + 1) * 128, :],
                      tin[:, j, :], reads=[("tin", j)], writes=[("out", ti)],
                      tag="p7st")
        sc7.close()

    def finalize(self):
        S = self.S
        S.barrier()
        S.op("vector", lambda e: e.memset(self.ones_p[0:1, 0:1], 1.0),
             writes=["ones_p"], tag="fin")
        S.emit(self.stack)
        return self.nc

    def build(self):
        nc = self.nc
        self.ln_sq = nc.alloc_sbuf_tensor("ln_sq", [128, 512], F32)
        self.ln_nsc = nc.alloc_sbuf_tensor("ln_nsc", [128, 512], F32)
        self.ln_sv = nc.alloc_sbuf_tensor("ln_sv", [1, 4 * 512], F32)
        self.load_consts()
        self.emit_cpool()
        self.p0_transpose_in()
        if self.upto >= 1:
            self.p1_attn_a()
        if self.upto >= 2:
            self.ffn("F1")
        if self.upto >= 3:
            self.mamba()
        if self.upto >= 4:
            self.ffn("F2")
        if self.dbg:
            S = self.S
            S.barrier()
            dz = self.nc.dram_tensor("dzst", [D, L], F32, kind="ExternalOutput").ap()
            da_ = self.nc.dram_tensor("dao", [D, L], BF16, kind="ExternalOutput").ap()
            dx_ = self.nc.dram_tensor("dxmoe", [D, L], BF16, kind="ExternalOutput").ap()
            S.dma("sync", dz, self.zst, reads=[], writes=["dbg1"], tag="dbg")
            dk_ = self.nc.dram_tensor("dkh", [16, 64, L], BF16, kind="ExternalOutput").ap()
            dq_ = self.nc.dram_tensor("dqh", [16, 64, L], BF16, kind="ExternalOutput").ap()
            dv_ = self.nc.dram_tensor("dvk", [16, 128, 32 * 65], BF16, kind="ExternalOutput").ap()
            S.dma("sync", dk_, self.khbm, reads=[], writes=["dbg4"], tag="dbg")
            S.dma("sync", dq_, self.qhbm, reads=[], writes=["dbg5"], tag="dbg")
            S.dma("sync", dv_, self.vkhbm, reads=[], writes=["dbg6"], tag="dbg")
            S.dma("sync", da_, self.ao, reads=[], writes=["dbg2"], tag="dbg")
            S.dma("sync", dx_, self.xmoe, reads=[], writes=["dbg3"], tag="dbg")
        self.p7_out()
        return self.finalize()


def build_program(upto=99, dbg=False):
    mk = MK(upto, dbg)
    nc = mk.build()
    return nc, mk


def make_in_maps(inputs, n_cores=8):
    import ml_dtypes
    bf16 = ml_dtypes.bfloat16
    g = lambda k: np.ascontiguousarray(np.asarray(inputs[k], dtype=np.float32))
    ppvec = lambda v: np.ascontiguousarray(v.reshape(DC, 128).T)

    shared = {}
    shared["ident"] = np.eye(128, dtype=np.float32)
    c0 = host_consts(0)
    shared["tri"] = c0["tri"]
    shared["wmask"] = c0["wmask"]
    shared["ropec"] = c0["rope_ck"].astype(bf16)
    shared["ropes"] = c0["rope_sk"].astype(bf16)
    shared["pmb"] = c0["rope_pm"].astype(bf16)
    for w, k in (("aWq", "L0_attn_a_Wq"), ("aWk", "L0_attn_a_Wk"),
                 ("aWv", "L0_attn_a_Wv"), ("aWc", "L0_attn_a_Wc"),
                 ("fWq", "L0_ffn_wq"), ("fWk", "L0_ffn_wk"),
                 ("fWv", "L0_ffn_wv"), ("fWo", "L0_ffn_wo"),
                 ("fW1", "L0_ffn_w1"), ("fW2", "L0_ffn_w2"),
                 ("fWout", "L0_ffn_wout"), ("mWup", "L0_attn_b_W_up"),
                 ("mWdelta", "L0_attn_b_W_delta"), ("mWB", "L0_attn_b_W_B"),
                 ("mWC", "L0_attn_b_W_C"), ("mWdown", "L0_attn_b_W_down"),
                 ("mAlog", "L0_attn_b_A_log"), ("mconvw", "L0_attn_b_conv_w")):
        shared[w] = g(k)
    shared["fscale1"] = ppvec(g("L0_ffn_scale1"))
    shared["fscale2"] = ppvec(g("L0_ffn_scale2"))
    rw = g("L0_ffn_router_w")
    shared["frw"] = np.ascontiguousarray(rw.reshape(DC, 128, 8).transpose(1, 0, 2))
    for w, k in (("fb1", "L0_ffn_b1"), ("fb2", "L0_ffn_b2")):
        b = g(k)
        shared[w] = np.ascontiguousarray(b.reshape(8, DC, 128).transpose(0, 2, 1))
    shared["fboutall"] = np.ascontiguousarray(
        g("L0_ffn_bout").astype(np.float32))
    oh8 = np.zeros((8, 1024), dtype=np.float32)
    for e in range(8):
        oh8[e, e * 128:(e + 1) * 128] = 1.0
    shared["oh8"] = oh8
    moh = np.zeros((64, 8192), dtype=np.float32)
    for n in range(64):
        moh[n, n * 128:(n + 1) * 128] = 1.0
    shared["moh"] = moh.astype(bf16)
    shared["mbup"] = np.ascontiguousarray(
        g("L0_attn_b_b_up").reshape(2, 128).T)
    shared["mconvb"] = g("L0_attn_b_conv_b").reshape(128, 1)
    shared["mbdelta"] = g("L0_attn_b_b_delta").reshape(128, 1)
    shared["mbB"] = g("L0_attn_b_b_B").reshape(64, 1)
    shared["mbC"] = g("L0_attn_b_b_C").reshape(64, 1)
    shared["mdskip"] = g("L0_attn_b_D_skip").reshape(128, 1)
    shared["mbdown"] = ppvec(g("L0_attn_b_b_down"))

    X = np.asarray(inputs["L0_attn_a_X"], dtype=np.float32)
    noise = _driver_gumbel((8, L, 8)) * np.float32(0.05)
    maps = []
    for core in range(n_cores):
        m = dict(shared)
        m["xin"] = np.ascontiguousarray(X[core])
        m["fnoise"] = np.ascontiguousarray(
            noise[core].reshape(32, 128, 8).transpose(1, 0, 2))
        maps.append(m)
    return maps


_CACHE = {}


def kernel(**inputs):
    if "nc" not in _CACHE:
        nc, mk = build_program(4)
        _CACHE["nc"] = nc
    nc = _CACHE["nc"]
    in_maps = make_in_maps(inputs, 8)
    res = run_bass_kernel_spmd(nc, in_maps, core_ids=list(range(8)))
    out = np.stack([res.results[c]["out"] for c in range(8)], axis=0)
    return out


# version 06 (final)
# speedup vs baseline: 1.1322x; 1.0888x over previous
from contextlib import ExitStack

import numpy as np
import concourse.bass as bass
import concourse.mybir as mybir
from concourse.bass_utils import run_bass_kernel_spmd

F32, BF16 = mybir.dt.float32, mybir.dt.bfloat16
AF = mybir.ActivationFunctionType
ALU = mybir.AluOpType
AX = mybir.AxisListType.X

D, L = 1024, 4096
DC = D // 128
NT = L // 512
NTT = L // 128
LN_EPS = 1e-5
RMS_EPS = 1e-6
NHP = 8
EXPERTS = 8

ENGS = ("sync", "tensor", "vector", "scalar", "gpsimd")
DMA_RING = {"sync": 10, "gpsimd": 8, "scalar": 4, "vector": 4, "tensor": 4}


class Op:
    __slots__ = ("eng", "idx", "fn", "is_dma", "marked", "inc_id", "dma_serial",
                 "slot", "cycle", "deps", "tag")

    def __init__(self, eng, idx, fn, is_dma, tag):
        self.eng, self.idx, self.fn, self.is_dma = eng, idx, fn, is_dma
        self.marked = False
        self.inc_id = None
        self.dma_serial = None
        self.slot = None
        self.cycle = None
        self.deps = []
        self.tag = tag


class Sched:
    def __init__(self, nc):
        self.nc = nc
        self.streams = {e: [] for e in ENGS}
        self.bufs = {}
        self.pending = {e: [] for e in ENGS}
        self.dma_count = {e: 0 for e in ENGS}

    def op(self, eng, fn, reads=(), writes=(), dma=False, tag=""):
        o = Op(eng, len(self.streams[eng]), fn, dma, tag)
        if dma:
            o.dma_serial = self.dma_count[eng]
            self.dma_count[eng] += 1
            r = DMA_RING[eng]
            o.slot = o.dma_serial % r
            o.cycle = o.dma_serial // r
        deps = {}

        def add_dep(p):
            if p is o:
                return
            if p.eng == eng and not p.is_dma and not dma:
                return
            if p.is_dma:
                key = (p.eng, "dma", p.slot)
            else:
                key = (p.eng, "eng")
            cur = deps.get(key)
            if cur is None or p.idx > cur.idx:
                deps[key] = p

        for k in reads:
            st = self.bufs.get(k)
            if st:
                for w in st[0]:
                    add_dep(w)
                st[1].append(o)
        for k in writes:
            st = self.bufs.get(k)
            if st is None:
                st = ([], [])
                self.bufs[k] = st
            for w in st[0]:
                add_dep(w)
            for r in st[1]:
                add_dep(r)
            if st[1]:
                st[0].clear()
                st[1].clear()
            st[0].append(o)
        for p in self.pending[eng]:
            add_dep(p)
        self.pending[eng] = []
        o.deps = list(deps.values())
        for p in o.deps:
            if not p.is_dma:
                p.marked = True
        self.streams[eng].append(o)
        return o

    def dma(self, eng, out, in_, reads=(), writes=(), tag=""):
        return self.op(eng, lambda e: e.dma_start(out=out, in_=in_),
                       reads=reads, writes=writes, dma=True, tag=tag)

    def barrier(self):
        lasts = []
        for e in ENGS:
            if self.streams[e]:
                lasts.append(self.streams[e][-1])
        for e in ENGS:
            self.pending[e] = [p for p in lasts
                               if not (p.eng == e and not p.is_dma)]
            for p in self.pending[e]:
                if not p.is_dma:
                    p.marked = True

    def emit(self, stack):
        nc = self.nc
        esem = {}
        dsem = {}
        for e in ENGS:
            ops = self.streams[e]
            cnt = 0
            for o in ops:
                if o.marked and not o.is_dma:
                    cnt += 1
                    o.inc_id = cnt
            if cnt:
                esem[e] = stack.enter_context(nc.semaphore(f"es_{e}"))
            if self.dma_count[e]:
                nslots = min(DMA_RING[e], self.dma_count[e])
                dsem[e] = [stack.enter_context(nc.semaphore(f"ds_{e}{i}"))
                           for i in range(nslots)]

        def emit_stream(eng_obj, e):
            water = {}
            for o in self.streams[e]:
                for p in o.deps:
                    if p.is_dma:
                        sem = dsem[p.eng][p.slot]
                        val = 16 * (p.cycle + 1)
                        wkey = (p.eng, "dma", p.slot)
                    else:
                        sem, val = esem[p.eng], p.inc_id
                        wkey = (p.eng, "eng")
                    if water.get(wkey, 0) < val:
                        water[wkey] = val
                        eng_obj.wait_ge(sem, val)
                if o.is_dma and o.cycle > 0:
                    sem = dsem[e][o.slot]
                    val = 16 * o.cycle
                    wkey = (e, "dma", o.slot)
                    if water.get(wkey, 0) < val:
                        water[wkey] = val
                        eng_obj.wait_ge(sem, val)
                ins = o.fn(eng_obj)
                if o.is_dma:
                    ins.then_inc(dsem[e][o.slot], 16)
                elif o.marked:
                    ins.then_inc(esem[e], 1)

        with nc.Block() as block:
            block.sync(lambda e: emit_stream(e, "sync"))
            block.tensor(lambda e: emit_stream(e, "tensor"))
            block.vector(lambda e: emit_stream(e, "vector"))
            block.scalar(lambda e: emit_stream(e, "scalar"))
            block.gpsimd(lambda e: emit_stream(e, "gpsimd"))

    def stats(self):
        out = {}
        for e in ENGS:
            ops = self.streams[e]
            out[e] = dict(n=len(ops), marked=sum(1 for o in ops if o.marked),
                          deps=sum(len(o.deps) for o in ops))
        return out



def _threefry2x32_np(k1, k2, x0, x1):
    def rotl(x, r):
        return ((x << np.uint32(r)) | (x >> np.uint32(32 - r))).astype(np.uint32)
    rotations = [(13, 15, 26, 6), (17, 29, 16, 24)]
    ks0 = np.uint32(k1)
    ks1 = np.uint32(k2)
    ks2 = np.uint32(np.uint32(0x1BD11BDA) ^ ks0 ^ ks1)
    x0 = (x0 + ks0).astype(np.uint32)
    x1 = (x1 + ks1).astype(np.uint32)
    ks = [ks0, ks1, ks2]
    for i in range(5):
        for r in rotations[i % 2]:
            x0 = (x0 + x1).astype(np.uint32)
            x1 = rotl(x1, r)
            x1 = (x1 ^ x0).astype(np.uint32)
        x0 = (x0 + ks[(i + 1) % 3]).astype(np.uint32)
        x1 = (x1 + ks[(i + 2) % 3] + np.uint32(i + 1)).astype(np.uint32)
    return x0, x1


def _driver_gumbel(shape):
    size = int(np.prod(shape))
    idx = np.arange(size, dtype=np.uint64)
    c1 = (idx >> np.uint64(32)).astype(np.uint32)
    c2 = (idx & np.uint64(0xFFFFFFFF)).astype(np.uint32)
    b1, b2 = _threefry2x32_np(0, 42, c1, c2)
    bits = (b1 ^ b2).astype(np.uint32)
    fl = ((bits >> np.uint32(9)) | np.uint32(0x3F800000)).view(np.float32) \
        - np.float32(1.0)
    tiny = np.float32(np.finfo(np.float32).tiny)
    u = np.maximum(tiny, (fl * (np.float32(1.0) - tiny) + tiny)).astype(np.float32)
    return (-np.log(-np.log(u))).astype(np.float32).reshape(shape)


def host_consts(core_id):
    c = {}
    c["ident"] = np.eye(128, dtype=np.float32)
    half = 32
    inv = 1.0 / 10000.0 ** (np.arange(half, dtype=np.float64) / half)
    jmod = np.arange(128) % half
    ang = np.arange(L, dtype=np.float64)[None, :] * inv[jmod][:, None]
    cos = np.cos(ang).astype(np.float32)
    sin = np.sin(ang).astype(np.float32)
    c["rope_cq"] = (cos / 8.0).astype(np.float32)
    c["rope_sq"] = (sin / 8.0).astype(np.float32)
    c["rope_ck"] = cos
    c["rope_sk"] = sin
    pm = np.zeros((128, 128), dtype=np.float32)
    for m in range(128):
        if m % 64 < 32:
            pm[m + 32, m] = -1.0
        else:
            pm[m - 32, m] = 1.0
    c["rope_pm"] = pm
    c["tri"] = np.where(np.arange(128)[None, :] <= np.arange(128)[:, None],
                        0.0, -1e9).astype(np.float32)
    j = np.arange(896)[None, :]
    k = np.arange(128)[:, None]
    c["wmask"] = np.where(j >= k + 384, 0.0, -1e9).astype(np.float32)
    noise = _driver_gumbel((8, L, 8)) * np.float32(0.05)
    c["noise"] = np.ascontiguousarray(noise[core_id])
    return c


class MK:

    def __init__(self, upto=99, dbg=False):
        self.dbg = dbg
        self.upto = upto
        nc = bass.Bass("TRN2", target_bir_lowering=False)
        self.nc = nc
        self.S = Sched(nc)
        self.stack = ExitStack()
        self.inp = {}

        def ein(name, shape, dtype=F32):
            t = nc.dram_tensor(name, shape, dtype, kind="ExternalInput").ap()
            self.inp[name] = t
            return t

        self.ein = ein
        self.xin = ein("xin", [L, D])
        self.out = nc.dram_tensor("out", [L, D], F32, kind="ExternalOutput").ap()
        self.c_ident = ein("ident", [128, 128])
        self.c_tri = ein("tri", [128, 128])
        for w in ("aWq", "aWk", "aWv", "aWc"):
            ein(w, [D, D])
        for w in ("fWq", "fWk", "fWv", "fWo"):
            ein(w, [D, D])
        for w in ("fW1", "fW2", "fWout"):
            ein(w, [8, D, D])
        ein("fscale1", [128, 8])
        ein("fscale2", [128, 8])
        ein("frw", [128, 8, 8])
        ein("fnoise", [128, 32, 8])
        ein("fb1", [8, 128, 8])
        ein("fb2", [8, 128, 8])
        ein("fboutall", [8, 1024])
        ein("ropec", [128, L], BF16)
        ein("ropes", [128, L], BF16)
        ein("pmb", [128, 128], BF16)
        ein("wmask", [128, 896])
        ein("oh8", [8, 1024])
        ein("mWup", [D, 256])
        ein("mWdelta", [128, 128])
        ein("mWB", [128, 64])
        ein("mWC", [128, 64])
        ein("mWdown", [128, D])
        ein("mbup", [128, 2])
        ein("mconvw", [128, 3])
        ein("mconvb", [128, 1])
        ein("mbdelta", [128, 1])
        ein("mbB", [64, 1])
        ein("mbC", [64, 1])
        ein("mAlog", [128, 64])
        ein("mdskip", [128, 1])
        ein("mbdown", [128, 8])
        ein("moh", [64, 8192], BF16)
        self.xres = nc.dram_tensor("xres", [D, L], F32, kind="Internal").ap()
        self.zst = nc.dram_tensor("zst", [D, L], F32, kind="Internal").ap()
        self.ao = nc.dram_tensor("ao", [D, L], BF16, kind="Internal").ap()
        self.xmoe = nc.dram_tensor("xmoe", [D, L], BF16, kind="Internal").ap()
        self.wcache = nc.dram_tensor("wcache", [24, 128, 8 * 1024], BF16,
                                     kind="Internal").ap()
        self.khbm = nc.dram_tensor("khbm", [16, 64, L], BF16, kind="Internal").ap()
        self.qhbm = nc.dram_tensor("qhbm", [16, 64, L], BF16, kind="Internal").ap()
        self.vkhbm = nc.dram_tensor("vkhbm", [16, 128, 32 * 65], BF16,
                                    kind="Internal").ap()

        sbt = nc.alloc_sbuf_tensor
        self.cpool = sbt("s_cpool", [128, 16], F32)
        self._cvals = {}
        self.ident = sbt("s_ident", [128, 128], F32)
        self.tri = sbt("s_tri", [128, 128], F32)
        self.ones_p = sbt("s_ones_p", [128, 1], F32)
        self.ones_r = sbt("s_ones_r", [1, 128], F32)
        self.pb = [nc.alloc_psum_tensor(f"pb{i}", [128, 512], F32)
                   for i in range(8)]
        self._pb_rr = 0
        self.fscale1 = sbt("s_fsc1", [128, 8], F32)
        self.fscale2 = sbt("s_fsc2", [128, 8], F32)
        self.ones64 = self.ones_r[:, 0:64]

    def bank(self, i=None):
        if i is None:
            i = self._pb_rr % 8
            self._pb_rr += 1
        return self.pb[i], f"pb{i}"

    def scope(self):
        return ExitStack()

    def sal(self, sc, name, shape, dtype):
        return sc.enter_context(self.nc.sbuf_tensor(name, shape, dtype))

    def cbias(self, v):
        v = float(v)
        if getattr(self, "_cpool_done", False):
            assert v in self._cvals, f"const {v} not pre-seeded"
        i = self._cvals.setdefault(v, len(self._cvals))
        assert i < 16
        return self.cpool[:, i:i + 1]

    def cbias1(self, v):
        return self.cbias(v)[0:1, :]

    def emit_cpool(self):
        S = self.S
        for v in (1e-6, LN_EPS, RMS_EPS, 1.0, 0.5):
            self.cbias(v)
        for v, i in sorted(self._cvals.items(), key=lambda kv: kv[1]):
            S.op("vector",
                 (lambda v=v, i=i: lambda e: e.memset(self.cpool[:, i:i + 1], v))(),
                 writes=["cpool"], tag="cpool")
        self._cpool_done = True

    def load_consts(self):
        S = self.S
        S.dma("sync", self.ident[:], self.c_ident, writes=["ident"], tag="c")
        S.dma("sync", self.tri[:], self.c_tri, writes=["tri"], tag="c")
        S.op("vector", lambda e: e.memset(self.ones_p[:], 1.0), writes=["ones_p"])
        S.op("vector", lambda e: e.memset(self.ones_r[:], 1.0), writes=["ones_r"])
        S.dma("sync", self.fscale1[:], self.inp["fscale1"], writes=["fsc1"], tag="c")
        S.dma("sync", self.fscale2[:], self.inp["fscale2"], writes=["fsc2"], tag="c")

    def p0_transpose_in(self):
        S = self.S
        sc0 = self.scope()
        tin = self.sal(sc0, "t0_tin", [128, 4, 1024], F32)
        tst = self.sal(sc0, "t0_tst", [128, 8, 512], F32)
        for g in range(8):
            for j in range(4):
                ti = g * 4 + j
                S.dma("sync", tin[:, j, :], self.xin[ti * 128:(ti + 1) * 128, :],
                      writes=[("tin", j)], tag="p0")
            for c in range(DC):
                pb, pk = self.bank()
                for j in range(4):
                    S.op("tensor",
                         (lambda pb=pb, j=j, c=c:
                          lambda e: e.transpose(pb[:, j * 128:(j + 1) * 128],
                                                tin[:, j, c * 128:(c + 1) * 128],
                                                self.ident[:]))(),
                         reads=[("tin", j), "ident"], writes=[pk], tag="p0t")
                S.op("scalar",
                     (lambda pb=pb, c=c: lambda e: e.copy(tst[:, c, :], pb[:]))(),
                     reads=[pk], writes=[("tst", c)], tag="p0e")
            for c in range(DC):
                S.dma("gpsimd",
                      self.xres[c * 128:(c + 1) * 128, g * 512:(g + 1) * 512],
                      tst[:, c, :], reads=[("tst", c)], writes=[("xres", c, g)],
                      tag="p0s")
        sc0.close()
        S.barrier()

    def set_cast(self, sc, pref):
        self._cast_stg = [self.sal(sc, pref + f"cstg{b}", [128, 1024], F32)
                          for b in range(2)]
        self._cast_i = 0

    def cast_weight_to(self, wname, dst, dst_key):
        S = self.S
        src = self.inp[wname]
        dout = src.shape[1]
        rs = src.rearrange("(c p) m -> p c m", p=128)
        for c in range(DC):
            b = self._cast_i % 2
            self._cast_i += 1
            stg = self._cast_stg[b]
            S.dma("sync", stg[:, 0:dout], rs[:, c, :],
                  reads=[("cstg", b)], writes=[("cstg", b)], tag="cw")
            S.op("scalar",
                 (lambda b=b, c=c, stg=stg, dout=dout:
                  lambda e: e.copy(dst[:, c, 0:dout], stg[:, 0:dout]))(),
                 reads=[("cstg", b)], writes=[(dst_key, c)], tag="cw")

    def ln_strip(self, xs, xname, zs, zname, sq, rms_scale=None):
        S = self.S
        sv = self.ln_sv
        pb1, pk1 = self.bank()
        pb2, pk2 = self.bank()
        xk = [(xname, c) for c in range(DC)]
        for c in range(DC):
            S.op("tensor",
                 (lambda pb1=pb1, c=c: lambda e: e.matmul(
                     pb1[0:1, :], self.ones_p[:], xs[:, c, :],
                     start=(c == 0), stop=(c == DC - 1)))(),
                 reads=[xk[c], "ones_p"], writes=[pk1], tag="lnmm")
        for c in range(DC):
            S.op("scalar",
                 (lambda c=c: lambda e: e.activation(sq[:], xs[:, c, :],
                                                     AF.Square))(),
                 reads=[xk[c]], writes=["ln_sq"], tag="lnsq")
            S.op("tensor",
                 (lambda pb2=pb2, c=c: lambda e: e.matmul(
                     pb2[0:1, :], self.ones_p[:], sq[:],
                     start=(c == 0), stop=(c == DC - 1)))(),
                 reads=["ln_sq", "ones_p"], writes=[pk2], tag="lnmm2")
        S.op("vector", lambda e: e.tensor_scalar(sv[0:1, 0:512], pb1[0:1, :],
                                                 1.0 / D, None, ALU.mult),
             reads=[pk1], writes=["sv"], tag="lnst")
        S.op("vector", lambda e: e.tensor_scalar(sv[0:1, 512:1024], pb2[0:1, :],
                                                 1.0 / D, None, ALU.mult),
             reads=[pk2], writes=["sv"], tag="lnst")
        S.op("vector", lambda e: e.tensor_mul(sv[0:1, 1536:2048], sv[0:1, 0:512], sv[0:1, 0:512]),
             reads=["sv"], writes=["sv"], tag="lnst")
        S.op("vector", lambda e: e.tensor_sub(sv[0:1, 512:1024], sv[0:1, 512:1024], sv[0:1, 1536:2048]),
             reads=["sv"], writes=["sv"], tag="lnst")
        S.op("scalar", lambda e: e.activation(sv[0:1, 1024:1536], sv[0:1, 512:1024],
                                              AF.Sqrt, bias=self.cbias1(LN_EPS)),
             reads=["sv", "cpool"], writes=["sv"], tag="lnst")
        S.op("vector", lambda e: e.reciprocal(sv[0:1, 1024:1536], sv[0:1, 1024:1536]),
             reads=["sv"], writes=["sv"], tag="lnst")
        S.op("vector", lambda e: e.tensor_mul(sv[0:1, 1536:2048], sv[0:1, 0:512], sv[0:1, 1024:1536]),
             reads=["sv"], writes=["sv"], tag="lnst")
        pb3, pk3 = self.bank()
        pb4, pk4 = self.bank()
        S.op("tensor", lambda e: e.matmul(pb3[:], self.ones_r[:], sv[0:1, 1024:1536],
                                          start=True, stop=True),
             reads=["sv", "ones_r"], writes=[pk3], tag="lnbc")
        S.op("tensor", lambda e: e.matmul(pb4[:], self.ones_r[:], sv[0:1, 1536:2048],
                                          start=True, stop=True),
             reads=["sv", "ones_r"], writes=[pk4], tag="lnbc")
        zk = [(zname, c) for c in range(DC)]
        for c in range(DC):
            S.op("vector",
                 (lambda c=c, pb3=pb3: lambda e: e.tensor_mul(
                     self.ln_nsc[:], xs[:, c, :], pb3[:]))(),
                 reads=[xk[c], pk3], writes=["ln_nsc"], tag="lnn1")
            S.op("vector",
                 (lambda c=c, pb4=pb4: lambda e: e.tensor_sub(
                     zs[:, c, :], self.ln_nsc[:], pb4[:]))(),
                 reads=["ln_nsc", pk4], writes=[zk[c]], tag="lnn2")

    def proj(self, w, wkey, zs, zname, dst, dname, evac="copy"):
        S = self.S
        for m in range(DC):
            pb, pk = self.bank()
            for k in range(DC):
                S.op("tensor",
                     (lambda pb=pb, m=m, k=k, first=(k == 0), last=(k == DC - 1):
                      lambda e: e.matmul(pb[:], w[:, k, m * 128:(m + 1) * 128],
                                         zs[:, k, :], start=first, stop=last))(),
                     reads=[(wkey, k), (zname, k)], writes=[pk], tag="pj")
            if evac == "copy":
                fn = (lambda pb=pb, m=m: lambda e: e.copy(dst[:, m, :], pb[:]))()
            elif evac == "exp":
                fn = (lambda pb=pb, m=m:
                      lambda e: e.activation(dst[:, m, :], pb[:], AF.Exp))()
            elif evac == "sigmoid":
                fn = (lambda pb=pb, m=m:
                      lambda e: e.activation(dst[:, m, :], pb[:], AF.Sigmoid))()
            S.op("scalar", fn, reads=[pk], writes=[(dname, m)], tag="pjev")

    def p1_attn_a(self):
        nc, S = self.nc, self.S
        sc1 = self.scope()
        wq = self.sal(sc1, "a_wq", [128, 8, 1024], BF16)
        wk = self.sal(sc1, "a_wk", [128, 8, 1024], BF16)
        wv = self.sal(sc1, "a_wv", [128, 8, 1024], BF16)
        wc = self.sal(sc1, "a_wc", [128, 8, 1024], BF16)
        self._mc_stg = [self.sal(sc1, f"mcs{b}", [128, 4, 1024], F32)
                        for b in range(2)]
        self._mc_cb = [self.sal(sc1, f"mcb{b}", [128, 4, 1024], BF16)
                       for b in range(2)]
        self._mc_i = 0
        self.set_cast(sc1, "p1")
        self.cast_weight_to("aWv", wv, "a_wv")
        self.cast_weight_to("aWk", wk, "a_wk")
        self.cast_weight_to("aWq", wq, "a_wq")
        self.cast_weight_to("aWc", wc, "a_wc")

        xs = self.sal(sc1, "a_xs", [128, 8, 512], F32)
        zs = self.sal(sc1, "a_zs", [128, 8, 512], BF16)
        b1 = self.sal(sc1, "a_b1", [128, 8, 512], F32)
        b2 = self.sal(sc1, "a_b2", [128, 8, 512], F32)
        carK = self.sal(sc1, "a_carK", [128, 8], F32)
        carKV = self.sal(sc1, "a_carKV", [128, 8], F32)
        S.op("vector", lambda e: e.memset(carK[:], 0.0), writes=["carK"])
        S.op("vector", lambda e: e.memset(carKV[:], 0.0), writes=["carKV"])

        for s in range(NT):
            if self.upto >= 2:
                for j in range(3):
                    mi = s * 3 + j
                    self.cache_moe_matrix(mi, self.inp[("fW1", "fW2", "fWout")[mi % 3]][mi // 3])
            t0 = s * 512
            for c in range(DC):
                S.dma("sync", xs[:, c, :],
                      self.xres[c * 128:(c + 1) * 128, t0:t0 + 512],
                      reads=[("xres", c, s), ("a_xs", c)], writes=[("a_xs", c)],
                      tag="p1ld")
            self.ln_strip(xs, "a_xs", zs, "a_zs", self.ln_sq)
            self.proj(wv, "a_wv", zs, "a_zs", b1, "b1", evac="copy")
            self.proj(wk, "a_wk", zs, "a_zs", b2, "b2", evac="exp")
            for c in range(DC):
                S.op("vector",
                     (lambda c=c: lambda e: e.tensor_mul(b1[:, c, :], b2[:, c, :],
                                                         b1[:, c, :]))(),
                     reads=[("b1", c), ("b2", c)], writes=[("b1", c)], tag="p1kv")
            for (buf, bk, car, ck) in ((b2, "b2", carK, "carK"),
                                       (b1, "b1", carKV, "carKV")):
                for c in range(DC):
                    S.op("vector",
                         (lambda buf=buf, car=car, c=c:
                          lambda e: e.tensor_tensor_scan(
                              buf[:, c, :], buf[:, c, :], buf[:, c, :],
                              car[:, c:c + 1], ALU.add, ALU.bypass))(),
                         reads=[(bk, c), (ck, c)], writes=[(bk, c)], tag="p1sc")
                    S.op("scalar",
                         (lambda buf=buf, car=car, c=c:
                          lambda e: e.copy(car[:, c:c + 1],
                                           buf[:, c, 511:512]))(),
                         reads=[(bk, c)], writes=[(ck, c)], tag="p1cr")
            for c in range(DC):
                S.op("scalar",
                     (lambda c=c: lambda e: e.activation(b2[:, c, :], b2[:, c, :],
                                                         AF.Ln, bias=self.cbias(1e-6)))(),
                     reads=[("b2", c), "cpool"], writes=[("b2", c)], tag="p1ln")
                S.op("scalar",
                     (lambda c=c: lambda e: e.activation(b2[:, c, :], b2[:, c, :],
                                                         AF.Exp, scale=-1.0))(),
                     reads=[("b2", c)], writes=[("b2", c)], tag="p1ex")
                S.op("vector",
                     (lambda c=c: lambda e: e.tensor_mul(b1[:, c, :], b1[:, c, :],
                                                         b2[:, c, :]))(),
                     reads=[("b1", c), ("b2", c)], writes=[("b1", c)], tag="p1s")
            self.proj(wq, "a_wq", zs, "a_zs", b2, "b2", evac="sigmoid")
            for c in range(DC):
                S.op("vector",
                     (lambda c=c: lambda e: e.tensor_mul(zs[:, c, :], b1[:, c, :],
                                                         b2[:, c, :]))(),
                     reads=[("b1", c), ("b2", c)], writes=[("a_zs", c)], tag="p1o")
            for m in range(DC):
                pb, pk = self.bank()
                for k in range(DC):
                    S.op("tensor",
                         (lambda pb=pb, m=m, k=k, first=(k == 0),
                          last=(k == DC - 1):
                          lambda e: e.matmul(pb[:],
                                             wc[:, k, m * 128:(m + 1) * 128],
                                             zs[:, k, :], start=first,
                                             stop=last))(),
                         reads=[("a_wc", k), ("a_zs", k)], writes=[pk], tag="p1mm")
                S.op("vector",
                     (lambda pb=pb, m=m: lambda e: e.tensor_add(
                         xs[:, m, :], pb[:], xs[:, m, :]))(),
                     reads=[pk, ("a_xs", m)], writes=[("a_xs", m)], tag="p1ra")
                S.dma("gpsimd",
                      self.xres[m * 128:(m + 1) * 128, t0:t0 + 512],
                      xs[:, m, :], reads=[("a_xs", m)], writes=[("xres", m, s)],
                      tag="p1st")
        sc1.close()
        S.barrier()


    def cache_moe_matrix(self, mi, wap):
        S = self.S
        rs = wap.rearrange("(c p) m -> p c m", p=128)
        wcr = self.wcache[mi].rearrange("p (c m) -> p c m", m=1024)
        for hhalf in range(2):
            b = self._mc_i % 2
            self._mc_i += 1
            stg = self._mc_stg[b]
            cb = self._mc_cb[b]
            S.dma("sync", stg[:], rs[:, hhalf * 4:hhalf * 4 + 4, :],
                  reads=[("mcs", b)], writes=[("mcs", b)], tag="mc")
            for c in range(4):
                S.op("scalar",
                     (lambda b=b, c=c, stg=stg, cb=cb:
                      lambda e: e.copy(cb[:, c, :], stg[:, c, :]))(),
                     reads=[("mcs", b)], writes=[("mcb", b)], tag="mc")
            S.dma("gpsimd", wcr[:, hhalf * 4:hhalf * 4 + 4, :],
                  cb[:], reads=[("mcb", b)], writes=[("wc", mi)], tag="mc")

    def ffn(self, t):
        nc, S = self.nc, self.S
        inp = self.inp

        scAB = self.scope()
        xat = self.sal(scAB, t + "_xat", [128, DC, L], BF16)
        wq = self.sal(scAB, t + "_wq", [128, 8, 1024], BF16)
        wk = self.sal(scAB, t + "_wk", [128, 8, 1024], BF16)
        wv = self.sal(scAB, t + "_wv", [128, 8, 1024], BF16)

        scA = self.scope()
        fxs = self.sal(scA, t + "_fxs", [128, 8, 512], F32)
        fzs = self.sal(scA, t + "_fzs", [128, 8, 512], F32)
        self.set_cast(scA, t + "A")
        self.cast_weight_to("fWq", wq, t + "wq")
        self.cast_weight_to("fWk", wk, t + "wk")
        self.cast_weight_to("fWv", wv, t + "wv")
        for s in range(NT):
            t0 = s * 512
            for c in range(DC):
                S.dma("sync", fxs[:, c, :],
                      self.xres[c * 128:(c + 1) * 128, t0:t0 + 512],
                      reads=[("xres", c, s), (t + "fxs", c)],
                      writes=[(t + "fxs", c)], tag=t + "Ald")
            self.ln_strip(fxs, t + "fxs", fzs, t + "fzs", self.ln_sq)
            for c in range(DC):
                S.dma("gpsimd", self.zst[c * 128:(c + 1) * 128, t0:t0 + 512],
                      fzs[:, c, :], reads=[(t + "fzs", c)],
                      writes=[("zst", c, s)], tag=t + "Ast")
            pbS, pkS = self.bank()
            for c in range(DC):
                S.op("scalar",
                     (lambda c=c: lambda e: e.activation(self.ln_sq[:],
                                                         fzs[:, c, :],
                                                         AF.Square))(),
                     reads=[(t + "fzs", c)], writes=["ln_sq"], tag=t + "Asq")
                S.op("tensor",
                     (lambda pbS=pbS, c=c: lambda e: e.matmul(
                         pbS[0:1, :], self.ones_p[:], self.ln_sq[:],
                         start=(c == 0), stop=(c == DC - 1)))(),
                     reads=["ln_sq", "ones_p"], writes=[pkS], tag=t + "Amm")
            sv = self.ln_sv
            S.op("vector", (lambda pbS=pbS: lambda e: e.tensor_scalar(
                sv[0:1, 512:1024], pbS[0:1, :], 1.0 / D, None, ALU.mult))(),
                 reads=[pkS], writes=["sv"], tag=t + "Ast2")
            S.op("scalar", lambda e: e.activation(sv[0:1, 1024:1536],
                                                  sv[0:1, 512:1024], AF.Sqrt,
                                                  bias=self.cbias1(RMS_EPS)),
                 reads=["sv", "cpool"], writes=["sv"], tag=t + "Ast2")
            S.op("vector", lambda e: e.reciprocal(sv[0:1, 1024:1536],
                                                  sv[0:1, 1024:1536]),
                 reads=["sv"], writes=["sv"], tag=t + "Ast2")
            pbB, pkB = self.bank()
            S.op("tensor", (lambda pbB=pbB: lambda e: e.matmul(
                pbB[:], self.ones_r[:], sv[0:1, 1024:1536],
                start=True, stop=True))(),
                 reads=["sv", "ones_r"], writes=[pkB], tag=t + "Abc")
            for c in range(DC):
                S.op("vector",
                     (lambda c=c, pbB=pbB, t0=t0: lambda e: e.scalar_tensor_tensor(
                         xat[:, c, t0:t0 + 512], fzs[:, c, :],
                         self.fscale1[:, c:c + 1], pbB[:],
                         ALU.mult, ALU.mult))(),
                     reads=[(t + "fzs", c), pkB, "fsc1"],
                     writes=[(t + "xat", c, s)], tag=t + "Axm")
        scA.close()
        S.barrier()
        if self.dbg and t == "F1":
            dxat = self.nc.dram_tensor("dxat", [128, DC, L], BF16,
                                       kind="ExternalOutput").ap()
            S.dma("sync", dxat, xat[:],
                  reads=[(t + "xat", c, s2) for c in range(DC)
                         for s2 in range(NT)],
                  writes=["dbgxat"], tag="dbg")
            S.barrier()
        scB = self.scope()
        qb = self.sal(scB, t + "_qb", [128, 8, 512], BF16)
        khp = self.sal(scB, t + "_khp", [64, 2 * L], BF16)
        qhp = self.sal(scB, t + "_qhp", [64, 2 * L], BF16)
        vk0 = self.sal(scB, t + "_vk0", [128, 32, 65], BF16)
        vk1 = self.sal(scB, t + "_vk1", [128, 32, 65], BF16)
        s1 = self.sal(scB, t + "_s1", [64, 512], F32)
        s2 = self.sal(scB, t + "_s2", [64, 512], F32)
        ctab = self.sal(scB, t + "_ctab", [128, L], BF16)
        stab = self.sal(scB, t + "_stab", [128, L], BF16)
        pmt = self.sal(scB, t + "_pmt", [128, 128], BF16)
        S.dma("sync", ctab[:], inp["ropec"], writes=["ctab"], tag=t + "B")
        S.dma("sync", stab[:], inp["ropes"], writes=["stab"], tag=t + "B")
        S.dma("sync", pmt[:], inp["pmb"], writes=["pmt"], tag=t + "B")

        for p in range(NHP):
            for (wmat, wkey, dsttab, dkey, hbm) in (
                    (wq, t + "wq", qhp, "qhp", self.qhbm),
                    (wk, t + "wk", khp, "khp", self.khbm)):
                for n in range(8):
                    pbQ, pkQ = self.bank()
                    for k in range(DC):
                        S.op("tensor",
                             (lambda pbQ=pbQ, wmat=wmat, k=k, p=p, n=n,
                              first=(k == 0), last=(k == DC - 1):
                              lambda e: e.matmul(
                                  pbQ[:], wmat[:, k, p * 128:(p + 1) * 128],
                                  xat[:, k, n * 512:(n + 1) * 512],
                                  start=first, stop=last))(),
                             reads=[(wkey, k), (t + "xat", k, n)],
                             writes=[pkQ], tag=t + "Bqk")
                    S.op("scalar",
                         (lambda pbQ=pbQ, n=n: lambda e: e.copy(qb[:, n, :],
                                                                pbQ[:]))(),
                         reads=[pkQ], writes=[("qb", n)], tag=t + "Bqe")
                for h01 in range(2):
                    off = h01 * L
                    rs = slice(h01 * 64, h01 * 64 + 64)
                    for n in range(8):
                        pbP, pkP = self.bank()
                        S.op("tensor",
                             (lambda pbP=pbP, rs=rs, n=n: lambda e: e.matmul(
                                 pbP[0:64, :], pmt[rs, rs], qb[rs, n, :],
                                 start=True, stop=True))(),
                             reads=[("qb", n), "pmt"], writes=[pkP],
                             tag=t + "Brp")
                        S.op("vector",
                             (lambda rs=rs, n=n: lambda e: e.tensor_mul(
                                 s1[:], qb[rs, n, :],
                                 ctab[rs, n * 512:(n + 1) * 512]))(),
                             reads=[("qb", n), "ctab"], writes=["s1"],
                             tag=t + "Brt")
                        S.op("vector",
                             (lambda pbP=pbP, n=n: lambda e: e.tensor_mul(
                                 s2[:], pbP[0:64, :],
                                 stab[0:64, n * 512:(n + 1) * 512]))(),
                             reads=[pkP, "stab"], writes=["s2"], tag=t + "Brs")
                        S.op("vector",
                             (lambda dsttab=dsttab, off=off, n=n:
                              lambda e: e.tensor_add(
                                  dsttab[0:64, off + n * 512:off + (n + 1) * 512],
                                  s1[:], s2[:]))(),
                             reads=["s1", "s2"],
                             writes=[(dkey, h01, n)], tag=t + "Brr")
                    h = 2 * p + h01
                    S.dma("sync", hbm[h], dsttab[0:64, off:off + L],
                          reads=[(dkey, h01, n2) for n2 in range(8)],
                          writes=[("qh" if dkey == "qhp" else "kh", h)],
                          tag=t + "Bqs")
            for (h01, vkb) in ((0, vk0), (1, vk1)):
                S.op("vector",
                     (lambda vkb=vkb: lambda e: e.memset(vkb[:, :, 64:65], 1.0))(),
                     writes=[("vkm", h01)], tag=t + "Bvm")
            for ti in range(NTT):
                pbV, pkV = self.bank()
                for k in range(DC):
                    S.op("tensor",
                         (lambda pbV=pbV, k=k, ti=ti, p=p, first=(k == 0),
                          last=(k == DC - 1): lambda e: e.matmul(
                              pbV[:, 0:128], xat[:, k, ti * 128:(ti + 1) * 128],
                              wv[:, k, p * 128:(p + 1) * 128],
                              start=first, stop=last))(),
                         reads=[(t + "wv", k), (t + "xat", k, ti // 4)],
                         writes=[pkV], tag=t + "Bv")
                for (h01, vkb) in ((0, vk0), (1, vk1)):
                    S.op("scalar",
                         (lambda pbV=pbV, ti=ti, h01=h01, vkb=vkb:
                          lambda e: e.copy(vkb[:, ti, 0:64],
                                           pbV[:, h01 * 64:h01 * 64 + 64]))(),
                         reads=[pkV], writes=[("vkm", h01)], tag=t + "Bve")
            for (h01, vkb) in ((0, vk0), (1, vk1)):
                h = 2 * p + h01
                S.dma("sync", self.vkhbm[h],
                      vkb[:].rearrange("p a b -> p (a b)"),
                      reads=[("vkm", h01)], writes=[("vh", h)], tag=t + "Bvs")
        scB.close()
        scAB.close()
        S.barrier()

        scB2 = self.scope()
        khs = [self.sal(scB2, t + f"_khs{i}", [65, L], BF16) for i in range(2)]
        qhs = [self.sal(scB2, t + f"_qhs{i}", [65, L], BF16) for i in range(2)]
        vks = [self.sal(scB2, t + f"_vks{i}", [128, 32, 65], BF16)
               for i in range(2)]
        aoh = [self.sal(scB2, t + f"_aoh{i}", [64, L], BF16) for i in range(2)]
        ptb = self.sal(scB2, t + "_ptb", [128, 32, 512], BF16)
        dscr = self.sal(scB2, t + "_dscr", [128, 4, 512], F32)
        rmall = self.sal(scB2, t + "_rmall", [128, 320], F32)
        rm = self.sal(scB2, t + "_rm", [128, 1], F32)
        rlb = self.sal(scB2, t + "_rlb", [64, 512], F32)
        rl1 = self.sal(scB2, t + "_rl1", [1, 512], F32)
        wmsk = self.sal(scB2, t + "_wmsk", [128, 896], F32)
        S.dma("sync", wmsk[:], inp["wmask"], writes=["wmsk"], tag=t + "B2")
        for h in range(16):
            i = h % 2
            kh, qh, vk, ao1 = khs[i], qhs[i], vks[i], aoh[i]
            S.dma("sync", kh[0:64, :], self.khbm[h],
                  reads=[("kh", h), ("khs", i)], writes=[("khs", i)],
                  tag=t + "B2l")
            S.op("vector", (lambda kh=kh: lambda e: e.memset(kh[64:65, :], 1.0))(),
                 reads=[("khs", i)], writes=[("khs1", i)], tag=t + "B2l")
            S.dma("sync", qh[0:64, :], self.qhbm[h],
                  reads=[("qh", h), ("qhs", i)], writes=[("qhs", i)],
                  tag=t + "B2l")
            S.dma("sync", vk[:].rearrange("p a b -> p (a b)"), self.vkhbm[h],
                  reads=[("vh", h), ("vks", i)], writes=[("vks", i)],
                  tag=t + "B2l")
            nbs = []
            for qi in range(NTT):
                kmax = 128 * (qi + 1)
                nfull = (kmax - 128) // 512
                rfull = (kmax - 128) % 512
                nb = 0
                for kb in range(nfull + (1 if rfull else 0)):
                    kn = 512 if kb < nfull else rfull
                    pbS, pkS = self.bank(3 + (kb % 2))
                    S.op("tensor",
                         (lambda pbS=pbS, qh=qh, kh=kh, qi=qi, kb=kb, kn=kn:
                          lambda e: e.matmul(
                              pbS[:, 0:kn],
                              qh[0:64, qi * 128:qi * 128 + 128],
                              kh[0:64, kb * 512:kb * 512 + kn],
                              start=True, stop=True))(),
                         reads=[("qhs", i), ("khs", i)],
                         writes=[pkS], tag=t + "Bp1")
                    S.op("vector",
                         (lambda pbS=pbS, qi=qi, nb=nb, kn=kn: lambda e:
                          e.reduce_max(rmall[:, qi * 10 + nb:qi * 10 + nb + 1],
                                       pbS[:, 0:kn], axis=AX))(),
                         reads=[pkS], writes=[("rmall", qi, nb)],
                         tag=t + "Bp1r")
                    nb += 1
                pbD, pkD = self.bank(5)
                S.op("tensor",
                     (lambda pbD=pbD, qh=qh, kh=kh, qi=qi, kmax=kmax:
                      lambda e: e.matmul(
                          pbD[:, 0:128],
                          qh[0:64, qi * 128:qi * 128 + 128],
                          kh[0:64, kmax - 128:kmax],
                          start=True, stop=True))(),
                     reads=[("qhs", i), ("khs", i)],
                     writes=[pkD], tag=t + "Bp1d")
                S.op("vector",
                     (lambda pbD=pbD: lambda e: e.tensor_add(
                         dscr[:, 0, 0:128], pbD[:, 0:128], self.tri[:]))(),
                     reads=[pkD, "tri"], writes=[("dscr", 0)], tag=t + "Bp1d")
                S.op("vector",
                     (lambda qi=qi, nb=nb: lambda e: e.reduce_max(
                         rmall[:, qi * 10 + nb:qi * 10 + nb + 1],
                         dscr[:, 0, 0:128], axis=AX))(),
                     reads=[("dscr", 0)], writes=[("rmall", qi, nb)],
                     tag=t + "Bp1r")
                nbs.append(nb + 1)
            for qi in range(NTT):
                nb = nbs[qi]
                S.op("vector",
                     (lambda qi=qi, nb=nb: lambda e: e.tensor_reduce(
                         rm[:], rmall[:, qi * 10:qi * 10 + nb], axis=AX,
                         op=ALU.max))(),
                     reads=[("rmall", qi, j) for j in range(nb)] + ["rm"],
                     writes=["rm"], tag=t + "Bp1c")
                pbM, pkM = self.bank(6)
                S.op("tensor",
                     (lambda pbM=pbM: lambda e: e.matmul(
                         pbM[0:1, 0:128], rm[:], self.ident[:],
                         start=True, stop=True))(),
                     reads=["rm", "ident"], writes=[pkM], tag=t + "Bp1t")
                S.op("vector",
                     (lambda pbM=pbM, qh=qh, qi=qi: lambda e:
                      e.tensor_scalar(
                          qh[64:65, qi * 128:qi * 128 + 128],
                          pbM[0:1, 0:128], -1.0, None, ALU.mult))(),
                     reads=[pkM], writes=[("qh64", i, qi // 4)],
                     tag=t + "Bp1w")
            for qJ in range(NT):
                nkb = 4 * (qJ + 1)
                pvb, pvk = self.bank(2)

                def emit_s2(kb, qJ=qJ, i=i, kh=kh, qh=qh):
                    pbA, pkA = self.bank((0, 1, 3, 4)[kb % 4])
                    S.op("tensor",
                         (lambda pbA=pbA, kh=kh, qh=qh, kb=kb, qJ=qJ:
                          lambda e: e.matmul(
                              pbA[:],
                              kh[0:65, kb * 128:kb * 128 + 128],
                              qh[0:65, qJ * 512:qJ * 512 + 512],
                              start=True, stop=True))(),
                         reads=[("khs", i), ("khs1", i), ("qhs", i),
                                ("qh64", i, qJ)],
                         writes=[pkA], tag=t + "Bs2")
                    i_rel = kb - 4 * qJ
                    sl = kb % 4
                    if i_rel >= 0:
                        S.op("vector",
                             (lambda pbA=pbA, i_rel=i_rel, sl=sl: lambda e:
                              e.tensor_add(dscr[:, sl, :],
                                           pbA[:],
                                           wmsk[:, 384 - 128 * i_rel:
                                                896 - 128 * i_rel]))(),
                             reads=[pkA, "wmsk"], writes=[("dscr", sl)],
                             tag=t + "Bs2m")
                        S.op("scalar",
                             (lambda kb=kb, sl=sl: lambda e: e.activation(
                                 ptb[:, kb, :], dscr[:, sl, :], AF.Exp,
                                 scale=0.125))(),
                             reads=[("dscr", sl)], writes=[("ptb", kb)],
                             tag=t + "Bex")
                    else:
                        S.op("scalar",
                             (lambda pbA=pbA, kb=kb: lambda e: e.activation(
                                 ptb[:, kb, :], pbA[:], AF.Exp,
                                 scale=0.125))(),
                             reads=[pkA], writes=[("ptb", kb)],
                             tag=t + "Bex")

                def emit_pv(kb, qJ=qJ, i=i, vk=vk, nkb=nkb, pvb=pvb, pvk=pvk):
                    S.op("tensor",
                         (lambda pvb=pvb, vk=vk, kb=kb, first=(kb == 0),
                          last=(kb == nkb - 1): lambda e: e.matmul(
                              pvb[0:65, :], vk[:, kb, :],
                              ptb[:, kb, :],
                              start=first, stop=last))(),
                         reads=[("vks", i), ("ptb", kb)],
                         writes=[pvk], tag=t + "Bpv")

                for kb in range(nkb):
                    emit_s2(kb)
                    if kb >= 4:
                        emit_pv(kb - 4)
                for kb in range(max(0, nkb - 4), nkb):
                    emit_pv(kb)
                S.op("vector",
                     (lambda pvb=pvb: lambda e: e.reciprocal(
                         rl1[:], pvb[64:65, :]))(),
                     reads=[pvk], writes=["rl1"], tag=t + "Bnr")
                pbL, pkL = self.bank(7)
                S.op("tensor",
                     (lambda pbL=pbL: lambda e: e.matmul(
                         pbL[0:64, :], self.ones64, rl1[:],
                         start=True, stop=True))(),
                     reads=["rl1", "ones_r"], writes=[pkL], tag=t + "Bnb")
                S.op("scalar",
                     (lambda pbL=pbL: lambda e: e.copy(rlb[:], pbL[0:64, :]))(),
                     reads=[pkL], writes=["rlb"], tag=t + "Bne")
                S.op("vector",
                     (lambda pvb=pvb, ao1=ao1, qJ=qJ: lambda e: e.tensor_mul(
                         ao1[:, qJ * 512:(qJ + 1) * 512],
                         pvb[0:64, :], rlb[:]))(),
                     reads=[pvk, "rlb"], writes=[("aoh", i, qJ)],
                     tag=t + "Bno")
            S.dma("gpsimd", self.ao[h * 64:(h + 1) * 64, :], ao1[:],
                  reads=[("aoh", i, qJ2) for qJ2 in range(NT)],
                  writes=[("ao", h // 2)], tag=t + "Bst")
        scB2.close()
        S.barrier()

        scCD = self.scope()
        gfm = self.sal(scCD, t + "_gfm", [8, L], F32)

        scC = self.scope()
        wo = self.sal(scC, t + "_wo", [128, 8, 1024], BF16)
        aos = self.sal(scC, t + "_aos", [128, 8, 512], BF16)
        zsC = self.sal(scC, t + "_zsC", [128, 8, 512], F32)
        hs = self.sal(scC, t + "_hs", [128, 8, 512], F32)
        rw2 = self.sal(scC, t + "_rw2", [128, 8, 8], F32)
        nsb = self.sal(scC, t + "_nsb", [128, 32, 8], F32)
        gl = self.sal(scC, t + "_gl", [128, 4, 8], F32)
        gv = self.sal(scC, t + "_gv", [128, 4, 8], F32)
        gsc = self.sal(scC, t + "_gsc", [128, 4, 4], F32)
        rcol = self.sal(scC, t + "_rcol", [128, 4], F32)
        self.set_cast(scC, t + "C")
        self.cast_weight_to("fWo", wo, t + "wo")
        S.dma("sync", nsb[:], inp["fnoise"], writes=["nsb"], tag=t + "C")
        S.dma("sync", rw2[:], inp["frw"], writes=["rw2"], tag=t + "C")
        for c in range(DC):
            S.op("vector",
                 (lambda c=c: lambda e: e.tensor_scalar(
                     rw2[:, c, :], rw2[:, c, :], self.fscale2[:, c:c + 1],
                     None, ALU.mult))(),
                 reads=["rw2", "fsc2"], writes=["rw2"], tag=t + "C")
        for s in range(NT):
            t0 = s * 512
            for c in range(DC):
                S.dma("sync", aos[:, c, :],
                      self.ao[c * 128:(c + 1) * 128, t0:t0 + 512],
                      reads=[("ao", c), (t + "aos", c)],
                      writes=[(t + "aos", c)], tag=t + "Cld")
                S.dma("sync", zsC[:, c, :],
                      self.zst[c * 128:(c + 1) * 128, t0:t0 + 512],
                      reads=[("zst", c, s), (t + "zsC", c)],
                      writes=[(t + "zsC", c)], tag=t + "Cld")
            for m in range(DC):
                pbW, pkW = self.bank()
                for k in range(DC):
                    S.op("tensor",
                         (lambda pbW=pbW, m=m, k=k, first=(k == 0),
                          last=(k == DC - 1): lambda e: e.matmul(
                              pbW[:], wo[:, k, m * 128:(m + 1) * 128],
                              aos[:, k, :], start=first, stop=last))(),
                         reads=[(t + "wo", k), (t + "aos", k)],
                         writes=[pkW], tag=t + "Cmm")
                S.op("vector",
                     (lambda pbW=pbW, m=m: lambda e: e.tensor_add(
                         hs[:, m, :], pbW[:], zsC[:, m, :]))(),
                     reads=[pkW, (t + "zsC", m)], writes=[(t + "hs", m)],
                     tag=t + "Cha")
                S.dma("gpsimd", self.zst[m * 128:(m + 1) * 128, t0:t0 + 512],
                      hs[:, m, :], reads=[(t + "hs", m)],
                      writes=[("zst", m, s)], tag=t + "Chs")
            pbS, pkS = self.bank()
            for m in range(DC):
                S.op("scalar",
                     (lambda m=m: lambda e: e.activation(self.ln_sq[:],
                                                         hs[:, m, :],
                                                         AF.Square))(),
                     reads=[(t + "hs", m)], writes=["ln_sq"], tag=t + "Csq")
                S.op("tensor",
                     (lambda pbS=pbS, m=m: lambda e: e.matmul(
                         pbS[0:1, :], self.ones_p[:], self.ln_sq[:],
                         start=(m == 0), stop=(m == DC - 1)))(),
                     reads=["ln_sq", "ones_p"], writes=[pkS], tag=t + "Cst")
            sv = self.ln_sv
            S.op("vector", (lambda pbS=pbS: lambda e: e.tensor_scalar(
                sv[0:1, 512:1024], pbS[0:1, :], 1.0 / D, None, ALU.mult))(),
                 reads=[pkS], writes=["sv"], tag=t + "Cs2")
            S.op("scalar", lambda e: e.activation(sv[0:1, 1024:1536],
                                                  sv[0:1, 512:1024], AF.Sqrt,
                                                  bias=self.cbias1(RMS_EPS)),
                 reads=["sv", "cpool"], writes=["sv"], tag=t + "Cs2")
            S.op("vector", lambda e: e.reciprocal(sv[0:1, 1024:1536],
                                                  sv[0:1, 1024:1536]),
                 reads=["sv"], writes=["sv"], tag=t + "Cs2")
            pbB, pkB = self.bank()
            S.op("tensor", (lambda pbB=pbB: lambda e: e.matmul(
                pbB[:], self.ones_r[:], sv[0:1, 1024:1536],
                start=True, stop=True))(),
                 reads=["sv", "ones_r"], writes=[pkB], tag=t + "Cbc")
            for m in range(DC):
                S.op("vector",
                     (lambda m=m, pbB=pbB: lambda e: e.scalar_tensor_tensor(
                         aos[:, m, :], hs[:, m, :],
                         self.fscale2[:, m:m + 1], pbB[:],
                         ALU.mult, ALU.mult))(),
                     reads=[(t + "hs", m), pkB, "fsc2"],
                     writes=[(t + "aos", m)], tag=t + "Cxm")
                S.dma("gpsimd", self.xmoe[m * 128:(m + 1) * 128, t0:t0 + 512],
                      aos[:, m, :], reads=[(t + "aos", m)],
                      writes=[("xmoe", m, s)], tag=t + "Cxs")
            for tj in range(4):
                ti = s * 4 + tj
                pbR, pkR = self.bank()
                for k in range(DC):
                    S.op("tensor",
                         (lambda pbR=pbR, k=k, tj=tj, first=(k == 0),
                          last=(k == DC - 1): lambda e: e.matmul(
                              pbR[:, 0:8], hs[:, k, tj * 128:(tj + 1) * 128],
                              rw2[:, k, :], start=first, stop=last))(),
                         reads=[(t + "hs", k), "rw2"], writes=[pkR],
                         tag=t + "Crt")
                pbX, pkX = self.bank()
                S.op("tensor",
                     (lambda pbX=pbX, tj=tj: lambda e: e.matmul(
                         pbX[:, 0:1],
                         sv[0:1, 1024 + tj * 128:1024 + tj * 128 + 128],
                         self.ones_r[0:1, 0:1], start=True, stop=True))(),
                     reads=["sv", "ones_r"], writes=[pkX], tag=t + "Crc")
                S.op("scalar",
                     (lambda pbX=pbX, tj=tj: lambda e: e.copy(
                         rcol[:, tj:tj + 1], pbX[:, 0:1]))(),
                     reads=[pkX], writes=[("rcol", tj)], tag=t + "Crc")
                S.op("vector",
                     (lambda pbR=pbR, tj=tj, ti=ti: lambda e:
                      e.scalar_tensor_tensor(
                          gl[:, tj, :], pbR[:, 0:8], rcol[:, tj:tj + 1],
                          nsb[:, ti, :], ALU.mult, ALU.add))(),
                     reads=[pkR, ("rcol", tj), "nsb"], writes=[("gl", tj)],
                     tag=t + "Cg")
            def gstep(fn, rd, wr):
                for tj in range(4):
                    S.op("vector", fn(tj), reads=[(r, tj) for r in rd],
                         writes=[(w, tj) for w in wr], tag=t + "Cg")
            gstep(lambda tj: (lambda e: e.reduce_max(gsc[:, tj, 0:1],
                                                     gl[:, tj, :], axis=AX)),
                  ["gl"], ["gsc"])
            gstep(lambda tj: (lambda e: e.tensor_scalar(gsc[:, tj, 1:2],
                                                        gsc[:, tj, 0:1], -1.0,
                                                        None, ALU.mult)),
                  ["gsc"], ["gsc"])
            for tj in range(4):
                S.op("scalar",
                     (lambda tj=tj: lambda e: e.activation(
                         gv[:, tj, :], gl[:, tj, :], AF.Exp,
                         bias=gsc[:, tj, 1:2]))(),
                     reads=[("gl", tj), ("gsc", tj)], writes=[("gv", tj)],
                     tag=t + "Cg")
            gstep(lambda tj: (lambda e: e.reduce_sum(gsc[:, tj, 2:3],
                                                     gv[:, tj, :], axis=AX)),
                  ["gv"], ["gsc"])
            gstep(lambda tj: (lambda e: e.reciprocal(gsc[:, tj, 3:4],
                                                     gsc[:, tj, 2:3])),
                  ["gsc"], ["gsc"])
            gstep(lambda tj: (lambda e: e.tensor_scalar(gv[:, tj, :],
                                                        gv[:, tj, :],
                                                        gsc[:, tj, 3:4],
                                                        None, ALU.mult)),
                  ["gv", "gsc"], ["gv"])
            gstep(lambda tj: (lambda e: e.reduce_max(gsc[:, tj, 0:1],
                                                     gv[:, tj, :], axis=AX)),
                  ["gv"], ["gsc"])
            gstep(lambda tj: (lambda e: e.tensor_scalar(gl[:, tj, :],
                                                        gv[:, tj, :],
                                                        gsc[:, tj, 0:1],
                                                        None, ALU.is_equal)),
                  ["gv", "gsc"], ["gl"])
            gstep(lambda tj: (lambda e: e.scalar_tensor_tensor(
                gl[:, tj, :], gl[:, tj, :], -2.0, gv[:, tj, :],
                ALU.mult, ALU.add)), ["gl", "gv"], ["gl"])
            gstep(lambda tj: (lambda e: e.reduce_max(gsc[:, tj, 1:2],
                                                     gl[:, tj, :], axis=AX)),
                  ["gl"], ["gsc"])
            gstep(lambda tj: (lambda e: e.tensor_scalar(gl[:, tj, :],
                                                        gv[:, tj, :],
                                                        gsc[:, tj, 1:2],
                                                        None, ALU.is_ge)),
                  ["gv", "gsc"], ["gl"])
            gstep(lambda tj: (lambda e: e.tensor_mul(gv[:, tj, :],
                                                     gv[:, tj, :],
                                                     gl[:, tj, :])),
                  ["gv", "gl"], ["gv"])
            for tj in range(4):
                ti = s * 4 + tj
                pbT, pkT = self.bank()
                S.op("tensor",
                     (lambda pbT=pbT, tj=tj: lambda e: e.matmul(
                         pbT[0:8, 0:128], gv[:, tj, :], self.ident[:],
                         start=True, stop=True))(),
                     reads=[("gv", tj), "ident"], writes=[pkT], tag=t + "Cgt")
                S.op("scalar",
                     (lambda pbT=pbT, ti=ti: lambda e: e.copy(
                         gfm[0:8, ti * 128:(ti + 1) * 128],
                         pbT[0:8, 0:128]))(),
                     reads=[pkT], writes=[("gfm", ti // 4)], tag=t + "Cge")
        scC.close()
        S.barrier()

        scD = self.scope()
        xm = self.sal(scD, t + "_xm", [128, 8, 4, 512], BF16)
        yb = self.sal(scD, t + "_yb", [128, 8, 4, 512], F32)
        hb = self.sal(scD, t + "_hb", [128, 8, 512], BF16)
        ws = [self.sal(scD, t + f"_ws{j}", [128, 8, 1024], BF16)
              for j in range(3)]
        gsb = self.sal(scD, t + "_gsb", [128, 512], F32)
        asc = self.sal(scD, t + "_asc", [128, 512], F32)
        bsc = self.sal(scD, t + "_bsc", [128, 512], F32)
        b1a = self.sal(scD, t + "_b1a", [128, 8, 8], F32)
        b2a = self.sal(scD, t + "_b2a", [128, 8, 8], F32)
        boh = self.sal(scD, t + "_boh", [8, 1024], F32)
        exs = self.sal(scD, t + "_exs", [128, 2, 512], F32)
        ehs = self.sal(scD, t + "_ehs", [128, 2, 512], F32)
        S.dma("sync", b1a[:], inp["fb1"].rearrange("e p c -> p e c"),
              writes=["b1a"], tag=t + "D")
        S.dma("sync", b2a[:], inp["fb2"].rearrange("e p c -> p e c"),
              writes=["b2a"], tag=t + "D")

        oh8 = self.sal(scD, t + "_oh8", [8, 1024], F32)
        S.dma("sync", oh8[:], inp["oh8"], writes=["oh8"], tag=t + "D")
        S.dma("sync", boh[:], inp["fboutall"], writes=["boh"], tag=t + "D")
        wser = 0
        for hf in range(2):
            h0 = hf * 2048
            for c in range(DC):
                S.dma("sync", xm[:, c, :, :].rearrange("p a b -> p (a b)"),
                      self.xmoe[c * 128:(c + 1) * 128, h0:h0 + 2048],
                      reads=[("xmoe", c, 4 * hf), ("xmoe", c, 4 * hf + 1),
                             ("xmoe", c, 4 * hf + 2), ("xmoe", c, 4 * hf + 3),
                             (t + "xm", c)],
                      writes=[(t + "xm", c)], tag=t + "Dxm")
            for e8 in range(EXPERTS):
                wkeys = []
                for j in range(3):
                    sl = wser % 3
                    wser += 1
                    S.dma("sync", ws[sl][:].rearrange("p c m -> p (c m)"),
                          self.wcache[e8 * 3 + j],
                          reads=[("wc", e8 * 3 + j), ("wsl", sl)],
                          writes=[("wsl", sl)], tag=t + "Dwl")
                    wkeys.append(("wsl", sl))
                w1s, w2s, wos = ws[(wser - 3) % 3], ws[(wser - 2) % 3], ws[(wser - 1) % 3]
                k1, k2, ko = wkeys
                for n in range(4):
                    gn = hf * 2048 + n * 512
                    pbG, pkG = self.bank(0)
                    S.op("tensor",
                         (lambda pbG=pbG, e8=e8, gn=gn: lambda e: e.matmul(
                             pbG[:], oh8[0:8, e8 * 128:(e8 + 1) * 128],
                             gfm[0:8, gn:gn + 512], start=True, stop=True))(),
                         reads=[("gfm", gn // 512), "oh8"],
                         writes=[pkG], tag=t + "Dg")
                    S.op("scalar",
                         (lambda pbG=pbG: lambda e: e.copy(gsb[:], pbG[:]))(),
                         reads=[pkG], writes=["gsb"], tag=t + "Dge")
                    for m in range(DC):
                        pbA, pkA = self.bank(1 + (m % 2))
                        for k in range(DC):
                            S.op("tensor",
                                 (lambda pbA=pbA, w1s=w1s, k=k, m=m, n=n,
                                  first=(k == 0), last=(k == DC - 1):
                                  lambda e: e.matmul(
                                      pbA[:], w1s[:, k, m * 128:(m + 1) * 128],
                                      xm[:, k, n, :], start=first, stop=last))(),
                                 reads=[k1, (t + "xm", k)], writes=[pkA],
                                 tag=t + "Da")
                        S.op("scalar",
                             (lambda pbA=pbA, e8=e8, m=m: lambda e: e.activation(
                                 asc[:], pbA[:], AF.Silu,
                                 bias=b1a[:, e8, m:m + 1]))(),
                             reads=[pkA, "b1a"], writes=["asc"], tag=t + "Dae")
                        pbB, pkB = self.bank(3 + (m % 2))
                        for k in range(DC):
                            S.op("tensor",
                                 (lambda pbB=pbB, w2s=w2s, k=k, m=m, n=n,
                                  first=(k == 0), last=(k == DC - 1):
                                  lambda e: e.matmul(
                                      pbB[:], w2s[:, k, m * 128:(m + 1) * 128],
                                      xm[:, k, n, :], start=first, stop=last))(),
                                 reads=[k2, (t + "xm", k)], writes=[pkB],
                                 tag=t + "Db")
                        S.op("vector",
                             (lambda pbB=pbB, e8=e8, m=m: lambda e:
                              e.scalar_tensor_tensor(
                                  bsc[:], pbB[:], b2a[:, e8, m:m + 1], gsb[:],
                                  ALU.add, ALU.mult))(),
                             reads=[pkB, "b2a", "gsb"], writes=["bsc"],
                             tag=t + "Dbe")
                        S.op("vector",
                             (lambda m=m: lambda e: e.tensor_mul(
                                 hb[:, m, :], asc[:], bsc[:]))(),
                             reads=["asc", "bsc"], writes=[("hb", m)],
                             tag=t + "Dh")
                    for m2 in range(DC):
                        pbY, pkY = self.bank(5 + (m2 % 2))
                        for k in range(DC):
                            S.op("tensor",
                                 (lambda pbY=pbY, wos=wos, k=k, m2=m2,
                                  first=(k == 0),
                                  last=(k == DC - 1 and e8 > 0):
                                  lambda e: e.matmul(
                                      pbY[:], wos[:, k, m2 * 128:(m2 + 1) * 128],
                                      hb[:, k, :], start=first, stop=last))(),
                                 reads=[ko, ("hb", k)], writes=[pkY],
                                 tag=t + "Dw")
                        if e8 == 0:
                            S.op("tensor",
                                 (lambda pbY=pbY, m2=m2, gn=gn:
                                  lambda e: e.matmul(
                                      pbY[:], boh[0:8, m2 * 128:
                                                  (m2 + 1) * 128],
                                      gfm[0:8, gn:gn + 512],
                                      start=False, stop=True))(),
                                 reads=["boh", ("gfm", gn // 512)], writes=[pkY],
                                 tag=t + "Dwb")
                        if e8 == 0:
                            S.op("vector",
                                 (lambda pbY=pbY, m2=m2, n=n: lambda e:
                                  e.tensor_copy(yb[:, m2, n, :], pbY[:]))(),
                                 reads=[pkY], writes=[(t + "yb", m2, n)],
                                 tag=t + "Dy")
                        else:
                            S.op("vector",
                                 (lambda pbY=pbY, m2=m2, n=n: lambda e:
                                  e.tensor_add(yb[:, m2, n, :], pbY[:],
                                               yb[:, m2, n, :]))(),
                                 reads=[pkY, (t + "yb", m2, n)],
                                 writes=[(t + "yb", m2, n)], tag=t + "Dy")
            for n in range(4):
                s = hf * 4 + n
                t0 = s * 512
                for c in range(DC):
                    sl = c % 2
                    S.dma("sync", exs[:, sl, :],
                          self.xres[c * 128:(c + 1) * 128, t0:t0 + 512],
                          reads=[("xres", c, s), (t + "exs", sl)],
                          writes=[(t + "exs", sl)], tag=t + "Del")
                    S.dma("sync", ehs[:, sl, :],
                          self.zst[c * 128:(c + 1) * 128, t0:t0 + 512],
                          reads=[("zst", c, s), (t + "ehs", sl)],
                          writes=[(t + "ehs", sl)], tag=t + "Del")
                    S.op("vector",
                         (lambda c=c, n=n, sl=sl: lambda e: e.tensor_add(
                             ehs[:, sl, :], yb[:, c, n, :], ehs[:, sl, :]))(),
                         reads=[(t + "yb", c, n), (t + "ehs", sl)],
                         writes=[(t + "ehs", sl)], tag=t + "De1")
                    S.op("vector",
                         (lambda c=c, sl=sl: lambda e: e.tensor_add(
                             exs[:, sl, :], ehs[:, sl, :], exs[:, sl, :]))(),
                         reads=[(t + "ehs", sl), (t + "exs", sl)],
                         writes=[(t + "exs", sl)], tag=t + "De2")
                    S.dma("gpsimd",
                          self.xres[c * 128:(c + 1) * 128, t0:t0 + 512],
                          exs[:, sl, :], reads=[(t + "exs", sl)],
                          writes=[("xres", c, s)], tag=t + "Des")
        scD.close()
        scCD.close()
        S.barrier()


    def mamba(self):
        nc, S = self.nc, self.S
        inp = self.inp
        HL = L // 2
        sc = self.scope()
        wup = self.sal(sc, "m_wup", [128, 8, 256], BF16)
        wde = self.sal(sc, "m_wde", [128, 128], BF16)
        wB = self.sal(sc, "m_wB", [128, 64], BF16)
        wC = self.sal(sc, "m_wC", [128, 64], BF16)
        wdn = self.sal(sc, "m_wdn", [128, 1024], BF16)
        moh = self.sal(sc, "m_oh", [64, 8192], BF16)
        self.set_cast(sc, "m")
        self.cast_weight_to("mWup", wup, "m_wup")
        stg = self._cast_stg[0]
        for (wn, dst, dk) in (("mWdelta", wde, "m_wde"), ("mWB", wB, "m_wB"),
                              ("mWC", wC, "m_wC"), ("mWdown", wdn, "m_wdn")):
            dout = inp[wn].shape[1]
            S.dma("sync", stg[:, 0:dout], inp[wn], reads=[("cstg", 0)],
                  writes=[("cstg", 0)], tag="m_cw")
            S.op("scalar",
                 (lambda dst=dst, dout=dout, stg=stg:
                  lambda e: e.copy(dst[:, 0:dout], stg[:, 0:dout]))(),
                 reads=[("cstg", 0)], writes=[(dk,)], tag="m_cw")
        S.dma("sync", moh[:], inp["moh"], writes=["moh"], tag="m_c")
        vecs = self.sal(sc, "m_vec", [128, 32], F32)
        S.dma("sync", vecs[:, 0:2], inp["mbup"], writes=["mvec"], tag="m_c")
        S.dma("sync", vecs[:, 2:5], inp["mconvw"], writes=["mvec"], tag="m_c")
        S.dma("sync", vecs[:, 5:6], inp["mconvb"], writes=["mvec"], tag="m_c")
        S.dma("sync", vecs[:, 6:7], inp["mbdelta"], writes=["mvec"], tag="m_c")
        S.dma("sync", vecs[:, 7:8], inp["mdskip"], writes=["mvec"], tag="m_c")
        S.dma("sync", vecs[:, 8:16], inp["mbdown"], writes=["mvec"], tag="m_c")
        bBC = self.sal(sc, "m_bbc", [64, 2], F32)
        S.dma("sync", bBC[:, 0:1], inp["mbB"], writes=["mbbc"], tag="m_c")
        S.dma("sync", bBC[:, 1:2], inp["mbC"], writes=["mbbc"], tag="m_c")
        amat = self.sal(sc, "m_amat", [128, 64], F32)
        S.dma("sync", amat[:], inp["mAlog"], writes=["amat"], tag="m_c")
        S.op("scalar", lambda e: e.activation(amat[:], amat[:], AF.Exp),
             reads=["amat"], writes=["amat"], tag="m_c")
        S.op("vector", lambda e: e.tensor_scalar(amat[:], amat[:], -1.0, None,
                                                 ALU.mult),
             reads=["amat"], writes=["amat"], tag="m_c")
        mcar = self.sal(sc, "m_car", [128, 64], F32)
        S.op("vector", lambda e: e.memset(mcar[:], 0.0), writes=["mcar"],
             tag="m_c")

        lpad = self.sal(sc, "m_lpad", [128, 2 + HL], F32)
        rsig = self.sal(sc, "m_rsig", [128, HL], BF16)
        xmf = self.sal(sc, "m_xmf", [128, HL], F32)
        xmb = self.sal(sc, "m_xmb", [128, HL], BF16)
        delta = self.sal(sc, "m_delta", [128, HL], F32)
        dxb = self.sal(sc, "m_dx", [128, HL], F32)
        Bmat = self.sal(sc, "m_B", [64, HL], BF16)
        Cmat = self.sal(sc, "m_C", [64, HL], BF16)
        da = self.sal(sc, "m_da", [128, HL], F32)
        dbx = self.sal(sc, "m_dbx", [128, HL], F32)
        yac = self.sal(sc, "m_y", [128, HL], F32)
        ytmp = self.sal(sc, "m_yt", [128, 2, 512], F32)
        cbe = self.sal(sc, "m_cbe", [128, 512], F32)
        exm = self.sal(sc, "m_exm", [128, 2, 512], F32)
        scM1 = self.scope()
        xsm = self.sal(scM1, "m_xs", [128, 8, 512], F32)
        zsm = self.sal(scM1, "m_zs", [128, 8, 512], BF16)
        S.op("vector", lambda e: e.memset(lpad[:, 0:2], 0.0), writes=["lpad0"],
             tag="m_i")
        NH2 = NT // 2

        for hf in range(2):
            if hf == 1:
                S.op("scalar", lambda e: e.copy(lpad[:, 0:2], lpad[:, HL:HL + 2]),
                     reads=[("lpad", 3)], writes=["lpad0"], tag="m_bd")
            for sh in range(NH2):
                s = hf * NH2 + sh
                t0 = s * 512
                th = sh * 512
                for c in range(DC):
                    S.dma("sync", xsm[:, c, :],
                          self.xres[c * 128:(c + 1) * 128, t0:t0 + 512],
                          reads=[("xres", c, s), ("m_xs", c)],
                          writes=[("m_xs", c)], tag="m_ld")
                self.ln_strip(xsm, "m_xs", zsm, "m_zs", self.ln_sq)
                pbL, pkL = self.bank()
                for k in range(DC):
                    S.op("tensor",
                         (lambda pbL=pbL, k=k, first=(k == 0),
                          last=(k == DC - 1):
                          lambda e: e.matmul(pbL[:], wup[:, k, 0:128],
                                             zsm[:, k, :], start=first,
                                             stop=last))(),
                         reads=[("m_wup", k), ("m_zs", k)], writes=[pkL],
                         tag="m_um")
                S.op("vector",
                     (lambda pbL=pbL, th=th: lambda e: e.tensor_scalar(
                         lpad[:, 2 + th:2 + th + 512], pbL[:], vecs[:, 0:1],
                         None, ALU.add))(),
                     reads=[pkL, "mvec"], writes=[("lpad", sh)], tag="m_ue")
                pbR, pkR = self.bank()
                for k in range(DC):
                    S.op("tensor",
                         (lambda pbR=pbR, k=k, first=(k == 0),
                          last=(k == DC - 1):
                          lambda e: e.matmul(pbR[:], wup[:, k, 128:256],
                                             zsm[:, k, :], start=first,
                                             stop=last))(),
                         reads=[("m_wup", k), ("m_zs", k)], writes=[pkR],
                         tag="m_um")
                S.op("scalar",
                     (lambda pbR=pbR, th=th: lambda e: e.activation(
                         rsig[:, th:th + 512], pbR[:], AF.Silu,
                         bias=vecs[:, 1:2]))(),
                     reads=[pkR, "mvec"], writes=[("rsig", sh)], tag="m_ue")
            lk = [("lpad", sh) for sh in range(NH2)] + ["lpad0"]
            S.op("vector", lambda e: e.tensor_scalar(xmf[:], lpad[:, 0:HL],
                                                     vecs[:, 2:3], None,
                                                     ALU.mult),
                 reads=lk + ["mvec", "xmf"], writes=["xmf"], tag="m_cv")
            S.op("vector", lambda e: e.scalar_tensor_tensor(
                xmf[:], lpad[:, 1:1 + HL], vecs[:, 3:4], xmf[:], ALU.mult,
                ALU.add),
                 reads=lk + ["mvec", "xmf"], writes=["xmf"], tag="m_cv")
            S.op("vector", lambda e: e.scalar_tensor_tensor(
                xmf[:], lpad[:, 2:2 + HL], vecs[:, 4:5], xmf[:], ALU.mult,
                ALU.add),
                 reads=lk + ["mvec", "xmf"], writes=["xmf"], tag="m_cv")
            S.op("scalar", lambda e: e.activation(xmf[:], xmf[:], AF.Silu,
                                                  bias=vecs[:, 5:6]),
                 reads=["xmf", "mvec"], writes=["xmf"], tag="m_cv")
            S.op("scalar", lambda e: e.copy(xmb[:], xmf[:]),
                 reads=["xmf"], writes=["xmb"], tag="m_cv")
            for nblk in range(NH2):
                th = nblk * 512
                pbD, pkD = self.bank()
                S.op("tensor",
                     (lambda pbD=pbD, th=th: lambda e: e.matmul(
                         pbD[:], wde[:], xmb[:, th:th + 512], start=True,
                         stop=True))(),
                     reads=[("m_wde",), "xmb"], writes=[pkD], tag="m_dm")
                S.op("vector",
                     (lambda pbD=pbD: lambda e: e.tensor_scalar(
                         cbe[:], pbD[:], vecs[:, 6:7], None, ALU.add))(),
                     reads=[pkD, "mvec"], writes=["m_cbe"], tag="m_dsp")
                S.op("scalar",
                     (lambda th=th: lambda e: e.activation(
                         delta[:, th:th + 512], cbe[:], AF.Relu))(),
                     reads=["m_cbe"], writes=[("delta", nblk)], tag="m_dsp")
                S.op("scalar",
                     lambda e: e.activation(cbe[:], cbe[:], AF.Abs),
                     reads=["m_cbe"], writes=["m_cbe"], tag="m_dsp")
                S.op("scalar", lambda e: e.activation(cbe[:], cbe[:], AF.Exp,
                                                      scale=-1.0),
                     reads=["m_cbe"], writes=["m_cbe"], tag="m_dsp")
                S.op("scalar", lambda e: e.activation(cbe[:], cbe[:], AF.Ln,
                                                      bias=self.cbias(1.0)),
                     reads=["m_cbe", "cpool"], writes=["m_cbe"], tag="m_dsp")
                S.op("vector",
                     (lambda th=th: lambda e: e.tensor_add(
                         delta[:, th:th + 512], delta[:, th:th + 512],
                         cbe[:]))(),
                     reads=[("delta", nblk), "m_cbe"],
                     writes=[("delta", nblk)], tag="m_dsp")
                for (wmat, wkey, dst, dk, bcol) in (
                        (wB, "m_wB", Bmat, "m_B", 0),
                        (wC, "m_wC", Cmat, "m_C", 1)):
                    pbB, pkB = self.bank()
                    S.op("tensor",
                         (lambda pbB=pbB, wmat=wmat, th=th: lambda e: e.matmul(
                             pbB[0:64, :], wmat[:], xmb[:, th:th + 512],
                             start=True, stop=True))(),
                         reads=[(wkey,), "xmb"], writes=[pkB], tag="m_bc")
                    S.op("vector",
                         (lambda pbB=pbB, dst=dst, th=th, bcol=bcol:
                          lambda e: e.tensor_scalar(
                              dst[0:64, th:th + 512], pbB[0:64, :],
                              bBC[:, bcol:bcol + 1], None, ALU.add))(),
                         reads=[pkB, "mbbc"], writes=[(dk, nblk)], tag="m_bc")
            dk_all = [("delta", nblk) for nblk in range(NH2)]
            S.op("vector", lambda e: e.tensor_mul(dxb[:], delta[:], xmf[:]),
                 reads=dk_all + ["xmf", "m_dxk"], writes=["m_dxk"], tag="m_dx")
            bk_all = [("m_B", nblk) for nblk in range(NH2)]
            ck_all = [("m_C", nblk) for nblk in range(NH2)]
            for n64 in range(64):
                S.op("vector",
                     (lambda n64=n64: lambda e: e.tensor_scalar(
                         da[:], delta[:], amat[:, n64:n64 + 1], None,
                         ALU.mult))(),
                     reads=dk_all + ["amat", "m_dak"], writes=["m_dak"],
                     tag="m_n1")
                S.op("scalar", lambda e: e.activation(da[:], da[:], AF.Exp),
                     reads=["m_dak"], writes=["m_dak"], tag="m_n2")
                for nblk in range(NH2):
                    th = nblk * 512
                    pbB, pkB = self.bank(0 if nblk % 2 == 0 else 1)
                    S.op("tensor",
                         (lambda pbB=pbB, n64=n64, th=th: lambda e: e.matmul(
                             pbB[:], moh[:, n64 * 128:(n64 + 1) * 128],
                             Bmat[0:64, th:th + 512], start=True, stop=True))(),
                         reads=[bk_all[nblk], "moh"], writes=[pkB], tag="m_n3")
                    S.op("vector",
                         (lambda pbB=pbB, th=th: lambda e: e.tensor_mul(
                             dbx[:, th:th + 512], dxb[:, th:th + 512],
                             pbB[:]))(),
                         reads=[pkB, "m_dxk", "m_dbk"], writes=["m_dbk"],
                         tag="m_n3")
                S.op("vector",
                     (lambda n64=n64: lambda e: e.tensor_tensor_scan(
                         dbx[:], da[:], dbx[:], mcar[:, n64:n64 + 1],
                         ALU.mult, ALU.add))(),
                     reads=["m_dak", "m_dbk", ("mcar", n64)],
                     writes=["m_dbk"], tag="m_n4")
                if hf == 0:
                    S.op("scalar",
                         (lambda n64=n64: lambda e: e.copy(
                             mcar[:, n64:n64 + 1], dbx[:, HL - 1:HL]))(),
                         reads=["m_dbk"], writes=[("mcar", n64)], tag="m_n4c")
                for nblk in range(NH2):
                    th = nblk * 512
                    sl = nblk % 2
                    pbC, pkC = self.bank(2 if nblk % 2 == 0 else 3)
                    S.op("tensor",
                         (lambda pbC=pbC, n64=n64, th=th: lambda e: e.matmul(
                             pbC[:], moh[:, n64 * 128:(n64 + 1) * 128],
                             Cmat[0:64, th:th + 512], start=True, stop=True))(),
                         reads=[ck_all[nblk], "moh"], writes=[pkC], tag="m_n5")
                    S.op("vector",
                         (lambda pbC=pbC, th=th, sl=sl: lambda e: e.tensor_mul(
                             ytmp[:, sl, :], dbx[:, th:th + 512], pbC[:]))(),
                         reads=[pkC, "m_dbk", ("m_ytk", sl)],
                         writes=[("m_ytk", sl)], tag="m_n5")
                    if n64 == 0:
                        S.op("gpsimd",
                             (lambda th=th, sl=sl: lambda e: e.tensor_copy(
                                 yac[:, th:th + 512], ytmp[:, sl, :]))(),
                             reads=[("m_ytk", sl)], writes=[("m_yk", nblk)],
                             tag="m_n6")
                    else:
                        S.op("gpsimd",
                             (lambda th=th, sl=sl: lambda e: e.tensor_add(
                                 yac[:, th:th + 512], ytmp[:, sl, :],
                                 yac[:, th:th + 512]))(),
                             reads=[("m_ytk", sl), ("m_yk", nblk)],
                             writes=[("m_yk", nblk)], tag="m_n6")
            yk_all = [("m_yk", nblk) for nblk in range(NH2)]
            S.op("vector", lambda e: e.scalar_tensor_tensor(
                yac[:], xmf[:], vecs[:, 7:8], yac[:], ALU.mult, ALU.add),
                 reads=["xmf", "mvec"] + yk_all, writes=yk_all, tag="m_y")
            rk_all = [("rsig", sh) for sh in range(NH2)]
            S.op("vector", lambda e: e.tensor_mul(xmb[:], yac[:], rsig[:]),
                 reads=yk_all + rk_all + ["xmb"], writes=["xmb"], tag="m_g")
            for nblk in range(NH2):
                s = hf * NH2 + nblk
                t0 = s * 512
                th = nblk * 512
                for m in range(DC):
                    sl = m % 2
                    pbY, pkY = self.bank(4 + sl)
                    S.op("tensor",
                         (lambda pbY=pbY, m=m, th=th: lambda e: e.matmul(
                             pbY[:], wdn[:, m * 128:(m + 1) * 128],
                             xmb[:, th:th + 512], start=True, stop=True))(),
                         reads=[("m_wdn",), "xmb"], writes=[pkY], tag="m_om")
                    S.dma("sync", exm[:, sl, :],
                          self.xres[m * 128:(m + 1) * 128, t0:t0 + 512],
                          reads=[("xres", m, s), ("m_exm", sl)],
                          writes=[("m_exm", sl)], tag="m_ol")
                    S.op("vector",
                         (lambda pbY=pbY, m=m, sl=sl: lambda e:
                          e.scalar_tensor_tensor(
                              exm[:, sl, :], pbY[:], vecs[:, 8 + m:9 + m],
                              exm[:, sl, :], ALU.add, ALU.add))(),
                         reads=[pkY, "mvec", ("m_exm", sl)],
                         writes=[("m_exm", sl)], tag="m_oa")
                    S.dma("gpsimd",
                          self.xres[m * 128:(m + 1) * 128, t0:t0 + 512],
                          exm[:, sl, :], reads=[("m_exm", sl)],
                          writes=[("xres", m, s)], tag="m_os")
        scM1.close()
        sc.close()
        S.barrier()

    def p7_out(self):
        S = self.S
        S.barrier()
        sc7 = self.scope()
        tin = self.sal(sc7, "t7_tin", [128, 4, 1024], F32)
        tst = self.sal(sc7, "t7_tst", [128, 8, 512], F32)
        for g in range(8):
            for c in range(DC):
                S.dma("sync", tst[:, c, :],
                      self.xres[c * 128:(c + 1) * 128, g * 512:(g + 1) * 512],
                      reads=[("xres", c, g), ("tst", c)], writes=[("tst", c)],
                      tag="p7ld")
            for j in range(4):
                for c in range(DC):
                    pb, pk = self.bank()
                    S.op("tensor",
                         (lambda pb=pb, c=c, j=j: lambda e: e.transpose(
                             pb[:, 0:128], tst[:, c, j * 128:(j + 1) * 128],
                             self.ident[:]))(),
                         reads=[("tst", c), "ident"], writes=[pk], tag="p7t")
                    S.op("scalar",
                         (lambda pb=pb, c=c, j=j: lambda e: e.copy(
                             tin[:, j, c * 128:(c + 1) * 128], pb[:, 0:128]))(),
                         reads=[pk], writes=[("tin", j)], tag="p7e")
            for j in range(4):
                ti = g * 4 + j
                S.dma("gpsimd", self.out[ti * 128:(ti + 1) * 128, :],
                      tin[:, j, :], reads=[("tin", j)], writes=[("out", ti)],
                      tag="p7st")
        sc7.close()

    def finalize(self):
        S = self.S
        S.barrier()
        S.op("vector", lambda e: e.memset(self.ones_p[0:1, 0:1], 1.0),
             writes=["ones_p"], tag="fin")
        S.emit(self.stack)
        return self.nc

    def build(self):
        nc = self.nc
        self.ln_sq = nc.alloc_sbuf_tensor("ln_sq", [128, 512], F32)
        self.ln_nsc = nc.alloc_sbuf_tensor("ln_nsc", [128, 512], F32)
        self.ln_sv = nc.alloc_sbuf_tensor("ln_sv", [1, 4 * 512], F32)
        self.load_consts()
        self.emit_cpool()
        self.p0_transpose_in()
        if self.upto >= 1:
            self.p1_attn_a()
        if self.upto >= 2:
            self.ffn("F1")
        if self.upto >= 3:
            self.mamba()
        if self.upto >= 4:
            self.ffn("F2")
        if self.dbg:
            S = self.S
            S.barrier()
            dz = self.nc.dram_tensor("dzst", [D, L], F32, kind="ExternalOutput").ap()
            da_ = self.nc.dram_tensor("dao", [D, L], BF16, kind="ExternalOutput").ap()
            dx_ = self.nc.dram_tensor("dxmoe", [D, L], BF16, kind="ExternalOutput").ap()
            S.dma("sync", dz, self.zst, reads=[], writes=["dbg1"], tag="dbg")
            dk_ = self.nc.dram_tensor("dkh", [16, 64, L], BF16, kind="ExternalOutput").ap()
            dq_ = self.nc.dram_tensor("dqh", [16, 64, L], BF16, kind="ExternalOutput").ap()
            dv_ = self.nc.dram_tensor("dvk", [16, 128, 32 * 65], BF16, kind="ExternalOutput").ap()
            S.dma("sync", dk_, self.khbm, reads=[], writes=["dbg4"], tag="dbg")
            S.dma("sync", dq_, self.qhbm, reads=[], writes=["dbg5"], tag="dbg")
            S.dma("sync", dv_, self.vkhbm, reads=[], writes=["dbg6"], tag="dbg")
            S.dma("sync", da_, self.ao, reads=[], writes=["dbg2"], tag="dbg")
            S.dma("sync", dx_, self.xmoe, reads=[], writes=["dbg3"], tag="dbg")
        self.p7_out()
        return self.finalize()


def build_program(upto=99, dbg=False):
    mk = MK(upto, dbg)
    nc = mk.build()
    return nc, mk


def make_in_maps(inputs, n_cores=8):
    import ml_dtypes
    bf16 = ml_dtypes.bfloat16
    g = lambda k: np.ascontiguousarray(np.asarray(inputs[k], dtype=np.float32))
    ppvec = lambda v: np.ascontiguousarray(v.reshape(DC, 128).T)

    shared = {}
    shared["ident"] = np.eye(128, dtype=np.float32)
    c0 = host_consts(0)
    shared["tri"] = c0["tri"]
    shared["wmask"] = c0["wmask"]
    shared["ropec"] = c0["rope_ck"].astype(bf16)
    shared["ropes"] = c0["rope_sk"].astype(bf16)
    shared["pmb"] = c0["rope_pm"].astype(bf16)
    for w, k in (("aWq", "L0_attn_a_Wq"), ("aWk", "L0_attn_a_Wk"),
                 ("aWv", "L0_attn_a_Wv"), ("aWc", "L0_attn_a_Wc"),
                 ("fWq", "L0_ffn_wq"), ("fWk", "L0_ffn_wk"),
                 ("fWv", "L0_ffn_wv"), ("fWo", "L0_ffn_wo"),
                 ("fW1", "L0_ffn_w1"), ("fW2", "L0_ffn_w2"),
                 ("fWout", "L0_ffn_wout"), ("mWup", "L0_attn_b_W_up"),
                 ("mWdelta", "L0_attn_b_W_delta"), ("mWB", "L0_attn_b_W_B"),
                 ("mWC", "L0_attn_b_W_C"), ("mWdown", "L0_attn_b_W_down"),
                 ("mAlog", "L0_attn_b_A_log"), ("mconvw", "L0_attn_b_conv_w")):
        shared[w] = g(k)
    shared["fscale1"] = ppvec(g("L0_ffn_scale1"))
    shared["fscale2"] = ppvec(g("L0_ffn_scale2"))
    rw = g("L0_ffn_router_w")
    shared["frw"] = np.ascontiguousarray(rw.reshape(DC, 128, 8).transpose(1, 0, 2))
    for w, k in (("fb1", "L0_ffn_b1"), ("fb2", "L0_ffn_b2")):
        b = g(k)
        shared[w] = np.ascontiguousarray(b.reshape(8, DC, 128).transpose(0, 2, 1))
    shared["fboutall"] = np.ascontiguousarray(
        g("L0_ffn_bout").astype(np.float32))
    oh8 = np.zeros((8, 1024), dtype=np.float32)
    for e in range(8):
        oh8[e, e * 128:(e + 1) * 128] = 1.0
    shared["oh8"] = oh8
    moh = np.zeros((64, 8192), dtype=np.float32)
    for n in range(64):
        moh[n, n * 128:(n + 1) * 128] = 1.0
    shared["moh"] = moh.astype(bf16)
    shared["mbup"] = np.ascontiguousarray(
        g("L0_attn_b_b_up").reshape(2, 128).T)
    shared["mconvb"] = g("L0_attn_b_conv_b").reshape(128, 1)
    shared["mbdelta"] = g("L0_attn_b_b_delta").reshape(128, 1)
    shared["mbB"] = g("L0_attn_b_b_B").reshape(64, 1)
    shared["mbC"] = g("L0_attn_b_b_C").reshape(64, 1)
    shared["mdskip"] = g("L0_attn_b_D_skip").reshape(128, 1)
    shared["mbdown"] = ppvec(g("L0_attn_b_b_down"))

    X = np.asarray(inputs["L0_attn_a_X"], dtype=np.float32)
    noise = _driver_gumbel((8, L, 8)) * np.float32(0.05)
    maps = []
    for core in range(n_cores):
        m = dict(shared)
        m["xin"] = np.ascontiguousarray(X[core])
        m["fnoise"] = np.ascontiguousarray(
            noise[core].reshape(32, 128, 8).transpose(1, 0, 2))
        maps.append(m)
    return maps


_CACHE = {}


def kernel(**inputs):
    if "nc" not in _CACHE:
        nc, mk = build_program(4)
        _CACHE["nc"] = nc
    nc = _CACHE["nc"]
    in_maps = make_in_maps(inputs, 8)
    res = run_bass_kernel_spmd(nc, in_maps, core_ids=list(range(8)))
    out = np.stack([res.results[c]["out"] for c in range(8)], axis=0)
    return out
